# version 06
# baseline (speedup 1.0000x reference)
import numpy as np
import ml_dtypes
from contextlib import ExitStack

import concourse.bass as bass
import concourse.mybir as mybir
from concourse.bass_utils import run_bass_kernel_spmd

F32 = mybir.dt.float32
BF16 = mybir.dt.bfloat16
F16 = mybir.dt.float16
I32 = mybir.dt.int32
AF = mybir.ActivationFunctionType
ALU = mybir.AluOpType

T = 4096
D = 2048
H = 16
E = 128
CH = 128
NCH = 32
NG = 8
GT = 512
DT = 16
IF = 5632
IT = 44
TWO_PI = float(2.0 * np.pi)
PI = float(np.pi)
BF = ml_dtypes.bfloat16

DEBUG = False
LAST_MARKS = {}
PHASES = 9
RACE_DET = True
P1SUB = 3


def _slopes(h):
    start = 2.0 ** (-2.0 ** (-(np.log2(h) - 3)))
    return np.array([start ** (i + 1) for i in range(h)], dtype=np.float32)


class _DummyInst:
    def then_inc(self, sem, k):
        return self


class _DryEng:
    def __getattr__(self, name):
        def f(*a, **kw):
            return _DummyInst()
        return f


class Marks(dict):
    dry = True

    def __missing__(self, key):
        if self.dry:
            return 0
        raise KeyError(key)


def build_nc():
    global LAST_MARKS
    nc = bass.Bass("TRN2", target_bir_lowering=False, detect_race_conditions=RACE_DET)

    x_in = nc.dram_tensor("x", [T, D], F32, kind="ExternalInput")
    wqkv = nc.dram_tensor("wqkv", [3 * D, D], F32, kind="ExternalInput")
    wproj = nc.dram_tensor("wproj", [D, D], F32, kind="ExternalInput")
    theta_pp_in = nc.dram_tensor("theta_pp", [128, 128], F32, kind="ExternalInput")
    gup = nc.dram_tensor("gup", [2 * IF, D // 8], I32, kind="ExternalInput")
    gus = nc.dram_tensor("gus", [2 * IF, DT], F16, kind="ExternalInput")
    dnp = nc.dram_tensor("dnp", [D, IF // 8], I32, kind="ExternalInput")
    dns = nc.dram_tensor("dns", [D, IT], F16, kind="ExternalInput")
    ident_in = nc.dram_tensor("ident", [128, 128], BF16, kind="ExternalInput")
    idxloc_in = nc.dram_tensor("idxloc", [128, 128], F32, kind="ExternalInput")
    quart_in = nc.dram_tensor("quart", [128, 128], F32, kind="ExternalInput")
    diag_in = nc.dram_tensor("diagT", [128, H * 128], BF16, kind="ExternalInput")
    qdec_in = nc.dram_tensor("qdec", [128, H * 128], BF16, kind="ExternalInput")
    kdec_in = nc.dram_tensor("kdecpp", [128, H], F32, kind="ExternalInput")
    cv_in = nc.dram_tensor("cvals", [128, 8], F32, kind="ExternalInput")
    onesc_in = nc.dram_tensor("onescol", [128, 1], BF16, kind="ExternalInput")
    onesr_in = nc.dram_tensor("onesrow", [1, 128], F32, kind="ExternalInput")

    out = nc.dram_tensor("out", [T, D], F32, kind="ExternalOutput")
    skind = "ExternalOutput" if DEBUG else "Internal"
    xTs = nc.dram_tensor("xTs", [D, T], BF16, kind=skind)
    qs = nc.dram_tensor("qs", [D, T], BF16, kind=skind)
    ks = nc.dram_tensor("ks", [D, T], BF16, kind=skind)
    vs = nc.dram_tensor("vs", [T, D], BF16, kind=skind)
    ln1s = nc.dram_tensor("ln1s", [D, T], BF16, kind=skind)
    hs = nc.dram_tensor("hs", [IF, T], BF16, kind=skind)
    ysd = nc.dram_tensor("ysd", [D, T], BF16, kind=skind)
    wgd = nc.dram_tensor("wgd", [2 * IF, D], BF16, kind=skind)
    ydd = nc.dram_tensor("ydd", [D, T], BF16, kind=skind)
    wdd = nc.dram_tensor("wdd", [DT * 128, IF], BF16, kind=skind)
    r2d = nc.dram_tensor("r2d", [D, T], F32, kind=skind)
    std = nc.dram_tensor("std", [1, NG, 6, GT], F32, kind=skind)
    tdmp = nc.dram_tensor("tdmp", [128, 2, H * 128], BF16, kind=skind)
    cdmp = nc.dram_tensor("cdmp", [128, 2, H], F32, kind=skind)
    ldmp = nc.dram_tensor("ldmp", [128, 6, 128], BF16, kind=skind)
    sdmp = nc.dram_tensor("sdmp", [128, 6, 128], BF16, kind=skind)
    wdTs = nc.dram_tensor("wdTs", [DT * 128, IF], BF16, kind="Internal")

    slopes = _slopes(H)
    bdec = [float(np.exp(-slopes[hh] * CH)) for hh in range(H)]

    es = ExitStack()
    ident = es.enter_context(nc.sbuf_tensor("identS", [128, 128], BF16))
    idxloc = es.enter_context(nc.sbuf_tensor("idxlocS", [128, 128], F32))
    quart = es.enter_context(nc.sbuf_tensor("quartS", [128, 128], F32))
    diagT = es.enter_context(nc.sbuf_tensor("diagS", [128, H * 128], BF16))
    qdec = es.enter_context(nc.sbuf_tensor("qdecS", [128, H * 128], BF16))
    kdecpp = es.enter_context(nc.sbuf_tensor("kdecS", [128, H], F32))
    theta_pp = es.enter_context(nc.sbuf_tensor("thetaS", [128, 128], F32))
    cv = es.enter_context(nc.sbuf_tensor("cvS", [128, 8], F32))
    onescol = es.enter_context(nc.sbuf_tensor("onescolS", [128, 1], BF16))
    onesrow = es.enter_context(nc.sbuf_tensor("onesrowS", [1, 128], F32))

    CMAP = {0.0: 0, 1.0: 1, PI / 2: 2, 1e-5: 3, 1e-6: 4}
    for v, i in CMAP.items():
        nc.const_aps.register(F32, v, cv[:, i:i + 1])

    def c_ap(v):
        return cv[:, CMAP[v]:CMAP[v] + 1]

    def c_ap1(v):
        return cv[0:1, CMAP[v]:CMAP[v] + 1]

    sems = {}
    for name in ["LQ", "GQ", "GQ2", "SQ", "SV", "SS", "SP", "ST", "PH", "GRP"]:
        sems[name] = es.enter_context(nc.semaphore(name))

    blk_cm = nc.Block()
    blk = blk_cm.__enter__()

    jobs = {"sync": [], "vector": [], "scalar": [], "tensor": [], "gpsimd": []}
    marks = Marks()
    cnt = {k: 0 for k in sems}

    def J(eng):
        def deco(fn):
            jobs[eng].append(fn)
            return fn
        return deco

    def INC(sem, inst, k=1, mark=None):
        inst.then_inc(sems[sem], k)
        cnt[sem] += k
        if mark is not None:
            marks[mark] = cnt[sem]
        return cnt[sem]

    def W(eng, sem, val):
        eng.wait_ge(sems[sem], val)

    def WM(eng, sem, mark):
        eng.wait_ge(sems[sem], marks[mark])

    ENGS = ["sync", "vector", "scalar", "tensor", "gpsimd"]

    def run_jobs():
        saved = dict(cnt)
        Marks.dry = True
        for name in ENGS:
            eng = _DryEng()
            for fn in jobs[name]:
                fn(eng)
        for k in cnt:
            cnt[k] = saved[k]
        Marks.dry = False
        for name in ENGS:
            sub = list(jobs[name])

            def runner(fns):
                def _(eng):
                    for fn in fns:
                        fn(eng)
                return _
            getattr(blk, name)(runner(sub))
        for name in jobs:
            jobs[name] = []

    def phase_barrier(tag):
        target = cnt["PH"] + 5

        @J("scalar")
        def _(eng):
            W(eng, "SQ", cnt["SQ"])
            eng.drain()
            INC("PH", eng.nop())
            W(eng, "PH", target)

        @J("gpsimd")
        def _(eng):
            W(eng, "GQ", cnt["GQ"])
            W(eng, "GQ2", cnt["GQ2"])
            eng.drain()
            INC("PH", eng.nop())
            W(eng, "PH", target)
        for name in ["sync", "vector", "tensor"]:
            @J(name)
            def _(eng, name=name):
                eng.drain()
                INC("PH", eng.nop())
                W(eng, "PH", target)
        run_jobs()
        assert cnt["PH"] == target, (tag, cnt["PH"], target)

    p1 = ExitStack()
    xT = p1.enter_context(nc.sbuf_tensor("xT", [128, DT, T], BF16))
    xb = p1.enter_context(nc.sbuf_tensor("xb", [128, 1, D], BF16))
    xf = p1.enter_context(nc.sbuf_tensor("xf", [128, 1, D], F32))
    wb = p1.enter_context(nc.sbuf_tensor("wb", [128, 2, D], BF16))
    wf = p1.enter_context(nc.sbuf_tensor("wf", [128, 2, D], F32))
    wt = p1.enter_context(nc.sbuf_tensor("wt", [128, 2, DT * 128], BF16))
    wtv = p1.enter_context(nc.sbuf_tensor("wtv", [128, DT, 512], BF16))
    stg = p1.enter_context(nc.sbuf_tensor("stg", [128, 2, 512], BF16))
    ps_t = p1.enter_context(nc.psum_tensor("ps_t", [128, 1024], BF16))
    ps_t2 = p1.enter_context(nc.psum_tensor("ps_t2", [128, 1024], BF16))
    ps_m = p1.enter_context(nc.psum_tensor("ps_m", [128, 512], F32))
    ps_m2 = p1.enter_context(nc.psum_tensor("ps_m2", [128, 512], F32))

    @J("sync")
    def _(sync):
        for t_, s_ in [(ident, ident_in), (idxloc, idxloc_in), (quart, quart_in),
                       (diagT, diag_in), (qdec, qdec_in), (kdecpp, kdec_in),
                       (theta_pp, theta_pp_in), (cv, cv_in), (onescol, onesc_in),
                       (onesrow, onesr_in)]:
            INC("LQ", sync.dma_start(t_[:], s_[:, :]), 16)
        marks["consts"] = cnt["LQ"]

    @J("sync")
    def _(sync):
        for tt in range(NCH):
            if tt >= 1:
                WM(sync, "SV", ("xcast", tt - 1))
            INC("LQ", sync.dma_start(xf[:, 0, :], x_in[tt * 128:(tt + 1) * 128, :]), 16,
                ("xldf", tt))
        for o in range(32 if P1SUB >= 2 else 0):
            if o >= 2:
                WM(sync, "SV", ("wcast", o - 2))
            INC("LQ", sync.dma_start(wf[:, o % 2, :], wqkv[o * 128:(o + 1) * 128, :]), 16,
                ("wldf", o))
        for vsl in range(4 if P1SUB >= 3 else 0):
            for oo in range(4):
                o = 4 * vsl + oo
                if o >= 2:
                    WM(sync, "SV", ("vwcast", o - 2))
                elif P1SUB >= 2:
                    WM(sync, "SV", ("wcast", 30 + o))
                INC("LQ", sync.dma_start(wf[:, o % 2, :], wqkv[(32 + o) * 128:(33 + o) * 128, :]), 16,
                    ("vwldf", o))

    @J("vector")
    def _(vector):
        for tt in range(NCH):
            WM(vector, "LQ", ("xldf", tt))
            if tt >= 1:
                WM(vector, "ST", ("xtr2", tt - 1))
            INC("SV", vector.tensor_copy(xb[:, 0, :], xf[:, 0, :]), 1, ("xcast", tt))
        for o in range(32 if P1SUB >= 2 else 0):
            WM(vector, "LQ", ("wldf", o))
            if o >= 2:
                WM(vector, "ST", ("wtr2", o - 2))
            INC("SV", vector.tensor_copy(wb[:, o % 2, :], wf[:, o % 2, :]), 1, ("wcast", o))
        for vsl in range(4 if P1SUB >= 3 else 0):
            for oo in range(4):
                o = 4 * vsl + oo
                WM(vector, "LQ", ("vwldf", o))
                if o >= 2:
                    WM(vector, "ST", ("vwtr2", o - 2))
                elif P1SUB >= 2:
                    WM(vector, "ST", ("wtr2", 30 + o))
                INC("SV", vector.tensor_copy(wb[:, o % 2, :], wf[:, o % 2, :]), 1, ("vwcast", o))

    @J("tensor")
    def _(tensor):
        for tt in range(NCH):
            WM(tensor, "SV", ("xcast", tt))
            if tt >= 1:
                WM(tensor, "SS", ("xev1", tt - 1))
            for k in range(8):
                inst = tensor.transpose(ps_t[:, k * 128:(k + 1) * 128],
                                        xb[:, 0, k * 128:(k + 1) * 128], ident[:])
            INC("ST", inst, 1, ("xtr1", tt))
            if tt >= 1:
                WM(tensor, "SS", ("xev2", tt - 1))
            for k in range(8):
                inst = tensor.transpose(ps_t2[:, k * 128:(k + 1) * 128],
                                        xb[:, 0, (8 + k) * 128:(9 + k) * 128], ident[:])
            INC("ST", inst, 1, ("xtr2", tt))
        for o in range(32 if P1SUB >= 2 else 0):
            WM(tensor, "SV", ("wcast", o))
            WM(tensor, "SS", ("xev2", NCH - 1) if o == 0 else ("wev1", o - 1))
            for k in range(8):
                inst = tensor.transpose(ps_t[:, k * 128:(k + 1) * 128],
                                        wb[:, o % 2, k * 128:(k + 1) * 128], ident[:])
            INC("ST", inst, 1, ("wtr1", o))
            if o >= 1:
                WM(tensor, "SS", ("wev2", o - 1))
            for k in range(8):
                inst = tensor.transpose(ps_t2[:, k * 128:(k + 1) * 128],
                                        wb[:, o % 2, (8 + k) * 128:(9 + k) * 128], ident[:])
            INC("ST", inst, 1, ("wtr2", o))
            WM(tensor, "SS", ("wev2", o))
            for tc in range(8):
                psm = ps_m if tc % 2 == 0 else ps_m2
                if tc >= 2:
                    WM(tensor, "SS", ("qkev", o, tc - 2))
                elif o >= 1:
                    WM(tensor, "SS", ("qkev", o - 1, 6 + tc))
                for k in range(DT):
                    inst = tensor.matmul(psm[:, :], wt[:, o % 2, k * 128:(k + 1) * 128],
                                         xT[:, k, tc * 512:(tc + 1) * 512],
                                         start=(k == 0), stop=(k == DT - 1))
                INC("ST", inst, 1, ("qkmm", o, tc))
        ci = 0
        for vsl in range(4 if P1SUB >= 3 else 0):
            for oo in range(4):
                o = 4 * vsl + oo
                WM(tensor, "SV", ("vwcast", o))
                WM(tensor, "SS", ("wev2", 31) if o == 0 else ("vwev1", o - 1))
                if vsl >= 1:
                    WM(tensor, "SS", ("vev", 32 * vsl - 1))
                for k in range(8):
                    inst = tensor.transpose(ps_t[:, k * 128:(k + 1) * 128],
                                            wb[:, o % 2, k * 128:(k + 1) * 128], ident[:])
                INC("ST", inst, 1, ("vwtr1", o))
                if o >= 1:
                    WM(tensor, "SS", ("vwev2", o - 1))
                for k in range(8):
                    inst = tensor.transpose(ps_t2[:, k * 128:(k + 1) * 128],
                                            wb[:, o % 2, (8 + k) * 128:(9 + k) * 128], ident[:])
                INC("ST", inst, 1, ("vwtr2", o))
            WM(tensor, "SS", ("vwev2", 4 * vsl + 3))
            for tt in range(NCH):
                psm = ps_m if ci % 2 == 0 else ps_m2
                if ci >= 2:
                    WM(tensor, "SS", ("vev", ci - 2))
                elif vsl == 0 and ci < 2:
                    WM(tensor, "SS", ("qkev", 31, 6 + ci))
                for k in range(DT):
                    inst = tensor.matmul(psm[:, :], xT[:, k, tt * 128:(tt + 1) * 128],
                                         wtv[:, k, :], start=(k == 0), stop=(k == DT - 1))
                INC("ST", inst, 1, ("vmm", ci))
                ci += 1

    @J("scalar")
    def _(scalar):
        for tt in range(NCH):
            WM(scalar, "ST", ("xtr1", tt))
            for k in range(8):
                INC("SS", scalar.copy(xT[:, k, tt * 128:(tt + 1) * 128], ps_t[:, k * 128:(k + 1) * 128]),
                    1, ("xev1", tt) if k == 7 else None)
            WM(scalar, "ST", ("xtr2", tt))
            for k in range(8):
                INC("SS", scalar.copy(xT[:, 8 + k, tt * 128:(tt + 1) * 128], ps_t2[:, k * 128:(k + 1) * 128]),
                    1, ("xev2", tt) if k == 7 else None)
        WM(scalar, "SS", ("xev2", NCH - 1))
        for k in range(DT):
            INC("SQ", scalar.dma_start(xTs[k * 128:(k + 1) * 128, :], xT[:, k, :]), 16)
        for o in range(32 if P1SUB >= 2 else 0):
            WM(scalar, "ST", ("wtr1", o))
            for k in range(8):
                INC("SS", scalar.copy(wt[:, o % 2, k * 128:(k + 1) * 128], ps_t[:, k * 128:(k + 1) * 128]),
                    1, ("wev1", o) if k == 7 else None)
            WM(scalar, "ST", ("wtr2", o))
            for k in range(8):
                INC("SS", scalar.copy(wt[:, o % 2, (8 + k) * 128:(9 + k) * 128], ps_t2[:, k * 128:(k + 1) * 128]),
                    1, ("wev2", o) if k == 7 else None)
            dst = qs if o < DT else ks
            row0 = (o % DT) * 128
            for tc in range(8):
                WM(scalar, "ST", ("qkmm", o, tc))
                if not (o == 0 and tc < 2):
                    WM(scalar, "SQ", ("qkdma", o, tc - 2) if tc >= 2 else ("qkdma", o - 1, 6 + tc))
                psm = ps_m if tc % 2 == 0 else ps_m2
                INC("SS", scalar.copy(stg[:, tc % 2, :], psm[:, :]), 1, ("qkev", o, tc))
                WM(scalar, "SS", ("qkev", o, tc))
                INC("SQ", scalar.dma_start(dst[row0:row0 + 128, tc * 512:(tc + 1) * 512],
                                           stg[:, tc % 2, :]), 16, ("qkdma", o, tc))
        ci = 0
        for vsl in range(4 if P1SUB >= 3 else 0):
            for oo in range(4):
                o = 4 * vsl + oo
                WM(scalar, "ST", ("vwtr1", o))
                for k in range(8):
                    INC("SS", scalar.copy(wtv[:, k, oo * 128:(oo + 1) * 128], ps_t[:, k * 128:(k + 1) * 128]),
                        1, ("vwev1", o) if k == 7 else None)
                WM(scalar, "ST", ("vwtr2", o))
                for k in range(8):
                    INC("SS", scalar.copy(wtv[:, 8 + k, oo * 128:(oo + 1) * 128], ps_t2[:, k * 128:(k + 1) * 128]),
                        1, ("vwev2", o) if k == 7 else None)
            for tt in range(NCH):
                WM(scalar, "ST", ("vmm", ci))
                if ci >= 2:
                    WM(scalar, "SQ", ("vdma", ci - 2))
                elif ci < 2 and P1SUB >= 2:
                    WM(scalar, "SQ", ("qkdma", 31, 6 + ci))
                psm = ps_m if ci % 2 == 0 else ps_m2
                INC("SS", scalar.copy(stg[:, ci % 2, :], psm[:, :]), 1, ("vev", ci))
                WM(scalar, "SS", ("vev", ci))
                INC("SQ", scalar.dma_start(vs[tt * 128:(tt + 1) * 128, vsl * 512:(vsl + 1) * 512],
                                           stg[:, ci % 2, :]), 16, ("vdma", ci))
                ci += 1

    phase_barrier("p1")
    p1.close()
    if PHASES <= 1:
        blk_cm.__exit__(None, None, None)
        es.close()
        global LAST_MARKS
        LAST_MARKS = dict(marks)
        return nc

    p2 = ExitStack()
    wpt = p2.enter_context(nc.sbuf_tensor("wpt", [128, DT, D], BF16))
    r_g = p2.enter_context(nc.sbuf_tensor("r_g", [128, DT, GT], BF16))
    st32 = p2.enter_context(nc.sbuf_tensor("st32", [128, 2 * H, 128], F32))
    stbf = p2.enter_context(nc.sbuf_tensor("stbf", [128, 2 * H, 128], BF16))
    cil = p2.enter_context(nc.sbuf_tensor("cil", [128, H, 128], BF16))
    silt = p2.enter_context(nc.sbuf_tensor("silt", [128, H, 128], BF16))


    ps_T = p2.enter_context(nc.psum_tensor("p2_T", [128, 1024], BF16))
    ps_T2 = p2.enter_context(nc.psum_tensor("p2_Tb", [128, 1024], BF16))
    ps_S = p2.enter_context(nc.psum_tensor("p2_S", [128, 512], F32))
    ps_O = p2.enter_context(nc.psum_tensor("p2_O", [128, 512], F32))
    ps_U = p2.enter_context(nc.psum_tensor("p2_U", [128, 512], F32))
    ps_P = p2.enter_context(nc.psum_tensor("p2_P", [128, 512], F32))
    ps_st = p2.enter_context(nc.psum_tensor("p2_st", [128, 512], F32))
    ps_bc = p2.enter_context(nc.psum_tensor("p2_bc", [128, 512], F32))

    ploc = ExitStack()
    loc = ploc.enter_context(nc.sbuf_tensor("loc", [128, 4, 128], F32))
    loci = ploc.enter_context(nc.sbuf_tensor("loci", [128, 2, 128], I32))
    wbp = ploc.enter_context(nc.sbuf_tensor("wbp", [128, 1, D], BF16))
    wfp = ploc.enter_context(nc.sbuf_tensor("wfp", [128, 1, D], F32))

    @J("sync")
    def _(sync):
        for o in range(DT):
            if o >= 1:
                WM(sync, "SV", ("pcast", o - 1))
            INC("LQ", sync.dma_start(wfp[:, 0, :], wproj[o * 128:(o + 1) * 128, :]), 16,
                ("pldf", o))

    @J("vector")
    def _(vector):
        for o in range(DT):
            WM(vector, "LQ", ("pldf", o))
            if o >= 1:
                WM(vector, "ST", ("ptr2", o - 1))
            INC("SV", vector.tensor_copy(wbp[:, 0, :], wfp[:, 0, :]), 1, ("pcast", o))

    @J("vector")
    def _(vector):
        WM(vector, "LQ", "consts")
        for hh in range(H):
            if hh >= 1:
                WM(vector, "SS", ("loct", hh - 1))
            vector.tensor_scalar(loc[:, 0, :], idxloc[:], theta_pp[:, hh:hh + 1], None, ALU.mult)
            vector.tensor_scalar(loc[:, 1, :], loc[:, 0, :], 1.0 / TWO_PI, None, ALU.mult)
            vector.tensor_copy(loci[:, 0, :], loc[:, 1, :])
            vector.scalar_tensor_tensor(loc[:, 2, :], loc[:, 0, :], 1.0 / TWO_PI, quart[:], ALU.mult, ALU.add)
            INC("SV", vector.tensor_copy(loci[:, 1, :], loc[:, 2, :]), 1, ("loci", hh))
            WM(vector, "SS", ("locf", hh))
            vector.scalar_tensor_tensor(loc[:, 3, :], loc[:, 1, :], -TWO_PI, loc[:, 0, :], ALU.mult, ALU.add)
            vector.scalar_tensor_tensor(loc[:, 2, :], loc[:, 2, :], -TWO_PI, loc[:, 0, :], ALU.mult, ALU.add)
            vector.tensor_scalar(loc[:, 1, :], loc[:, 3, :], PI, None, ALU.is_gt)
            vector.scalar_tensor_tensor(loc[:, 3, :], loc[:, 1, :], -TWO_PI, loc[:, 3, :], ALU.mult, ALU.add)
            vector.tensor_scalar(loc[:, 1, :], loc[:, 2, :], PI / 2, None, ALU.is_gt)
            INC("SV", vector.scalar_tensor_tensor(loc[:, 2, :], loc[:, 1, :], -TWO_PI, loc[:, 2, :], ALU.mult, ALU.add),
                1, ("locm", hh))
        vector.memset(st32[:], 0.0)
        INC("SV", vector.memset(stbf[:], 0.0), 1, ("stinit",))

    @J("tensor")
    def _(tensor):
        for o in range(DT):
            WM(tensor, "SV", ("pcast", o))
            if o >= 1:
                WM(tensor, "SS", ("pev2", o - 1))
            for k in range(8):
                inst = tensor.transpose(ps_T[:, k * 128:(k + 1) * 128],
                                        wbp[:, 0, k * 128:(k + 1) * 128], ident[:])
            INC("ST", inst, 1, ("ptr1", o))
            WM(tensor, "SS", ("pev1", o))
            for k in range(8):
                inst = tensor.transpose(ps_T[:, k * 128:(k + 1) * 128],
                                        wbp[:, 0, (8 + k) * 128:(9 + k) * 128], ident[:])
            INC("ST", inst, 1, ("ptr2", o))

    @J("scalar")
    def _(scalar):
        for o in range(DT):
            WM(scalar, "ST", ("ptr1", o))
            for k in range(8):
                INC("SS", scalar.copy(wpt[:, k, o * 128:(o + 1) * 128], ps_T[:, k * 128:(k + 1) * 128]),
                    1, ("pev1", o) if k == 7 else None)
            WM(scalar, "ST", ("ptr2", o))
            for k in range(8):
                INC("SS", scalar.copy(wpt[:, 8 + k, o * 128:(o + 1) * 128], ps_T[:, k * 128:(k + 1) * 128]),
                    1, ("pev2", o) if k == 7 else None)
        for hh in range(H):
            WM(scalar, "SV", ("loci", hh))
            scalar.copy(loc[:, 1, :], loci[:, 0, :])
            INC("SS", scalar.copy(loc[:, 2, :], loci[:, 1, :]), 1, ("locf", hh))
            WM(scalar, "SV", ("locm", hh))
            scalar.activation(silt[:, hh, :], loc[:, 3, :], AF.Sin, bias=c_ap(0.0), scale=c_ap(1.0))
            INC("SS", scalar.activation(cil[:, hh, :], loc[:, 2, :], AF.Sin, bias=c_ap(PI / 2), scale=c_ap(1.0)),
                1, ("loct", hh))

    @J("scalar")
    def _(scalar):
        if DEBUG:
            WM(scalar, "SS", ("loct", H - 1))
            INC("SQ", scalar.dma_start(tdmp[:, 0, :], cil[:, :, :]), 16)
            INC("SQ", scalar.dma_start(tdmp[:, 1, :], silt[:, :, :]), 16)

    phase_barrier("p2prolog")
    ploc.close()
    q_c = p2.enter_context(nc.sbuf_tensor("q_c", [128, 2, H, 128], BF16))
    k_c = p2.enter_context(nc.sbuf_tensor("k_c", [128, 2, H, 128], BF16))
    v_c = p2.enter_context(nc.sbuf_tensor("v_c", [128, 2, D], BF16))
    xT_g = p2.enter_context(nc.sbuf_tensor("xT_g", [128, DT, GT], BF16))
    y_g = p2.enter_context(nc.sbuf_tensor("y_g", [128, H, GT], BF16))
    tabw = p2.enter_context(nc.sbuf_tensor("tabw", [128, 3, 128], F32))
    lrp = p2.enter_context(nc.sbuf_tensor("lrp", [128, 2, 6, 128], BF16))
    kT = p2.enter_context(nc.sbuf_tensor("kT", [128, 2, 2, 128], BF16))
    scb = p2.enter_context(nc.sbuf_tensor("scb", [128, 2, 128], BF16))
    c0s0 = p2.enter_context(nc.sbuf_tensor("c0s0", [128, 2, 128], F32))
    rrq = p2.enter_context(nc.sbuf_tensor("rrq", [128, 3, 128], F32))
    rri = p2.enter_context(nc.sbuf_tensor("rri", [128, 2, 128], I32))
    stats = p2.enter_context(nc.sbuf_tensor("stats", [1, 6, GT], F32))
    rsq = p2.enter_context(nc.sbuf_tensor("rsq", [128, 2, GT], BF16))
    lnt = p2.enter_context(nc.sbuf_tensor("lnt", [128, 2, GT], BF16))


    def prev_ch(C_abs, hh):
        pc = C_abs - 1
        return (pc // 4, pc % 4, hh)

    @J("sync")
    def _(sync):
        for g in range(NG):
            if g >= 1:
                W(sync, "GRP", g)
            for k in range(DT):
                sync.dma_start(xT_g[:, k, :], xTs[k * 128:(k + 1) * 128, g * GT:(g + 1) * GT]).then_inc(sems["LQ"], 16)
                cnt["LQ"] += 16
            marks[("xtg", g)] = cnt["LQ"]
            for c in range(4):
                C_abs = 4 * g + c
                par = C_abs % 2
                if C_abs >= 2:
                    WM(sync, "SV", ("A", (C_abs - 2) // 4, (C_abs - 2) % 4, H - 1))
                    WM(sync, "ST", ("F", (C_abs - 2) // 4, (C_abs - 2) % 4, H - 1))
                for hh in range(H):
                    sync.dma_start(q_c[:, par, hh, :],
                                   qs[hh * 128:(hh + 1) * 128, C_abs * 128:(C_abs + 1) * 128]).then_inc(sems["LQ"], 16)
                    cnt["LQ"] += 16
                for hh in range(H):
                    sync.dma_start(k_c[:, par, hh, :],
                                   ks[hh * 128:(hh + 1) * 128, C_abs * 128:(C_abs + 1) * 128]).then_inc(sems["LQ"], 16)
                    cnt["LQ"] += 16
                sync.dma_start(v_c[:, par, :], vs[C_abs * 128:(C_abs + 1) * 128, :]).then_inc(sems["LQ"], 16)
                cnt["LQ"] += 16
                marks[("chld", C_abs)] = cnt["LQ"]

    for g in range(NG):
        for c in range(4):
            C_abs = 4 * g + c
            cpar = C_abs % 2
            t0f = float(C_abs * CH)

            @J("vector")
            def _(vector, g=g, c=c, t0f=t0f, C_abs=C_abs):
                WM(vector, "LQ", ("chld", C_abs))
                if C_abs >= 1:
                    WM(vector, "SS", ("ts2", (C_abs - 1) // 4, (C_abs - 1) % 4))
                    WM(vector, "ST", ("D", (C_abs - 1) // 4, (C_abs - 1) % 4, H - 1))
                vector.tensor_scalar(rrq[:, 0, :], theta_pp[:], t0f, None, ALU.mult)
                vector.tensor_scalar(rrq[:, 1, :], rrq[:, 0, :], 1.0 / TWO_PI, None, ALU.mult)
                vector.tensor_copy(rri[:, 0, :], rrq[:, 1, :])
                vector.scalar_tensor_tensor(rrq[:, 2, :], rrq[:, 0, :], 1.0 / TWO_PI, quart[:], ALU.mult, ALU.add)
                INC("SV", vector.tensor_copy(rri[:, 1, :], rrq[:, 2, :]), 1, ("tv1", g, c))

            @J("scalar")
            def _(scalar, g=g, c=c):
                WM(scalar, "SV", ("tv1", g, c))
                scalar.copy(rrq[:, 1, :], rri[:, 0, :])
                INC("SS", scalar.copy(rrq[:, 2, :], rri[:, 1, :]), 1, ("ts1", g, c))

            @J("vector")
            def _(vector, g=g, c=c):
                WM(vector, "SS", ("ts1", g, c))
                vector.scalar_tensor_tensor(rrq[:, 1, :], rrq[:, 1, :], -TWO_PI, rrq[:, 0, :], ALU.mult, ALU.add)
                vector.scalar_tensor_tensor(rrq[:, 2, :], rrq[:, 2, :], -TWO_PI, rrq[:, 0, :], ALU.mult, ALU.add)
                vector.tensor_scalar(rrq[:, 0, :], rrq[:, 1, :], PI, None, ALU.is_gt)
                vector.scalar_tensor_tensor(rrq[:, 1, :], rrq[:, 0, :], -TWO_PI, rrq[:, 1, :], ALU.mult, ALU.add)
                vector.tensor_scalar(rrq[:, 0, :], rrq[:, 2, :], PI / 2, None, ALU.is_gt)
                INC("SV", vector.scalar_tensor_tensor(rrq[:, 2, :], rrq[:, 0, :], -TWO_PI, rrq[:, 2, :], ALU.mult, ALU.add),
                    1, ("tv2", g, c))

            @J("scalar")
            def _(scalar, g=g, c=c):
                WM(scalar, "SV", ("tv2", g, c))
                scalar.activation(c0s0[:, 1, :], rrq[:, 1, :], AF.Sin, bias=c_ap(0.0), scale=c_ap(1.0))
                INC("SS", scalar.activation(c0s0[:, 0, :], rrq[:, 2, :], AF.Sin, bias=c_ap(PI / 2), scale=c_ap(1.0)),
                    1, ("ts2", g, c))

            for h in range(H):
                hp = h % 2

                @J("vector")
                def _(vector, g=g, c=c, h=h, hp=hp, C_abs=C_abs, cpar=cpar):
                    WM(vector, "SS", ("ts2", g, c))
                    if C_abs > 0 or h >= 2:
                        key = ("F", g, c, h - 2) if h >= 2 else ("F",) + prev_ch(C_abs, H - 2 + h)
                        WM(vector, "ST", key)
                    vector.tensor_scalar(tabw[:, 0, :], silt[:, h, :], c0s0[:, 1, h:h + 1], None, ALU.mult)
                    vector.scalar_tensor_tensor(tabw[:, 1, :], cil[:, h, :], c0s0[:, 0, h:h + 1], tabw[:, 0, :], ALU.mult, ALU.subtract)
                    vector.tensor_scalar(tabw[:, 0, :], cil[:, h, :], c0s0[:, 1, h:h + 1], None, ALU.mult)
                    vector.scalar_tensor_tensor(tabw[:, 2, :], silt[:, h, :], c0s0[:, 0, h:h + 1], tabw[:, 0, :], ALU.mult, ALU.add)
                    qsl = q_c[:, cpar, h, :]
                    ksl = k_c[:, cpar, h, :]
                    vector.tensor_tensor(lrp[:, hp, 0, :], qsl, tabw[:, 1, :], ALU.mult)
                    vector.tensor_tensor(lrp[:, hp, 1, :], qsl, tabw[:, 2, :], ALU.mult)
                    vector.tensor_tensor(lrp[:, hp, 2, :], lrp[:, hp, 0, :], qdec[:, h * 128:(h + 1) * 128], ALU.mult)
                    vector.tensor_tensor(lrp[:, hp, 3, :], lrp[:, hp, 1, :], qdec[:, h * 128:(h + 1) * 128], ALU.mult)
                    vector.tensor_tensor(lrp[:, hp, 4, :], ksl, tabw[:, 1, :], ALU.mult)
                    INC("SV", vector.tensor_tensor(lrp[:, hp, 5, :], ksl, tabw[:, 2, :], ALU.mult), 1, ("A", g, c, h))

                @J("scalar")
                def _(scalar, g=g, c=c, h=h, hp=hp, C_abs=C_abs):
                    if DEBUG and C_abs == 6 and h == 0:
                        WM(scalar, "SS", ("ts2", g, c))
                        INC("SQ", scalar.dma_start(cdmp[:, :, :], c0s0[:, :, 0:H]), 16)
                        WM(scalar, "SV", ("A", g, c, h))
                        INC("SQ", scalar.dma_start(ldmp[:, :, :], lrp[:, 0, :, :]), 16)

                @J("tensor")
                def _(tensor, g=g, c=c, h=h, hp=hp, C_abs=C_abs):
                    WM(tensor, "SV", ("A", g, c, h))
                    if C_abs > 0 or h >= 2:
                        key = ("C", g, c, h - 2) if h >= 2 else ("C",) + prev_ch(C_abs, H - 2 + h)
                        WM(tensor, "SS", key)
                    psT = ps_T if hp == 0 else ps_T2
                    tensor.transpose(psT[:, 0:128], lrp[:, hp, 4, :], ident[:])
                    INC("ST", tensor.transpose(psT[:, 128:256], lrp[:, hp, 5, :], ident[:]),
                        1, ("B", g, c, h))

                @J("scalar")
                def _(scalar, g=g, c=c, h=h, hp=hp):
                    WM(scalar, "ST", ("B", g, c, h))
                    psT = ps_T if hp == 0 else ps_T2
                    scalar.activation(kT[:, hp, 0, :], psT[:, 0:128], AF.Copy,
                                      bias=0.0, scale=kdecpp[:, h:h + 1])
                    INC("SS", scalar.activation(kT[:, hp, 1, :], psT[:, 128:256], AF.Copy,
                                                bias=0.0, scale=kdecpp[:, h:h + 1]), 1, ("C", g, c, h))

                @J("tensor")
                def _(tensor, g=g, c=c, h=h, hp=hp, C_abs=C_abs):
                    if C_abs > 0 or h >= 1:
                        key = ("Esc", g, c, h - 1) if h >= 1 else ("Esc",) + prev_ch(C_abs, H - 1)
                        WM(tensor, "SV", key)
                    tensor.matmul(ps_S[:, 0:128], lrp[:, hp, 4, :], lrp[:, hp, 0, :], start=True, stop=False)
                    INC("ST", tensor.matmul(ps_S[:, 0:128], lrp[:, hp, 5, :], lrp[:, hp, 1, :], start=False, stop=True),
                        1, ("D", g, c, h))

                @J("vector")
                def _(vector, g=g, c=c, h=h, hp=hp):
                    WM(vector, "ST", ("D", g, c, h))
                    INC("SV", vector.tensor_tensor(scb[:, hp, :], ps_S[:, 0:128], diagT[:, h * 128:(h + 1) * 128], ALU.mult),
                        1, ("Esc", g, c, h))

                @J("scalar")
                def _(scalar, g=g, c=c, h=h, hp=hp, C_abs=C_abs):
                    if DEBUG and C_abs == 6 and h == 0:
                        WM(scalar, "SV", ("Esc", g, c, h))
                        INC("SQ", scalar.dma_start(sdmp[:, 0, :], scb[:, 0, :]), 16)
                        WM(scalar, "SS", ("C", g, c, h))
                        INC("SQ", scalar.dma_start(sdmp[:, 1, :], kT[:, 0, 0, :]), 16)
                        INC("SQ", scalar.dma_start(sdmp[:, 2, :], stbf[:, 0, :]), 16)
                        INC("SQ", scalar.dma_start(sdmp[:, 3, :], stbf[:, 1, :]), 16)
                        INC("SQ", scalar.dma_start(sdmp[:, 4, :], lrp[:, 0, 2, :]), 16)
                        INC("SQ", scalar.dma_start(sdmp[:, 5, :], lrp[:, 0, 3, :]), 16)

                @J("tensor")
                def _(tensor, g=g, c=c, h=h, hp=hp, C_abs=C_abs, cpar=cpar):
                    WM(tensor, "SV", ("Esc", g, c, h))
                    WM(tensor, "SS", ("C", g, c, h))
                    if C_abs > 0 or h >= 1:
                        gk = ("G", g, c, h - 1) if h >= 1 else ("G",) + prev_ch(C_abs, H - 1)
                        hk = ("Hst", g, c, h - 1) if h >= 1 else ("Hst",) + prev_ch(C_abs, H - 1)
                        WM(tensor, "SS", gk)
                        WM(tensor, "SV", hk)
                    if C_abs >= 1:
                        WM(tensor, "SV", ("Hst",) + prev_ch(C_abs, h))
                    vsl = v_c[:, cpar, h * 128:(h + 1) * 128]
                    tensor.matmul(ps_O[:, 0:128], vsl, scb[:, hp, :], start=True, stop=False)
                    tensor.matmul(ps_O[:, 0:128], stbf[:, 2 * h, :], lrp[:, hp, 2, :], start=False, stop=False)
                    tensor.matmul(ps_O[:, 0:128], stbf[:, 2 * h + 1, :], lrp[:, hp, 3, :], start=False, stop=True)
                    tensor.matmul(ps_U[:, 0:128], kT[:, hp, 0, :], vsl, start=True, stop=True)
                    INC("ST", tensor.matmul(ps_U[:, 128:256], kT[:, hp, 1, :], vsl, start=True, stop=True),
                        1, ("F", g, c, h))

                @J("scalar")
                def _(scalar, g=g, c=c, h=h):
                    WM(scalar, "ST", ("F", g, c, h))
                    INC("SS", scalar.copy(y_g[:, h, c * 128:(c + 1) * 128], ps_O[:, 0:128]), 1, ("G", g, c, h))

                @J("vector")
                def _(vector, g=g, c=c, h=h, bd=bdec[h]):
                    WM(vector, "ST", ("F", g, c, h))
                    vector.scalar_tensor_tensor(st32[:, 2 * h, :], st32[:, 2 * h, :], bd, ps_U[:, 0:128], ALU.mult, ALU.add)
                    vector.scalar_tensor_tensor(st32[:, 2 * h + 1, :], st32[:, 2 * h + 1, :], bd, ps_U[:, 128:256], ALU.mult, ALU.add)
                    vector.tensor_copy(stbf[:, 2 * h, :], st32[:, 2 * h, :])
                    INC("SV", vector.tensor_copy(stbf[:, 2 * h + 1, :], st32[:, 2 * h + 1, :]), 1, ("Hst", g, c, h))

        @J("tensor")
        def _(tensor, g=g):
            WM(tensor, "SS", ("G", g, 3, H - 1))
            WM(tensor, "LQ", ("xtg", g))
            for ot in range(DT):
                psm = ps_P
                if ot >= 1:
                    WM(tensor, "SV", ("rres", g, ot - 1))
                for k in range(DT):
                    inst = tensor.matmul(psm[:, :], wpt[:, k, ot * 128:(ot + 1) * 128], y_g[:, k, :],
                                         start=(k == 0), stop=(k == DT - 1))
                INC("ST", inst, 1, ("pjmm", g, ot))
            for ot in range(DT):
                WM(tensor, "SV", ("rres", g, ot))
                INC("ST", tensor.matmul(ps_st[0:1, 0:512], onescol[:], xT_g[:, ot, :],
                                        start=(ot == 0), stop=(ot == DT - 1)), 1, ("summ", g, ot))
            for ot in range(DT):
                WM(tensor, "SS", ("rsq", g, ot))
                INC("ST", tensor.matmul(ps_bc[0:1, 0:512], onescol[:], rsq[:, ot % 2, :],
                                        start=(ot == 0), stop=(ot == DT - 1)), 1, ("sqmm", g, ot))
            WM(tensor, "SV", ("fin", g))
            tensor.matmul(ps_st[:, 0:512], onesrow[:, :], stats[0:1, 0, :], start=True, stop=True)
            INC("ST", tensor.matmul(ps_bc[:, 0:512], onesrow[:, :], stats[0:1, 3, :], start=True, stop=True),
                1, ("bcast", g))

        @J("vector")
        def _(vector, g=g):
            for ot in range(DT):
                WM(vector, "ST", ("pjmm", g, ot))
                INC("SV", vector.tensor_tensor(xT_g[:, ot, :], ps_P[:, :], xT_g[:, ot, :], ALU.add),
                    1, ("rres", g, ot))
            WM(vector, "ST", ("summ", g, DT - 1))
            WM(vector, "ST", ("sqmm", g, DT - 1))
            vector.tensor_scalar(stats[0:1, 0, :], ps_st[0:1, 0:512], 1.0 / D, None, ALU.mult)
            vector.tensor_scalar(stats[0:1, 1, :], ps_bc[0:1, 0:512], 1.0 / D, None, ALU.mult)
            vector.tensor_tensor(stats[0:1, 2, :], stats[0:1, 0, :], stats[0:1, 0, :], ALU.mult)
            INC("SV", vector.tensor_tensor(stats[0:1, 2, :], stats[0:1, 1, :], stats[0:1, 2, :], ALU.subtract),
                1, ("var", g))
            WM(vector, "SS", ("sqv", g))
            vector.reciprocal(stats[0:1, 3, :], stats[0:1, 5, :])
            INC("SV", vector.tensor_tensor(stats[0:1, 4, :], stats[0:1, 0, :], stats[0:1, 3, :], ALU.mult),
                1, ("fin", g))
            for ot in range(DT):
                WM(vector, "ST", ("bcast", g))
                if ot >= 2:
                    WM(vector, "SQ", ("lndma", g, ot - 2))
                vector.tensor_tensor(lnt[:, ot % 2, :], xT_g[:, ot, :], ps_st[:, 0:512], ALU.subtract)
                INC("SV", vector.tensor_tensor(lnt[:, ot % 2, :], lnt[:, ot % 2, :], ps_bc[:, 0:512], ALU.mult),
                    1, ("ln", g, ot))
            INC("SV", vector.engine_nop(), 1, ("gdone", g))

        @J("scalar")
        def _(scalar, g=g):
            if DEBUG:
                WM(scalar, "SS", ("G", g, 3, H - 1))
                for hh in range(H):
                    INC("SQ", scalar.dma_start(ysd[hh * 128:(hh + 1) * 128, g * GT:(g + 1) * GT],
                                               y_g[:, hh, :]), 16)
            for ot in range(DT):
                WM(scalar, "SV", ("rres", g, ot))
                if ot >= 2:
                    WM(scalar, "ST", ("sqmm", g, ot - 2))
                INC("SS", scalar.activation(rsq[:, ot % 2, :], xT_g[:, ot, :], AF.Square,
                                            bias=c_ap(0.0), scale=c_ap(1.0)), 1, ("rsq", g, ot))
            WM(scalar, "SV", ("var", g))
            INC("SS", scalar.activation(stats[0:1, 5, :], stats[0:1, 2, :], AF.Sqrt,
                                        bias=c_ap1(1e-5), scale=c_ap1(1.0)), 1, ("sqv", g))
            for ot in range(DT):
                WM(scalar, "SV", ("ln", g, ot))
                INC("SQ", scalar.dma_start(ln1s[ot * 128:(ot + 1) * 128, g * GT:(g + 1) * GT],
                                           lnt[:, ot % 2, :]), 16, ("lndma", g, ot))

        @J("gpsimd")
        def _(gp, g=g):
            WM(gp, "SV", ("gdone", g))
            gp.sem_inc(sems["GRP"], 1)
            cnt["GRP"] += 1

    phase_barrier("p2")
    p2.close()
    if PHASES <= 2:
        blk_cm.__exit__(None, None, None)
        es.close()
        LAST_MARKS = dict(marks)
        return nc

    p3 = ExitStack()
    ps3_T = p3.enter_context(nc.psum_tensor("p3_T", [128, 1024], BF16))
    ps3_T2 = p3.enter_context(nc.psum_tensor("p3_T2", [128, 1024], BF16))
    ps3_T3 = p3.enter_context(nc.psum_tensor("p3_T3", [128, 1024], BF16))
    ps3_T4 = p3.enter_context(nc.psum_tensor("p3_T4", [128, 1024], BF16))
    ps3_G = p3.enter_context(nc.psum_tensor("p3_G", [128, 512], F32))
    ps3_G2 = p3.enter_context(nc.psum_tensor("p3_G2", [128, 512], F32))
    ps3_U = p3.enter_context(nc.psum_tensor("p3_U", [128, 512], F32))
    ps3_U2 = p3.enter_context(nc.psum_tensor("p3_U2", [128, 512], F32))
    TP = [(ps3_T, ps3_T2), (ps3_T3, ps3_T4)]

    def emit_cpool_init(cpool, mk):
        @J("gpsimd")
        def _(gp):
            gp.memset(cpool[:, 0, :], 251)
            gp.memset(cpool[:, 1, :], 127)
            INC("SP", gp.memset(cpool[:, 2, :], 126), 1, mk)

    DQ = {"q": []}

    def _dq_v1(d):
        @J("vector")
        def _(vector, d=d):
            WM(vector, d.get("ld_sem", "LQ"), d["ld_mark"])
            if d["prev2"] is not None:
                WM(vector, "SS", ("dqr", d["prev2"]))
            b = 5 * d["set"]
            dqa = d["dqa"]
            wN = d["wN"]
            L = 8 * wN
            src = d["src_ap_fn"]()
            vector.tensor_copy(dqa[:, b + 0, 0:wN], src)
            for j in range(1, 8):
                vector.tensor_scalar(dqa[:, b + 0, j * wN:(j + 1) * wN], src, 4 * j, None, ALU.logical_shift_right)
            vector.tensor_scalar(dqa[:, b + 1, 0:L], dqa[:, b + 0, 0:L], 0x7, None, ALU.bitwise_and)
            vector.tensor_scalar(dqa[:, b + 2, 0:L], dqa[:, b + 0, 0:L], 12, None, ALU.logical_shift_left)
            INC("SV", vector.tensor_scalar(dqa[:, b + 2, 0:L], dqa[:, b + 2, 0:L], 0x8000, None, ALU.bitwise_and),
                1, ("dqk", d["key"]))

    def _dq_p1(d):
        @J("gpsimd")
        def _(gp, d=d):
            WM(gp, "SV", ("dqk", d["key"]))
            b = 5 * d["set"]
            dqa = d["dqa"]
            cpool = d["cpool"]
            L = 8 * d["wN"]
            gp.tensor_tensor(dqa[:, b + 3, 0:L], dqa[:, b + 1, 0:L], cpool[:, 1, 0:L], ALU.add)
            INC("SP", gp.tensor_tensor(dqa[:, b + 4, 0:L], dqa[:, b + 1, 0:L], cpool[:, 2, 0:L], ALU.add),
                1, ("dqp1", d["key"]))

    def _dq_v2(d):
        @J("vector")
        def _(vector, d=d):
            WM(vector, "SP", ("dqp1", d["key"]))
            b = 5 * d["set"]
            dqa = d["dqa"]
            L = 8 * d["wN"]
            vector.tensor_scalar(dqa[:, b + 3, 0:L], dqa[:, b + 3, 0:L], 7, None, ALU.logical_shift_right)
            INC("SV", vector.tensor_scalar(dqa[:, b + 4, 0:L], dqa[:, b + 4, 0:L], 7, None, ALU.logical_shift_right),
                1, ("dqf", d["key"]))

    def _dq_p2(d):
        @J("gpsimd")
        def _(gp, d=d):
            WM(gp, "SV", ("dqf", d["key"]))
            b = 5 * d["set"]
            dqa = d["dqa"]
            cpool = d["cpool"]
            L = 8 * d["wN"]
            gp.tensor_tensor(dqa[:, b + 3, 0:L], dqa[:, b + 3, 0:L], cpool[:, 0, 0:L], ALU.mult)
            gp.tensor_tensor(dqa[:, b + 4, 0:L], dqa[:, b + 3, 0:L], dqa[:, b + 4, 0:L], ALU.add)
            INC("SP", gp.tensor_tensor(dqa[:, b + 4, 0:L], dqa[:, b + 4, 0:L], dqa[:, b + 1, 0:L], ALU.add),
                1, ("dqp2", d["key"]))

    def _dq_v3(d):
        @J("vector")
        def _(vector, d=d):
            WM(vector, "SP", ("dqp2", d["key"]))
            b = 5 * d["set"]
            dqa = d["dqa"]
            L = 8 * d["wN"]
            vector.tensor_scalar(dqa[:, b + 4, 0:L], dqa[:, b + 4, 0:L], 6, None, ALU.logical_shift_left)
            INC("SV", vector.tensor_tensor(dqa[:, b + 4, 0:L], dqa[:, b + 4, 0:L], dqa[:, b + 2, 0:L], ALU.bitwise_or),
                1, ("dqb", d["key"]))

    def _dq_a(d):
        @J("scalar")
        def _(scalar, d=d):
            WM(scalar, "SV", ("dqb", d["key"]))
            if d["sclc"] is not None:
                WM(scalar, "SV", d["sclc"])
            if d["dense_gate"] is not None:
                WM(scalar, "ST", d["dense_gate"])
            b = 5 * d["set"]
            dqa = d["dqa"]
            wN = d["wN"]
            L = 8 * wN
            ng = L // 128
            gw = wN // ng
            fb = dqa[:, b + 4, 0:L].bitcast(BF16)
            for gi in range(ng):
                srcap = bass.AP(fb.tensor, fb.offset + 2 * gi * gw, [[fb.ap[0][0], 128], [2 * wN, 8], [2, gw]])
                dst = bass.AP(d["dense_t"], d["dense_off"] + gi * 128, [[d["row_len"], 128], [1, 8], [8, gw]])
                inst = scalar.activation(dst, srcap, AF.Copy, bias=0.0, scale=d["scale_fn"](gi))
                if gi == ng - 1:
                    INC("SS", inst, 1, ("dqr", d["key"]))

    def _dq_round(r):
        q = DQ["q"]
        n = len(q)
        if 0 <= r - 2 < n and not q[r - 2]["s"][4]:
            _dq_v3(q[r - 2])
            q[r - 2]["s"][4] = True
        if 0 <= r - 1 < n and not q[r - 1]["s"][2]:
            _dq_v2(q[r - 1])
            q[r - 1]["s"][2] = True
        if r < n and not q[r]["s"][0]:
            _dq_v1(q[r])
            q[r]["s"][0] = True
        if 0 <= r - 1 < n and not q[r - 1]["s"][3]:
            _dq_p2(q[r - 1])
            q[r - 1]["s"][3] = True
        if r < n and not q[r]["s"][1]:
            _dq_p1(q[r])
            q[r]["s"][1] = True
        if 0 <= r - 2 < n and not q[r - 2]["s"][5]:
            _dq_a(q[r - 2])
            q[r - 2]["s"][5] = True

    def dq_pass(key, ld_mark, src_ap_fn, wN, dense_t, dense_off, row_len, scale_fn, dqa, cpool, sclc=None, dense_gate=None, ld_sem="LQ"):
        q = DQ["q"]
        idx = len(q)
        d = dict(key=key, ld_mark=ld_mark, src_ap_fn=src_ap_fn, wN=wN, dense_t=dense_t,
                 dense_off=dense_off, row_len=row_len, scale_fn=scale_fn, dqa=dqa,
                 cpool=cpool, sclc=sclc, dense_gate=dense_gate, set=idx % 2, ld_sem=ld_sem,
                 prev2=q[idx - 2]["key"] if idx >= 2 else None, s=[False] * 6)
        q.append(d)
        _dq_round(idx)

    def dq_drain():
        n = len(DQ["q"])
        _dq_round(n)
        _dq_round(n + 1)

    def dq_reset():
        DQ["q"] = []

    p3a = ExitStack()
    dqa_a = p3a.enter_context(nc.sbuf_tensor("dqa_a", [128, 10, 512], I32))
    cpool_a = p3a.enter_context(nc.sbuf_tensor("cpool_a", [128, 3, 512], I32))
    emit_cpool_init(cpool_a, ("p3ac",))
    ln1a = p3a.enter_context(nc.sbuf_tensor("ln1a", [128, DT, T], BF16))
    pk = p3a.enter_context(nc.sbuf_tensor("pk", [128, 2, 2, 256], I32))
    scl6 = p3a.enter_context(nc.sbuf_tensor("scl6", [128, 2 * IT, DT], F16))
    sclf = p3a.enter_context(nc.sbuf_tensor("sclf", [128, 2 * IT, DT], F32))
    dense = p3a.enter_context(nc.sbuf_tensor("dense", [128, 2, D], BF16))
    wtg = p3a.enter_context(nc.sbuf_tensor("wtg", [128, 2, 2, DT * 128], BF16))
    sg_sb = p3a.enter_context(nc.sbuf_tensor("sg_sb", [128, 2, 512], BF16))
    hstg = p3a.enter_context(nc.sbuf_tensor("hstg", [128, 2, 512], BF16))

    @J("sync")
    def _(sync):
        for b in range(2 * IT):
            sync.dma_start(scl6[:, b, :], gus[b * 128:(b + 1) * 128, :]).then_inc(sems["LQ"], 16)
            cnt["LQ"] += 16
        marks[("sclall",)] = cnt["LQ"]
        for k in range(DT):
            sync.dma_start(ln1a[:, k, :], ln1s[k * 128:(k + 1) * 128, :]).then_inc(sems["LQ"], 16)
            cnt["LQ"] += 16
        marks[("ln1a",)] = cnt["LQ"]
        for ig in range(IT):
            par = ig % 2
            gq = "GQ" if par == 0 else "GQ2"
            if ig >= 2:
                WM(sync, "SV", ("dqk", ("u", ig - 2, 3)))
            INC(gq, sync.dma_start(pk[:, par, 0, :], gup[ig * 128:(ig + 1) * 128, :]), 16)
            INC(gq, sync.dma_start(pk[:, par, 1, :], gup[IF + ig * 128:IF + (ig + 1) * 128, :]), 16,
                ("pkR", ig))

    @J("vector")
    def _(vector):
        WM(vector, "LQ", ("ln1a",))
        INC("SV", vector.tensor_copy(sclf[:], scl6[:]), 1, ("sclallc",))

    def emit_tr_group(tensor, tag, gu, igx, half):
        pst = TP[igx % 2][half]
        WM(tensor, "SS", ("dqr", (tag, igx, 2 * half + 1)))
        if gu == 1:
            WM(tensor, "SS", ("wtev", "g", igx, half))
        elif igx >= 2:
            WM(tensor, "SS", ("wtev", "u", igx - 2, half))
        for k in range(8):
            inst = tensor.transpose(pst[:, k * 128:(k + 1) * 128],
                                    dense[:, gu, (half * 8 + k) * 128:(half * 8 + k + 1) * 128], ident[:])
        INC("ST", inst, 1, ("wtr", tag, igx, half))

    def emit_evac(tag, gu, igx, half):
        @J("scalar")
        def _(scalar, tag=tag, gu=gu, igx=igx, half=half):
            WM(scalar, "ST", ("wtr", tag, igx, half))
            if igx >= 2:
                WM(scalar, "ST", ("umm", igx - 2, NG - 1))
            pst = TP[igx % 2][half]
            for k in range(8):
                INC("SS", scalar.copy(wtg[:, igx % 2, gu, (half * 8 + k) * 128:(half * 8 + k + 1) * 128],
                                      pst[:, k * 128:(k + 1) * 128]),
                    1, ("wtev", tag, igx, half) if k == 7 else None)

    def emit_silu(igx, gg):
        @J("scalar")
        def _(scalar, igx=igx, gg=gg):
            if gg >= 2:
                WM(scalar, "SV", ("hmul", igx, gg - 2))
            elif igx >= 1:
                WM(scalar, "SV", ("hmul", igx - 1, NG - 2 + gg))
            WM(scalar, "ST", ("gmm", igx, gg))
            psg = ps3_G if gg % 2 == 0 else ps3_G2
            INC("SS", scalar.activation(sg_sb[:, gg % 2, :], psg[:, :], AF.Silu,
                                        bias=c_ap(0.0), scale=c_ap(1.0)), 1, ("silu", igx, gg))

    def emit_hdma(igx, gg):
        @J("scalar")
        def _(scalar, igx=igx, gg=gg):
            WM(scalar, "SV", ("hmul", igx, gg))
            INC("SQ", scalar.dma_start(hs[igx * 128:(igx + 1) * 128, gg * GT:(gg + 1) * GT],
                                       hstg[:, gg % 2, :]), 16, ("hdma", igx, gg))

    def emit_hmul(igx, gg):
        @J("vector")
        def _(vector, igx=igx, gg=gg):
            WM(vector, "ST", ("umm", igx, gg))
            WM(vector, "SS", ("silu", igx, gg))
            if gg >= 2:
                WM(vector, "SQ", ("hdma", igx, gg - 2))
            elif igx >= 1:
                WM(vector, "SQ", ("hdma", igx - 1, NG - 2 + gg))
            psu = ps3_U if gg % 2 == 0 else ps3_U2
            INC("SV", vector.tensor_tensor(hstg[:, gg % 2, :], psu[:, :], sg_sb[:, gg % 2, :], ALU.mult),
                1, ("hmul", igx, gg))

    for ig in range(IT):
        par = ig % 2
        for p8 in range(8):
            gu = p8 // 4
            ps = p8 % 4
            tag = "g" if gu == 0 else "u"
            dq_pass(
                key=(tag, ig, ps),
                ld_mark=("pkR", ig), ld_sem="GQ" if ig % 2 == 0 else "GQ2",
                src_ap_fn=(lambda par=par, gu=gu, ps=ps: pk[:, par, gu, ps * 64:(ps + 1) * 64]),
                wN=64,
                dense_t=dense, dense_off=gu * D + ps * 512, row_len=2 * D,
                scale_fn=(lambda gi, ig=ig, gu=gu, ps=ps: sclf[:, gu * IT + ig, ps * 4 + gi:ps * 4 + gi + 1]),
                dqa=dqa_a, cpool=cpool_a, sclc=("sclallc",),
                dense_gate=("wtr", tag, ig - 1, ps // 2) if ig >= 1 else None,
            )
            if p8 == 7:
                dq_drain()
            if ig >= 1:
                if p8 >= 2:
                    emit_hmul(ig - 1, p8 - 2)
                emit_silu(ig - 1, p8)
                if p8 >= 3:
                    emit_hdma(ig - 1, p8 - 3)
            if p8 == 3:
                emit_evac("g", 0, ig, 0)
            elif p8 == 5:
                emit_evac("g", 0, ig, 1)
            elif p8 == 7:
                emit_evac("u", 1, ig, 0)
        emit_evac("u", 1, ig, 1)
        if ig >= 1:
            emit_hmul(ig - 1, NG - 2)
            emit_hmul(ig - 1, NG - 1)
            for gg in (NG - 3, NG - 2, NG - 1):
                emit_hdma(ig - 1, gg)
        if DEBUG:
            @J("scalar")
            def _(scalar, ig=ig):
                WM(scalar, "SS", ("dqr", ("g", ig, 3)))
                INC("SQ", scalar.dma_start(wgd[ig * 128:(ig + 1) * 128, :], dense[:, 0, :]), 16)
                WM(scalar, "SS", ("dqr", ("u", ig, 3)))
                INC("SQ", scalar.dma_start(wgd[IF + ig * 128:IF + (ig + 1) * 128, :], dense[:, 1, :]), 16)

        @J("tensor")
        def _(tensor, ig=ig, par=par):
            if ig == 0:
                WM(tensor, "LQ", ("ln1a",))
                for gu, tag in ((0, "g"), (1, "u")):
                    for half in range(2):
                        emit_tr_group(tensor, tag, gu, 0, half)
            for gg in range(NG):
                psg = ps3_G if gg % 2 == 0 else ps3_G2
                psu = ps3_U if gg % 2 == 0 else ps3_U2
                if gg == 0:
                    for half in range(2):
                        WM(tensor, "SS", ("wtev", "g", ig, half))
                if gg >= 2 or ig >= 1:
                    pkey = ("silu", ig, gg - 2) if gg >= 2 else ("silu", ig - 1, NG - 2 + gg)
                    WM(tensor, "SS", pkey)
                for k in range(DT):
                    inst = tensor.matmul(psg[:, :], wtg[:, par, 0, k * 128:(k + 1) * 128],
                                         ln1a[:, k, gg * GT:(gg + 1) * GT], start=(k == 0), stop=(k == DT - 1))
                INC("ST", inst, 1, ("gmm", ig, gg))
                if gg == 0:
                    for half in range(2):
                        WM(tensor, "SS", ("wtev", "u", ig, half))
                if gg >= 2 or ig >= 1:
                    pkey = ("hmul", ig, gg - 2) if gg >= 2 else ("hmul", ig - 1, NG - 2 + gg)
                    WM(tensor, "SV", pkey)
                for k in range(DT):
                    inst = tensor.matmul(psu[:, :], wtg[:, par, 1, k * 128:(k + 1) * 128],
                                         ln1a[:, k, gg * GT:(gg + 1) * GT], start=(k == 0), stop=(k == DT - 1))
                INC("ST", inst, 1, ("umm", ig, gg))
                if ig + 1 < IT:
                    if gg == 2:
                        emit_tr_group(tensor, "g", 0, ig + 1, 0)
                    elif gg == 4:
                        emit_tr_group(tensor, "g", 0, ig + 1, 1)
                    elif gg == 6:
                        emit_tr_group(tensor, "u", 1, ig + 1, 0)
                        emit_tr_group(tensor, "u", 1, ig + 1, 1)

    for p8 in range(8):
        if p8 >= 2:
            emit_hmul(IT - 1, p8 - 2)
        emit_silu(IT - 1, p8)
        if p8 >= 3:
            emit_hdma(IT - 1, p8 - 3)
    emit_hmul(IT - 1, NG - 2)
    emit_hmul(IT - 1, NG - 1)
    for gg in (NG - 3, NG - 2, NG - 1):
        emit_hdma(IT - 1, gg)


    phase_barrier("p3a")
    p3a.close()
    dq_reset()
    if PHASES <= 3:
        p3.close()
        blk_cm.__exit__(None, None, None)
        es.close()
        LAST_MARKS = dict(marks)
        return nc

    p3p = ExitStack()
    dqa_p = p3p.enter_context(nc.sbuf_tensor("dqa_p", [128, 10, 512], I32))
    cpool_p = p3p.enter_context(nc.sbuf_tensor("cpool_p", [128, 3, 512], I32))
    emit_cpool_init(cpool_p, ("p3pc",))
    pkd = p3p.enter_context(nc.sbuf_tensor("pkd", [128, 2, 704], I32))
    scld = p3p.enter_context(nc.sbuf_tensor("scld", [128, DT, IT], F16))
    scldf = p3p.enter_context(nc.sbuf_tensor("scldf", [128, DT, IT], F32))
    densed = p3p.enter_context(nc.sbuf_tensor("densed", [128, IF], BF16))
    wstg = p3p.enter_context(nc.sbuf_tensor("wstg", [128, IF], BF16))

    @J("sync")
    def _(sync):
        for b in range(DT):
            sync.dma_start(scld[:, b, :], dns[b * 128:(b + 1) * 128, :]).then_inc(sems["LQ"], 16)
            cnt["LQ"] += 16
        marks[("dsclall",)] = cnt["LQ"]
        WM(sync, "SV", ("dqk", ("u", IT - 1, 3)))
        for ot in range(DT):
            if ot >= 2:
                WM(sync, "SV", ("dqk", ("d", ot - 2, 10)))
            INC("GQ" if ot % 2 == 0 else "GQ2",
                sync.dma_start(pkd[:, ot % 2, :], dnp[ot * 128:(ot + 1) * 128, :]), 16, ("pkdR", ot))

    @J("vector")
    def _(vector):
        WM(vector, "LQ", ("dsclall",))
        INC("SV", vector.tensor_copy(scldf[:], scld[:]), 1, ("dsclallc",))

    NPASS = 11
    PSQ = [ps3_T, ps3_T2, ps3_T3, ps3_T4]
    for ot in range(DT):
        par = ot % 2
        for ps in range(NPASS):
            w0 = ps * 64
            dq_pass(
                key=("d", ot, ps),
                ld_mark=("pkdR", ot), ld_sem="GQ" if ot % 2 == 0 else "GQ2",
                src_ap_fn=(lambda par=par, w0=w0: pkd[:, par, w0:w0 + 64]),
                wN=64,
                dense_t=densed, dense_off=8 * w0, row_len=IF,
                scale_fn=(lambda gi, ot=ot, w0=w0: scldf[:, ot, (8 * w0) // 128 + gi:(8 * w0) // 128 + gi + 1]),
                dqa=dqa_p, cpool=cpool_p, sclc=("dsclallc",),
                dense_gate=("dtr", ot - 1, 4 * ps + 3) if ot >= 1 else None,
            )
        dq_drain()

        @J("tensor")
        def _(tensor, ot=ot):
            for ib in range(IT):
                pst = PSQ[ib % 4]
                if ib >= 4:
                    WM(tensor, "SS", ("dtev", ot, ib - 4))
                elif ot >= 1:
                    WM(tensor, "SS", ("dtev", ot - 1, IT - 4 + ib))
                WM(tensor, "SS", ("dqr", ("d", ot, ib // 4)))
                INC("ST", tensor.transpose(pst[:, 0:128], densed[:, ib * 128:(ib + 1) * 128], ident[:]),
                    1, ("dtr", ot, ib))

        @J("scalar")
        def _(scalar, ot=ot):
            if ot >= 1:
                WM(scalar, "SQ", ("wddma", ot - 1))
            for ib in range(IT):
                WM(scalar, "ST", ("dtr", ot, ib))
                pst = PSQ[ib % 4]
                INC("SS", scalar.copy(wstg[:, ib * 128:(ib + 1) * 128], pst[:, 0:128]), 1, ("dtev", ot, ib))
            WM(scalar, "SS", ("dtev", ot, IT - 1))
            INC("SQ", scalar.dma_start(wdTs[ot * 128:(ot + 1) * 128, :], wstg[:]), 16, ("wddma", ot))
            if DEBUG:
                INC("SQ", scalar.dma_start(wdd[ot * 128:(ot + 1) * 128, :], wstg[:]), 16)

    phase_barrier("p3prep")
    p3p.close()
    if PHASES <= 4:
        p3.close()
        blk_cm.__exit__(None, None, None)
        es.close()
        LAST_MARKS = dict(marks)
        return nc

    p3m = ExitStack()
    h_g = p3m.enter_context(nc.sbuf_tensor("h_g", [128, IT, GT], BF16))
    ln1g = p3m.enter_context(nc.sbuf_tensor("ln1g", [128, DT, GT], BF16))
    wdt = p3m.enter_context(nc.sbuf_tensor("wdt", [128, 2, IF], BF16))
    yg = p3m.enter_context(nc.sbuf_tensor("yg", [128, DT, GT], BF16))
    r2g = p3m.enter_context(nc.sbuf_tensor("r2g", [128, DT, GT], F32))
    sqb = p3m.enter_context(nc.sbuf_tensor("sqb", [128, 4, GT], BF16))
    st3 = p3m.enter_context(nc.sbuf_tensor("st3", [1, 6, GT], F32))
    ost = p3m.enter_context(nc.sbuf_tensor("ost", [128, 4, D], F32))
    lnb = p3m.enter_context(nc.sbuf_tensor("lnb", [128, 2, GT], BF16))

    @J("sync")
    def _(sync):
        for g in range(NG):
            if g >= 1:
                W(sync, "GRP", NG + g)
            for ib in range(IT):
                sync.dma_start(h_g[:, ib, :], hs[ib * 128:(ib + 1) * 128, g * GT:(g + 1) * GT]).then_inc(sems["LQ"], 16)
                cnt["LQ"] += 16
            for k in range(DT):
                sync.dma_start(ln1g[:, k, :], ln1s[k * 128:(k + 1) * 128, g * GT:(g + 1) * GT]).then_inc(sems["LQ"], 16)
                cnt["LQ"] += 16
            marks[("hld", g)] = cnt["LQ"]
            for ot in range(DT):
                if ot >= 2:
                    WM(sync, "ST", ("ymm", g, ot - 2))
                INC("GQ" if ot % 2 == 0 else "GQ2",
                    sync.dma_start(wdt[:, ot % 2, :], wdTs[ot * 128:(ot + 1) * 128, :]), 16, ("wdldR", g, ot))

    for g in range(NG):
        @J("tensor")
        def _(tensor, g=g):
            WM(tensor, "LQ", ("hld", g))
            for ot in range(DT):
                WM(tensor, "GQ" if ot % 2 == 0 else "GQ2", ("wdldR", g, ot))
                psm = ps3_G if ot % 2 == 0 else ps3_G2
                if ot >= 2:
                    WM(tensor, "SV", ("yev", g, ot - 2))
                    WM(tensor, "SS", ("ysq", g, ot - 2))
                for ib in range(IT):
                    inst = tensor.matmul(psm[:, :], wdt[:, ot % 2, ib * 128:(ib + 1) * 128],
                                         h_g[:, ib, :], start=(ib == 0), stop=(ib == IT - 1))
                INC("ST", inst, 1, ("ymm", g, ot))
                if ot >= 2:
                    WM(tensor, "SS", ("ysq", g, ot - 2))
                    INC("ST", tensor.matmul(ps3_U[0:1, 0:512], onescol[:], sqb[:, ot % 2, :],
                                            start=(ot - 2 == 0), stop=False), 1, ("rsmm", g, ot - 2))
            for ot in range(DT - 2, DT):
                WM(tensor, "SS", ("ysq", g, ot))
                INC("ST", tensor.matmul(ps3_U[0:1, 0:512], onescol[:], sqb[:, ot % 2, :],
                                        start=False, stop=(ot == DT - 1)), 1, ("rsmm", g, ot))
            WM(tensor, "SV", ("rmsfin", g))
            INC("ST", tensor.matmul(ps3_U2[:, 0:512], onesrow[:, :], st3[0:1, 3, :], start=True, stop=True),
                1, ("rmsbc", g))
            for ot in range(DT):
                WM(tensor, "SV", ("r2b", g, ot))
                INC("ST", tensor.matmul(ps3_G[0:1, 0:512], onescol[:], sqb[:, ot % 2, :],
                                        start=(ot == 0), stop=(ot == DT - 1)), 1, ("l2smm", g, ot))
                if ot >= 2:
                    WM(tensor, "SS", ("r2sq", g, ot - 2))
                    INC("ST", tensor.matmul(ps3_G2[0:1, 0:512], onescol[:], sqb[:, 2 + ot % 2, :],
                                            start=(ot - 2 == 0), stop=False), 1, ("l2qmm", g, ot - 2))
            for ot in range(DT - 2, DT):
                WM(tensor, "SS", ("r2sq", g, ot))
                INC("ST", tensor.matmul(ps3_G2[0:1, 0:512], onescol[:], sqb[:, 2 + ot % 2, :],
                                        start=False, stop=(ot == DT - 1)), 1, ("l2qmm", g, ot))
            WM(tensor, "SV", ("l2fin", g))
            tensor.matmul(ps3_U[:, 0:512], onesrow[:, :], st3[0:1, 0, :], start=True, stop=True)
            INC("ST", tensor.matmul(ps3_U2[:, 0:512], onesrow[:, :], st3[0:1, 3, :], start=True, stop=True),
                1, ("l2bc", g))
            for ot in range(DT):
                WM(tensor, "SV", ("lnb", g, ot))
                if ot >= 2:
                    WM(tensor, "SS", ("oev", g, ot - 2))
                pst = ps3_T if ot % 2 == 0 else ps3_T2
                for j in range(4):
                    inst = tensor.transpose(pst[:, j * 128:(j + 1) * 128],
                                            lnb[:, ot % 2, j * 128:(j + 1) * 128], ident[:])
                INC("ST", inst, 1, ("otr", g, ot))

        @J("vector")
        def _(vector, g=g):
            for ot in range(DT):
                WM(vector, "ST", ("ymm", g, ot))
                psm = ps3_G if ot % 2 == 0 else ps3_G2
                INC("SV", vector.tensor_copy(yg[:, ot, :], psm[:, :]), 1, ("yev", g, ot))
            WM(vector, "ST", ("rsmm", g, DT - 1))
            INC("SV", vector.tensor_scalar(st3[0:1, 0, :], ps3_U[0:1, 0:512], 1.0 / D, None, ALU.mult),
                1, ("rmse", g))
            WM(vector, "SS", ("rmsq", g))
            INC("SV", vector.reciprocal(st3[0:1, 3, :], st3[0:1, 1, :]), 1, ("rmsfin", g))
            for ot in range(DT):
                WM(vector, "ST", ("rmsbc", g))
                if ot >= 2:
                    WM(vector, "ST", ("l2smm", g, ot - 2))
                    WM(vector, "SS", ("r2sq", g, ot - 2))
                vector.tensor_tensor(r2g[:, ot, :], yg[:, ot, :], ps3_U2[:, 0:512], ALU.mult)
                vector.tensor_tensor(r2g[:, ot, :], r2g[:, ot, :], ln1g[:, ot, :], ALU.add)
                INC("SV", vector.tensor_copy(sqb[:, ot % 2, :], r2g[:, ot, :]), 1, ("r2b", g, ot))
            WM(vector, "ST", ("l2smm", g, DT - 1))
            WM(vector, "ST", ("l2qmm", g, DT - 1))
            vector.tensor_scalar(st3[0:1, 0, :], ps3_G[0:1, 0:512], 1.0 / D, None, ALU.mult)
            vector.tensor_scalar(st3[0:1, 1, :], ps3_G2[0:1, 0:512], 1.0 / D, None, ALU.mult)
            vector.tensor_tensor(st3[0:1, 2, :], st3[0:1, 0, :], st3[0:1, 0, :], ALU.mult)
            INC("SV", vector.tensor_tensor(st3[0:1, 1, :], st3[0:1, 1, :], st3[0:1, 2, :], ALU.subtract),
                1, ("l2var", g))
            WM(vector, "SS", ("l2sq", g))
            vector.reciprocal(st3[0:1, 3, :], st3[0:1, 4, :])
            INC("SV", vector.tensor_tensor(st3[0:1, 5, :], st3[0:1, 0, :], st3[0:1, 3, :], ALU.mult),
                1, ("l2fin", g))
            for ot in range(DT):
                WM(vector, "ST", ("l2bc", g))
                if ot >= 2:
                    WM(vector, "ST", ("otr", g, ot - 2))
                vector.tensor_tensor(lnb[:, ot % 2, :], r2g[:, ot, :], ps3_U[:, 0:512], ALU.subtract)
                INC("SV", vector.tensor_tensor(lnb[:, ot % 2, :], lnb[:, ot % 2, :], ps3_U2[:, 0:512], ALU.mult),
                    1, ("lnb", g, ot))

        @J("scalar")
        def _(scalar, g=g):
            for ot in range(DT):
                WM(scalar, "SV", ("yev", g, ot))
                if DEBUG:
                    INC("SQ", scalar.dma_start(ydd[ot * 128:(ot + 1) * 128, g * GT:(g + 1) * GT],
                                               yg[:, ot, :]), 16)
                if ot >= 2:
                    WM(scalar, "ST", ("rsmm", g, ot - 2))
                INC("SS", scalar.activation(sqb[:, ot % 2, :], yg[:, ot, :], AF.Square,
                                            bias=c_ap(0.0), scale=c_ap(1.0)), 1, ("ysq", g, ot))
            WM(scalar, "SV", ("rmse", g))
            INC("SS", scalar.activation(st3[0:1, 1, :], st3[0:1, 0, :], AF.Sqrt,
                                        bias=c_ap1(1e-6), scale=c_ap1(1.0)), 1, ("rmsq", g))
            for ot in range(DT):
                WM(scalar, "SV", ("r2b", g, ot))
                if DEBUG:
                    INC("SQ", scalar.dma_start(r2d[ot * 128:(ot + 1) * 128, g * GT:(g + 1) * GT],
                                               r2g[:, ot, :]), 16)
                if ot >= 2:
                    WM(scalar, "ST", ("l2qmm", g, ot - 2))
                INC("SS", scalar.activation(sqb[:, 2 + ot % 2, :], sqb[:, ot % 2, :], AF.Square,
                                            bias=c_ap(0.0), scale=c_ap(1.0)), 1, ("r2sq", g, ot))
            WM(scalar, "SV", ("l2var", g))
            INC("SS", scalar.activation(st3[0:1, 4, :], st3[0:1, 1, :], AF.Sqrt,
                                        bias=c_ap1(1e-5), scale=c_ap1(1.0)), 1, ("l2sq", g))
            if DEBUG:
                WM(scalar, "SV", ("l2fin", g))
                INC("SQ", scalar.dma_start(std[0:1, g, :, :], st3[0:1, :, :]), 16)
            for ot in range(DT):
                WM(scalar, "ST", ("otr", g, ot))
                pst = ps3_T if ot % 2 == 0 else ps3_T2
                for j in range(4):
                    INC("SS", scalar.copy(ost[:, j, ot * 128:(ot + 1) * 128], pst[:, j * 128:(j + 1) * 128]),
                        1, ("oev", g, ot) if j == 3 else None)
            WM(scalar, "SS", ("oev", g, DT - 1))
            for j in range(4):
                INC("SQ", scalar.dma_start(out[(g * 4 + j) * 128:(g * 4 + j + 1) * 128, :], ost[:, j, :]),
                    16, ("odma", g, j))

        @J("gpsimd")
        def _(gp, g=g):
            WM(gp, "SS", ("oev", g, DT - 1))
            WM(gp, "SQ", ("odma", g, 3))
            gp.sem_inc(sems["GRP"], 1)
            cnt["GRP"] += 1

    @J("scalar")
    def _(scalar):
        W(scalar, "SQ", cnt["SQ"])
    run_jobs()
    p3m.close()
    p3.close()

    blk_cm.__exit__(None, None, None)
    es.close()
    LAST_MARKS = dict(marks)
    return nc


def _host_consts():
    s = _slopes(H)
    idx = np.arange(CH, dtype=np.float32)
    ident = np.eye(128, dtype=BF)
    idxloc = np.broadcast_to(idx[None, :], (128, 128)).astype(np.float32).copy()
    quart = np.full((128, 128), 0.25, dtype=np.float32)
    diagTm = np.zeros((128, H * 128), dtype=np.float32)
    qdecv = np.zeros((128, H * 128), dtype=np.float32)
    kdecppv = np.zeros((128, H), dtype=np.float32)
    diffm = idx[:, None] - idx[None, :]
    for hh in range(H):
        dd = np.where(diffm >= 0, np.exp(-s[hh] * diffm), 0.0)
        diagTm[:, hh * 128:(hh + 1) * 128] = dd.T
        qdecv[:, hh * 128:(hh + 1) * 128] = np.exp(-s[hh] * (idx + 1.0))[None, :]
        kdecppv[:, hh] = np.exp(-s[hh] * (CH - 1.0 - idx))
    cvals = np.zeros((128, 8), dtype=np.float32)
    cvals[:, 1] = 1.0
    cvals[:, 2] = PI / 2
    cvals[:, 3] = 1e-5
    cvals[:, 4] = 1e-6
    onescolv = np.ones((128, 1), dtype=BF)
    onesrowv = np.ones((1, 128), dtype=np.float32)
    return dict(ident=ident, idxloc=idxloc, quart=quart,
                diagT=diagTm.astype(BF), qdec=qdecv.astype(BF),
                kdecpp=kdecppv, cvals=cvals, onescol=onescolv, onesrow=onesrowv)


def kernel(L0_attn_x, L0_attn_w_qkv, L0_attn_w_proj, L0_attn_theta,
           L0_ffn_gate_up_packed, L0_ffn_gate_up_scales,
           L0_ffn_down_packed, L0_ffn_down_scales):
    x = np.ascontiguousarray(np.asarray(L0_attn_x, dtype=np.float32))
    wq = np.ascontiguousarray(np.asarray(L0_attn_w_qkv, dtype=np.float32))
    wp = np.ascontiguousarray(np.asarray(L0_attn_w_proj, dtype=np.float32))
    th = np.asarray(L0_attn_theta, dtype=np.float32).reshape(H, E)
    theta_pp = np.ascontiguousarray(np.tile(th.T, (1, 8)))
    gup_a = np.ascontiguousarray(np.asarray(L0_ffn_gate_up_packed, dtype=np.int32))
    gus_a = np.ascontiguousarray(np.asarray(L0_ffn_gate_up_scales, dtype=np.float16).T)
    dnp_a = np.ascontiguousarray(np.asarray(L0_ffn_down_packed, dtype=np.int32))
    dns_a = np.ascontiguousarray(np.asarray(L0_ffn_down_scales, dtype=np.float16).T)
    consts = _host_consts()

    nc = build_nc()
    in_maps = []
    for c in range(8):
        in_maps.append(dict(x=x[c], wqkv=wq, wproj=wp, theta_pp=theta_pp,
                            gup=gup_a, gus=gus_a, dnp=dnp_a, dns=dns_a,
                            ident=consts["ident"], idxloc=consts["idxloc"],
                            quart=consts["quart"], diagT=consts["diagT"],
                            qdec=consts["qdec"], kdecpp=consts["kdecpp"],
                            cvals=consts["cvals"], onescol=consts["onescol"],
                            onesrow=consts["onesrow"]))
    res = run_bass_kernel_spmd(nc, in_maps, core_ids=list(range(8)))
    return np.stack([np.asarray(r["out"], dtype=np.float32) for r in res.results], axis=0)


# version 09
# speedup vs baseline: 1.1047x; 1.1047x over previous
import numpy as np
import ml_dtypes
from contextlib import ExitStack

import concourse.bass as bass
import concourse.mybir as mybir
from concourse.bass_utils import run_bass_kernel_spmd

F32 = mybir.dt.float32
BF16 = mybir.dt.bfloat16
F16 = mybir.dt.float16
I32 = mybir.dt.int32
AF = mybir.ActivationFunctionType
ALU = mybir.AluOpType

T = 4096
D = 2048
H = 16
E = 128
CH = 128
NCH = 32
NG = 8
GT = 512
DT = 16
IF = 5632
IT = 44
TWO_PI = float(2.0 * np.pi)
PI = float(np.pi)
BF = ml_dtypes.bfloat16

DEBUG = False
LAST_MARKS = {}
PHASES = 9
RACE_DET = True
P1SUB = 3


def _slopes(h):
    start = 2.0 ** (-2.0 ** (-(np.log2(h) - 3)))
    return np.array([start ** (i + 1) for i in range(h)], dtype=np.float32)


class _DummyInst:
    def then_inc(self, sem, k):
        return self


class _DryEng:
    def __getattr__(self, name):
        def f(*a, **kw):
            return _DummyInst()
        return f


class Marks(dict):
    dry = True

    def __missing__(self, key):
        if self.dry:
            return 0
        raise KeyError(key)


def build_nc():
    global LAST_MARKS
    nc = bass.Bass("TRN2", target_bir_lowering=False, detect_race_conditions=RACE_DET)

    x_in = nc.dram_tensor("x", [T, D], F32, kind="ExternalInput")
    wqkv = nc.dram_tensor("wqkv", [3 * D, D], F32, kind="ExternalInput")
    wproj = nc.dram_tensor("wproj", [D, D], F32, kind="ExternalInput")
    theta_pp_in = nc.dram_tensor("theta_pp", [128, 128], F32, kind="ExternalInput")
    gup = nc.dram_tensor("gup", [2 * IF, D // 8], I32, kind="ExternalInput")
    gus = nc.dram_tensor("gus", [2 * IF, DT], F16, kind="ExternalInput")
    dnp = nc.dram_tensor("dnp", [D, IF // 8], I32, kind="ExternalInput")
    dns = nc.dram_tensor("dns", [D, IT], F16, kind="ExternalInput")
    ident_in = nc.dram_tensor("ident", [128, 128], BF16, kind="ExternalInput")
    idxloc_in = nc.dram_tensor("idxloc", [128, 128], F32, kind="ExternalInput")
    quart_in = nc.dram_tensor("quart", [128, 128], F32, kind="ExternalInput")
    diag_in = nc.dram_tensor("diagT", [128, H * 128], BF16, kind="ExternalInput")
    qdec_in = nc.dram_tensor("qdec", [128, H * 128], BF16, kind="ExternalInput")
    kdec_in = nc.dram_tensor("kdecpp", [128, H], F32, kind="ExternalInput")
    cv_in = nc.dram_tensor("cvals", [128, 8], F32, kind="ExternalInput")
    onesc_in = nc.dram_tensor("onescol", [128, 1], BF16, kind="ExternalInput")
    onesr_in = nc.dram_tensor("onesrow", [1, 128], F32, kind="ExternalInput")

    out = nc.dram_tensor("out", [T, D], F32, kind="ExternalOutput")
    skind = "ExternalOutput" if DEBUG else "Internal"
    xTs = nc.dram_tensor("xTs", [D, T], BF16, kind=skind)
    qs = nc.dram_tensor("qs", [D, T], BF16, kind=skind)
    ks = nc.dram_tensor("ks", [D, T], BF16, kind=skind)
    vs = nc.dram_tensor("vs", [T, D], BF16, kind=skind)
    ln1s = nc.dram_tensor("ln1s", [D, T], BF16, kind=skind)
    hs = nc.dram_tensor("hs", [IF, T], BF16, kind=skind)
    ysd = nc.dram_tensor("ysd", [D, T], BF16, kind=skind)
    wgd = nc.dram_tensor("wgd", [2 * IF, D], BF16, kind=skind)
    ydd = nc.dram_tensor("ydd", [D, T], BF16, kind=skind)
    wdd = nc.dram_tensor("wdd", [DT * 128, IF], BF16, kind=skind)
    r2d = nc.dram_tensor("r2d", [D, T], F32, kind=skind)
    std = nc.dram_tensor("std", [1, NG, 6, GT], F32, kind=skind)
    tdmp = nc.dram_tensor("tdmp", [128, 2, H * 128], BF16, kind=skind)
    cdmp = nc.dram_tensor("cdmp", [128, 2, H], F32, kind=skind)
    ldmp = nc.dram_tensor("ldmp", [128, 6, 128], BF16, kind=skind)
    sdmp = nc.dram_tensor("sdmp", [128, 6, 128], BF16, kind=skind)
    wdTs = nc.dram_tensor("wdTs", [DT * 128, IF], BF16, kind="Internal")

    slopes = _slopes(H)
    bdec = [float(np.exp(-slopes[hh] * CH)) for hh in range(H)]

    es = ExitStack()
    ident = es.enter_context(nc.sbuf_tensor("identS", [128, 128], BF16))
    idxloc = es.enter_context(nc.sbuf_tensor("idxlocS", [128, 128], F32))
    diagT = es.enter_context(nc.sbuf_tensor("diagS", [128, H * 128], BF16))
    qdec = es.enter_context(nc.sbuf_tensor("qdecS", [128, H * 128], BF16))
    kdecpp = es.enter_context(nc.sbuf_tensor("kdecS", [128, H], F32))
    theta_pp = es.enter_context(nc.sbuf_tensor("thetaS", [128, 128], F32))
    cv = es.enter_context(nc.sbuf_tensor("cvS", [128, 8], F32))
    onescol = es.enter_context(nc.sbuf_tensor("onescolS", [128, 1], BF16))
    onesrow = es.enter_context(nc.sbuf_tensor("onesrowS", [1, 128], F32))

    CMAP = {0.0: 0, 1.0: 1, PI / 2: 2, 1e-5: 3, 1e-6: 4}
    for v, i in CMAP.items():
        nc.const_aps.register(F32, v, cv[:, i:i + 1])

    def c_ap(v):
        return cv[:, CMAP[v]:CMAP[v] + 1]

    def c_ap1(v):
        return cv[0:1, CMAP[v]:CMAP[v] + 1]

    sems = {}
    for name in ["LQ", "GQ", "GQ2", "SQ", "SV", "SS", "SP", "ST", "PH", "GRP"]:
        sems[name] = es.enter_context(nc.semaphore(name))

    blk_cm = nc.Block()
    blk = blk_cm.__enter__()

    jobs = {"sync": [], "vector": [], "scalar": [], "tensor": [], "gpsimd": []}
    marks = Marks()
    cnt = {k: 0 for k in sems}

    def J(eng):
        def deco(fn):
            jobs[eng].append(fn)
            return fn
        return deco

    def INC(sem, inst, k=1, mark=None):
        inst.then_inc(sems[sem], k)
        cnt[sem] += k
        if mark is not None:
            marks[mark] = cnt[sem]
        return cnt[sem]

    def W(eng, sem, val):
        eng.wait_ge(sems[sem], val)

    def WM(eng, sem, mark):
        eng.wait_ge(sems[sem], marks[mark])

    ENGS = ["sync", "vector", "scalar", "tensor", "gpsimd"]

    def run_jobs():
        saved = dict(cnt)
        Marks.dry = True
        for name in ENGS:
            eng = _DryEng()
            for fn in jobs[name]:
                fn(eng)
        for k in cnt:
            cnt[k] = saved[k]
        Marks.dry = False
        for name in ENGS:
            sub = list(jobs[name])

            def runner(fns):
                def _(eng):
                    for fn in fns:
                        fn(eng)
                return _
            getattr(blk, name)(runner(sub))
        for name in jobs:
            jobs[name] = []

    def phase_barrier(tag):
        target = cnt["PH"] + 5

        @J("scalar")
        def _(eng):
            W(eng, "SQ", cnt["SQ"])
            eng.drain()
            INC("PH", eng.nop())
            W(eng, "PH", target)

        @J("gpsimd")
        def _(eng):
            W(eng, "GQ", cnt["GQ"])
            W(eng, "GQ2", cnt["GQ2"])
            eng.drain()
            INC("PH", eng.nop())
            W(eng, "PH", target)
        for name in ["sync", "vector", "tensor"]:
            @J(name)
            def _(eng, name=name):
                eng.drain()
                INC("PH", eng.nop())
                W(eng, "PH", target)
        run_jobs()
        assert cnt["PH"] == target, (tag, cnt["PH"], target)

    p1 = ExitStack()
    xT = p1.enter_context(nc.sbuf_tensor("xT", [128, DT, T], BF16))
    xb = p1.enter_context(nc.sbuf_tensor("xb", [128, 1, D], BF16))
    xf = p1.enter_context(nc.sbuf_tensor("xf", [128, 1, D], F32))
    wb = p1.enter_context(nc.sbuf_tensor("wb", [128, 2, D], BF16))
    wf = p1.enter_context(nc.sbuf_tensor("wf", [128, 2, D], F32))
    wt = p1.enter_context(nc.sbuf_tensor("wt", [128, 2, DT * 128], BF16))
    wtv = p1.enter_context(nc.sbuf_tensor("wtv", [128, DT, 512], BF16))
    stg = p1.enter_context(nc.sbuf_tensor("stg", [128, 2, 512], BF16))
    ps_t = p1.enter_context(nc.psum_tensor("ps_t", [128, 1024], BF16))
    ps_t2 = p1.enter_context(nc.psum_tensor("ps_t2", [128, 1024], BF16))
    ps_m = p1.enter_context(nc.psum_tensor("ps_m", [128, 512], F32))
    ps_m2 = p1.enter_context(nc.psum_tensor("ps_m2", [128, 512], F32))

    @J("sync")
    def _(sync):
        for t_, s_ in [(ident, ident_in), (idxloc, idxloc_in),
                       (diagT, diag_in), (qdec, qdec_in), (kdecpp, kdec_in),
                       (theta_pp, theta_pp_in), (cv, cv_in), (onescol, onesc_in),
                       (onesrow, onesr_in)]:
            INC("LQ", sync.dma_start(t_[:], s_[:, :]), 16)
        marks["consts"] = cnt["LQ"]

    @J("sync")
    def _(sync):
        for tt in range(NCH):
            if tt >= 1:
                WM(sync, "SV", ("xcast", tt - 1))
            INC("GQ", sync.dma_start(xf[:, 0, :], x_in[tt * 128:(tt + 1) * 128, :]), 16,
                ("xldf", tt))
        for o in range(32 if P1SUB >= 2 else 0):
            if o >= 2:
                WM(sync, "SV", ("wcast", o - 2))
            INC("GQ2", sync.dma_start(wf[:, o % 2, :], wqkv[o * 128:(o + 1) * 128, :]), 16,
                ("wldf", o))
        for vsl in range(4 if P1SUB >= 3 else 0):
            for oo in range(4):
                o = 4 * vsl + oo
                if o >= 2:
                    WM(sync, "SV", ("vwcast", o - 2))
                elif P1SUB >= 2:
                    WM(sync, "SV", ("wcast", 30 + o))
                INC("GQ", sync.dma_start(wf[:, o % 2, :], wqkv[(32 + o) * 128:(33 + o) * 128, :]), 16,
                    ("vwldf", o))

    @J("vector")
    def _(vector):
        for tt in range(NCH):
            WM(vector, "GQ", ("xldf", tt))
            if tt >= 1:
                WM(vector, "ST", ("xtr2", tt - 1))
            INC("SV", vector.tensor_copy(xb[:, 0, :], xf[:, 0, :]), 1, ("xcast", tt))
        for o in range(32 if P1SUB >= 2 else 0):
            WM(vector, "GQ2", ("wldf", min(o + 1, 31)))
            if o >= 2:
                WM(vector, "ST", ("wtr2", o - 2))
            INC("SV", vector.tensor_copy(wb[:, o % 2, :], wf[:, o % 2, :]), 1, ("wcast", o))
        for vsl in range(4 if P1SUB >= 3 else 0):
            for oo in range(4):
                o = 4 * vsl + oo
                WM(vector, "GQ", ("vwldf", min(o + 1, 15)))
                if o >= 2:
                    WM(vector, "ST", ("vwtr2", o - 2))
                elif P1SUB >= 2:
                    WM(vector, "ST", ("wtr2", 30 + o))
                INC("SV", vector.tensor_copy(wb[:, o % 2, :], wf[:, o % 2, :]), 1, ("vwcast", o))

    @J("tensor")
    def _(tensor):
        for tt in range(NCH):
            WM(tensor, "SV", ("xcast", tt))
            if tt >= 1:
                WM(tensor, "SS", ("xev1", tt - 1))
            for k in range(8):
                inst = tensor.transpose(ps_t[:, k * 128:(k + 1) * 128],
                                        xb[:, 0, k * 128:(k + 1) * 128], ident[:])
            INC("ST", inst, 1, ("xtr1", tt))
            if tt >= 1:
                WM(tensor, "SS", ("xev2", tt - 1))
            for k in range(8):
                inst = tensor.transpose(ps_t2[:, k * 128:(k + 1) * 128],
                                        xb[:, 0, (8 + k) * 128:(9 + k) * 128], ident[:])
            INC("ST", inst, 1, ("xtr2", tt))
        for o in range(32 if P1SUB >= 2 else 0):
            WM(tensor, "SV", ("wcast", o))
            WM(tensor, "SS", ("xev2", NCH - 1) if o == 0 else ("wev1", o - 1))
            for k in range(8):
                inst = tensor.transpose(ps_t[:, k * 128:(k + 1) * 128],
                                        wb[:, o % 2, k * 128:(k + 1) * 128], ident[:])
            INC("ST", inst, 1, ("wtr1", o))
            if o >= 1:
                WM(tensor, "SS", ("wev2", o - 1))
            for k in range(8):
                inst = tensor.transpose(ps_t2[:, k * 128:(k + 1) * 128],
                                        wb[:, o % 2, (8 + k) * 128:(9 + k) * 128], ident[:])
            INC("ST", inst, 1, ("wtr2", o))
            WM(tensor, "SS", ("wev2", o))
            for tc in range(8):
                psm = ps_m if tc % 2 == 0 else ps_m2
                if tc >= 2:
                    WM(tensor, "SS", ("qkev", o, tc - 2))
                elif o >= 1:
                    WM(tensor, "SS", ("qkev", o - 1, 6 + tc))
                for k in range(DT):
                    inst = tensor.matmul(psm[:, :], wt[:, o % 2, k * 128:(k + 1) * 128],
                                         xT[:, k, tc * 512:(tc + 1) * 512],
                                         start=(k == 0), stop=(k == DT - 1))
                INC("ST", inst, 1, ("qkmm", o, tc))
        ci = 0
        for vsl in range(4 if P1SUB >= 3 else 0):
            for oo in range(4):
                o = 4 * vsl + oo
                WM(tensor, "SV", ("vwcast", o))
                WM(tensor, "SS", ("wev2", 31) if o == 0 else ("vwev1", o - 1))
                if vsl >= 1:
                    WM(tensor, "SS", ("vev", 32 * vsl - 1))
                for k in range(8):
                    inst = tensor.transpose(ps_t[:, k * 128:(k + 1) * 128],
                                            wb[:, o % 2, k * 128:(k + 1) * 128], ident[:])
                INC("ST", inst, 1, ("vwtr1", o))
                if o >= 1:
                    WM(tensor, "SS", ("vwev2", o - 1))
                for k in range(8):
                    inst = tensor.transpose(ps_t2[:, k * 128:(k + 1) * 128],
                                            wb[:, o % 2, (8 + k) * 128:(9 + k) * 128], ident[:])
                INC("ST", inst, 1, ("vwtr2", o))
            WM(tensor, "SS", ("vwev2", 4 * vsl + 3))
            for tt in range(NCH):
                psm = ps_m if ci % 2 == 0 else ps_m2
                if ci >= 2:
                    WM(tensor, "SS", ("vev", ci - 2))
                elif vsl == 0 and ci < 2:
                    WM(tensor, "SS", ("qkev", 31, 6 + ci))
                for k in range(DT):
                    inst = tensor.matmul(psm[:, :], xT[:, k, tt * 128:(tt + 1) * 128],
                                         wtv[:, k, :], start=(k == 0), stop=(k == DT - 1))
                INC("ST", inst, 1, ("vmm", ci))
                ci += 1

    @J("scalar")
    def _(scalar):
        for tt in range(NCH):
            WM(scalar, "ST", ("xtr1", tt))
            for k in range(8):
                INC("SS", scalar.copy(xT[:, k, tt * 128:(tt + 1) * 128], ps_t[:, k * 128:(k + 1) * 128]),
                    1, ("xev1", tt) if k == 7 else None)
            WM(scalar, "ST", ("xtr2", tt))
            for k in range(8):
                INC("SS", scalar.copy(xT[:, 8 + k, tt * 128:(tt + 1) * 128], ps_t2[:, k * 128:(k + 1) * 128]),
                    1, ("xev2", tt) if k == 7 else None)
        WM(scalar, "SS", ("xev2", NCH - 1))
        for k in range(DT):
            INC("SQ", scalar.dma_start(xTs[k * 128:(k + 1) * 128, :], xT[:, k, :]), 16)
        for o in range(32 if P1SUB >= 2 else 0):
            WM(scalar, "ST", ("wtr1", o))
            for k in range(8):
                INC("SS", scalar.copy(wt[:, o % 2, k * 128:(k + 1) * 128], ps_t[:, k * 128:(k + 1) * 128]),
                    1, ("wev1", o) if k == 7 else None)
            WM(scalar, "ST", ("wtr2", o))
            for k in range(8):
                INC("SS", scalar.copy(wt[:, o % 2, (8 + k) * 128:(9 + k) * 128], ps_t2[:, k * 128:(k + 1) * 128]),
                    1, ("wev2", o) if k == 7 else None)
            dst = qs if o < DT else ks
            row0 = (o % DT) * 128
            for tc in range(8):
                WM(scalar, "ST", ("qkmm", o, tc))
                if not (o == 0 and tc == 0):
                    WM(scalar, "SQ", ("qkdma", o, tc - 1) if tc >= 1 else ("qkdma", o - 1, 7))
                psm = ps_m if tc % 2 == 0 else ps_m2
                INC("SS", scalar.copy(stg[:, tc % 2, :], psm[:, :]), 1, ("qkev", o, tc))
                WM(scalar, "SS", ("qkev", o, tc))
                INC("SQ", scalar.dma_start(dst[row0:row0 + 128, tc * 512:(tc + 1) * 512],
                                           stg[:, tc % 2, :]), 16, ("qkdma", o, tc))
        ci = 0
        for vsl in range(4 if P1SUB >= 3 else 0):
            for oo in range(4):
                o = 4 * vsl + oo
                WM(scalar, "ST", ("vwtr1", o))
                for k in range(8):
                    INC("SS", scalar.copy(wtv[:, k, oo * 128:(oo + 1) * 128], ps_t[:, k * 128:(k + 1) * 128]),
                        1, ("vwev1", o) if k == 7 else None)
                WM(scalar, "ST", ("vwtr2", o))
                for k in range(8):
                    INC("SS", scalar.copy(wtv[:, 8 + k, oo * 128:(oo + 1) * 128], ps_t2[:, k * 128:(k + 1) * 128]),
                        1, ("vwev2", o) if k == 7 else None)
            for tt in range(NCH):
                WM(scalar, "ST", ("vmm", ci))
                if ci >= 1:
                    WM(scalar, "SQ", ("vdma", ci - 1))
                elif P1SUB >= 2:
                    WM(scalar, "SQ", ("qkdma", 31, 7))
                psm = ps_m if ci % 2 == 0 else ps_m2
                INC("SS", scalar.copy(stg[:, ci % 2, :], psm[:, :]), 1, ("vev", ci))
                WM(scalar, "SS", ("vev", ci))
                INC("SQ", scalar.dma_start(vs[tt * 128:(tt + 1) * 128, vsl * 512:(vsl + 1) * 512],
                                           stg[:, ci % 2, :]), 16, ("vdma", ci))
                ci += 1

    phase_barrier("p1")
    p1.close()
    if PHASES <= 1:
        blk_cm.__exit__(None, None, None)
        es.close()
        global LAST_MARKS
        LAST_MARKS = dict(marks)
        return nc

    p2 = ExitStack()
    wpt = p2.enter_context(nc.sbuf_tensor("wpt", [128, DT, D], BF16))
    r_g = p2.enter_context(nc.sbuf_tensor("r_g", [128, DT, GT], BF16))
    st32 = p2.enter_context(nc.sbuf_tensor("st32", [128, 2 * H, 128], F32))
    stbf = p2.enter_context(nc.sbuf_tensor("stbf", [128, 2 * H, 128], BF16))
    cil = p2.enter_context(nc.sbuf_tensor("cil", [128, H, 128], BF16))
    silt = p2.enter_context(nc.sbuf_tensor("silt", [128, H, 128], BF16))


    ps_T = p2.enter_context(nc.psum_tensor("p2_T", [128, 1024], BF16))
    ps_T2 = p2.enter_context(nc.psum_tensor("p2_Tb", [128, 1024], BF16))
    ps_S = p2.enter_context(nc.psum_tensor("p2_S", [128, 512], F32))
    ps_O = p2.enter_context(nc.psum_tensor("p2_O", [128, 512], F32))
    ps_U = p2.enter_context(nc.psum_tensor("p2_U", [128, 512], F32))
    ps_P = p2.enter_context(nc.psum_tensor("p2_P", [128, 512], F32))
    ps_st = p2.enter_context(nc.psum_tensor("p2_st", [128, 512], F32))
    ps_bc = p2.enter_context(nc.psum_tensor("p2_bc", [128, 512], F32))

    ploc = ExitStack()
    loc = ploc.enter_context(nc.sbuf_tensor("loc", [128, 4, 128], F32))
    loci = ploc.enter_context(nc.sbuf_tensor("loci", [128, 2, 128], I32))
    wbp = ploc.enter_context(nc.sbuf_tensor("wbp", [128, 1, D], BF16))
    wfp = ploc.enter_context(nc.sbuf_tensor("wfp", [128, 1, D], F32))

    @J("sync")
    def _(sync):
        for o in range(DT):
            if o >= 1:
                WM(sync, "SV", ("pcast", o - 1))
            INC("GQ2", sync.dma_start(wfp[:, 0, :], wproj[o * 128:(o + 1) * 128, :]), 16,
                ("pldf", o))

    @J("vector")
    def _(vector):
        for o in range(DT):
            WM(vector, "GQ2", ("pldf", o))
            if o >= 1:
                WM(vector, "ST", ("ptr2", o - 1))
            INC("SV", vector.tensor_copy(wbp[:, 0, :], wfp[:, 0, :]), 1, ("pcast", o))

    @J("vector")
    def _(vector):
        WM(vector, "LQ", "consts")
        for hh in range(H):
            if hh >= 1:
                WM(vector, "SS", ("loct", hh - 1))
            vector.tensor_scalar(loc[:, 0, :], idxloc[:], theta_pp[:, hh:hh + 1], None, ALU.mult)
            vector.tensor_scalar(loc[:, 1, :], loc[:, 0, :], 1.0 / TWO_PI, None, ALU.mult)
            vector.tensor_copy(loci[:, 0, :], loc[:, 1, :])
            vector.tensor_scalar(loc[:, 2, :], loc[:, 0, :], 1.0 / TWO_PI, None, ALU.mult)
            vector.tensor_scalar(loc[:, 2, :], loc[:, 2, :], 0.25, None, ALU.add)
            INC("SV", vector.tensor_copy(loci[:, 1, :], loc[:, 2, :]), 1, ("loci", hh))
            WM(vector, "SS", ("locf", hh))
            vector.scalar_tensor_tensor(loc[:, 3, :], loc[:, 1, :], -TWO_PI, loc[:, 0, :], ALU.mult, ALU.add)
            vector.scalar_tensor_tensor(loc[:, 2, :], loc[:, 2, :], -TWO_PI, loc[:, 0, :], ALU.mult, ALU.add)
            vector.tensor_scalar(loc[:, 1, :], loc[:, 3, :], PI, None, ALU.is_gt)
            vector.scalar_tensor_tensor(loc[:, 3, :], loc[:, 1, :], -TWO_PI, loc[:, 3, :], ALU.mult, ALU.add)
            vector.tensor_scalar(loc[:, 1, :], loc[:, 2, :], PI / 2, None, ALU.is_gt)
            INC("SV", vector.scalar_tensor_tensor(loc[:, 2, :], loc[:, 1, :], -TWO_PI, loc[:, 2, :], ALU.mult, ALU.add),
                1, ("locm", hh))
        vector.memset(st32[:], 0.0)
        INC("SV", vector.memset(stbf[:], 0.0), 1, ("stinit",))

    @J("tensor")
    def _(tensor):
        for o in range(DT):
            WM(tensor, "SV", ("pcast", o))
            if o >= 1:
                WM(tensor, "SS", ("pev2", o - 1))
            for k in range(8):
                inst = tensor.transpose(ps_T[:, k * 128:(k + 1) * 128],
                                        wbp[:, 0, k * 128:(k + 1) * 128], ident[:])
            INC("ST", inst, 1, ("ptr1", o))
            WM(tensor, "SS", ("pev1", o))
            for k in range(8):
                inst = tensor.transpose(ps_T[:, k * 128:(k + 1) * 128],
                                        wbp[:, 0, (8 + k) * 128:(9 + k) * 128], ident[:])
            INC("ST", inst, 1, ("ptr2", o))

    @J("scalar")
    def _(scalar):
        for o in range(DT):
            WM(scalar, "ST", ("ptr1", o))
            for k in range(8):
                INC("SS", scalar.copy(wpt[:, k, o * 128:(o + 1) * 128], ps_T[:, k * 128:(k + 1) * 128]),
                    1, ("pev1", o) if k == 7 else None)
            WM(scalar, "ST", ("ptr2", o))
            for k in range(8):
                INC("SS", scalar.copy(wpt[:, 8 + k, o * 128:(o + 1) * 128], ps_T[:, k * 128:(k + 1) * 128]),
                    1, ("pev2", o) if k == 7 else None)
        for hh in range(H):
            WM(scalar, "SV", ("loci", hh))
            scalar.copy(loc[:, 1, :], loci[:, 0, :])
            INC("SS", scalar.copy(loc[:, 2, :], loci[:, 1, :]), 1, ("locf", hh))
            WM(scalar, "SV", ("locm", hh))
            scalar.activation(silt[:, hh, :], loc[:, 3, :], AF.Sin, bias=c_ap(0.0), scale=c_ap(1.0))
            INC("SS", scalar.activation(cil[:, hh, :], loc[:, 2, :], AF.Sin, bias=c_ap(PI / 2), scale=c_ap(1.0)),
                1, ("loct", hh))

    @J("scalar")
    def _(scalar):
        if DEBUG:
            WM(scalar, "SS", ("loct", H - 1))
            INC("SQ", scalar.dma_start(tdmp[:, 0, :], cil[:, :, :]), 16)
            INC("SQ", scalar.dma_start(tdmp[:, 1, :], silt[:, :, :]), 16)

    phase_barrier("p2prolog")
    ploc.close()
    q_c = p2.enter_context(nc.sbuf_tensor("q_c", [128, 2, H, 128], BF16))
    k_c = p2.enter_context(nc.sbuf_tensor("k_c", [128, 2, H, 128], BF16))
    v_c = p2.enter_context(nc.sbuf_tensor("v_c", [128, 1, D], BF16))
    xT_g = p2.enter_context(nc.sbuf_tensor("xT_g", [128, DT, GT], BF16))
    y_g = p2.enter_context(nc.sbuf_tensor("y_g", [128, 2, H, GT], BF16))
    tabw = p2.enter_context(nc.sbuf_tensor("tabw", [128, 2, 128], F32))
    lrp = p2.enter_context(nc.sbuf_tensor("lrp", [128, 2, 6, 128], BF16))
    kT = p2.enter_context(nc.sbuf_tensor("kT", [128, 2, 2, 128], BF16))
    scb = p2.enter_context(nc.sbuf_tensor("scb", [128, 1, 128], BF16))
    c0s0 = p2.enter_context(nc.sbuf_tensor("c0s0", [128, 2, 2, 128], F32))
    rrq = p2.enter_context(nc.sbuf_tensor("rrq", [128, 3, 128], F32))
    stats = p2.enter_context(nc.sbuf_tensor("stats", [1, 4, GT], F32))
    rsq = p2.enter_context(nc.sbuf_tensor("rsq", [128, 1, GT], BF16))


    def prev_ch(C_abs, hh):
        pc = C_abs - 1
        return (pc // 4, pc % 4, hh)

    @J("sync")
    def _(sync):
        for g in range(NG):
            for c in range(4):
                C_abs = 4 * g + c
                par = C_abs % 2
                gq = "GQ" if par == 0 else "GQ2"
                if C_abs >= 2:
                    WM(sync, "SV", ("A", (C_abs - 2) // 4, (C_abs - 2) % 4, H - 1))
                    WM(sync, "ST", ("F", (C_abs - 2) // 4, (C_abs - 2) % 4, H - 1))
                for hh in range(H):
                    INC(gq, sync.dma_start(q_c[:, par, hh, :],
                                           qs[hh * 128:(hh + 1) * 128, C_abs * 128:(C_abs + 1) * 128]), 16)
                for hh in range(H - 1):
                    INC(gq, sync.dma_start(k_c[:, par, hh, :],
                                           ks[hh * 128:(hh + 1) * 128, C_abs * 128:(C_abs + 1) * 128]), 16)
                INC(gq, sync.dma_start(k_c[:, par, H - 1, :],
                                       ks[(H - 1) * 128:H * 128, C_abs * 128:(C_abs + 1) * 128]), 16,
                    ("chld", C_abs))
                if C_abs >= 1:
                    WM(sync, "ST", ("F", (C_abs - 1) // 4, (C_abs - 1) % 4, H - 1))
                INC(gq, sync.dma_start(v_c[:, 0, :], vs[C_abs * 128:(C_abs + 1) * 128, :]), 16,
                    ("vld", C_abs))
            if g >= 1:
                W(sync, "GRP", g)
            for k in range(DT):
                sync.dma_start(xT_g[:, k, :], xTs[k * 128:(k + 1) * 128, g * GT:(g + 1) * GT]).then_inc(sems["LQ"], 16)
                cnt["LQ"] += 16
            marks[("xtg", g)] = cnt["LQ"]

    def k3(n):
        C = n // H
        return (C // 4, C % 4, n % H)

    def p2_trig(Cn):
        g, c = Cn // 4, Cn % 4
        t0f = float(Cn * CH)

        @J("vector")
        def _(vector, g=g, c=c, t0f=t0f, Cn=Cn):
            if Cn >= 1:
                WM(vector, "SS", ("ts2", (Cn - 1) // 4, (Cn - 1) % 4))
            vector.tensor_scalar(rrq[:, 0, :], theta_pp[:], t0f, None, ALU.mult)
            vector.tensor_scalar(rrq[:, 1, :], rrq[:, 0, :], 1.0 / TWO_PI, None, ALU.mult)
            par = Cn % 2
            vector.tensor_copy(c0s0[:, par, 0, :].bitcast(I32), rrq[:, 1, :])
            vector.tensor_scalar(rrq[:, 2, :], rrq[:, 0, :], 1.0 / TWO_PI, None, ALU.mult)
            vector.tensor_scalar(rrq[:, 2, :], rrq[:, 2, :], 0.25, None, ALU.add)
            INC("SV", vector.tensor_copy(c0s0[:, par, 1, :].bitcast(I32), rrq[:, 2, :]), 1, ("tv1", g, c))

        @J("scalar")
        def _(scalar, g=g, c=c, Cn=Cn):
            WM(scalar, "SV", ("tv1", g, c))
            par = Cn % 2
            scalar.copy(rrq[:, 1, :], c0s0[:, par, 0, :].bitcast(I32))
            INC("SS", scalar.copy(rrq[:, 2, :], c0s0[:, par, 1, :].bitcast(I32)), 1, ("ts1", g, c))

        @J("vector")
        def _(vector, g=g, c=c):
            WM(vector, "SS", ("ts1", g, c))
            vector.scalar_tensor_tensor(rrq[:, 1, :], rrq[:, 1, :], -TWO_PI, rrq[:, 0, :], ALU.mult, ALU.add)
            vector.scalar_tensor_tensor(rrq[:, 2, :], rrq[:, 2, :], -TWO_PI, rrq[:, 0, :], ALU.mult, ALU.add)
            vector.tensor_scalar(rrq[:, 0, :], rrq[:, 1, :], PI, None, ALU.is_gt)
            vector.scalar_tensor_tensor(rrq[:, 1, :], rrq[:, 0, :], -TWO_PI, rrq[:, 1, :], ALU.mult, ALU.add)
            vector.tensor_scalar(rrq[:, 0, :], rrq[:, 2, :], PI / 2, None, ALU.is_gt)
            INC("SV", vector.scalar_tensor_tensor(rrq[:, 2, :], rrq[:, 0, :], -TWO_PI, rrq[:, 2, :], ALU.mult, ALU.add),
                1, ("tv2", g, c))

        @J("scalar")
        def _(scalar, g=g, c=c, Cn=Cn):
            WM(scalar, "SV", ("tv2", g, c))
            if Cn >= 2:
                WM(scalar, "SV", ("A",) + k3((Cn - 2) * H + H - 1))
            par = Cn % 2
            scalar.activation(c0s0[:, par, 1, :], rrq[:, 1, :], AF.Sin, bias=c_ap(0.0), scale=c_ap(1.0))
            INC("SS", scalar.activation(c0s0[:, par, 0, :], rrq[:, 2, :], AF.Sin, bias=c_ap(PI / 2), scale=c_ap(1.0)),
                1, ("ts2", g, c))

    def p2_A(n):
        g, c, h = k3(n)
        C_abs = n // H
        hp = h % 2
        cpar = C_abs % 2

        @J("vector")
        def _(vector, g=g, c=c, h=h, hp=hp, C_abs=C_abs, cpar=cpar, n=n):
            if h == 0:
                WM(vector, "GQ" if cpar == 0 else "GQ2", ("chld", C_abs))
            WM(vector, "SS", ("ts2", g, c))
            if n >= 2:
                WM(vector, "ST", ("F",) + k3(n - 2))
            vector.tensor_scalar(tabw[:, 0, :], silt[:, h, :], c0s0[:, cpar, 1, h:h + 1], None, ALU.mult)
            vector.scalar_tensor_tensor(tabw[:, 1, :], cil[:, h, :], c0s0[:, cpar, 0, h:h + 1], tabw[:, 0, :], ALU.mult, ALU.subtract)
            vector.tensor_scalar(tabw[:, 0, :], cil[:, h, :], c0s0[:, cpar, 1, h:h + 1], None, ALU.mult)
            vector.scalar_tensor_tensor(tabw[:, 0, :], silt[:, h, :], c0s0[:, cpar, 0, h:h + 1], tabw[:, 0, :], ALU.mult, ALU.add)
            qsl = q_c[:, cpar, h, :]
            ksl = k_c[:, cpar, h, :]
            vector.tensor_tensor(lrp[:, hp, 0, :], qsl, tabw[:, 1, :], ALU.mult)
            vector.tensor_tensor(lrp[:, hp, 1, :], qsl, tabw[:, 0, :], ALU.mult)
            vector.tensor_tensor(lrp[:, hp, 2, :], lrp[:, hp, 0, :], qdec[:, h * 128:(h + 1) * 128], ALU.mult)
            vector.tensor_tensor(lrp[:, hp, 3, :], lrp[:, hp, 1, :], qdec[:, h * 128:(h + 1) * 128], ALU.mult)
            vector.tensor_tensor(lrp[:, hp, 4, :], ksl, tabw[:, 1, :], ALU.mult)
            INC("SV", vector.tensor_tensor(lrp[:, hp, 5, :], ksl, tabw[:, 0, :], ALU.mult), 1, ("A", g, c, h))

    def p2_BD(n):
        g, c, h = k3(n)
        hp = h % 2

        @J("tensor")
        def _(tensor, g=g, c=c, h=h, hp=hp, n=n):
            WM(tensor, "SV", ("A", g, c, h))
            if n >= 2:
                WM(tensor, "SS", ("C",) + k3(n - 2))
            psT = ps_T if hp == 0 else ps_T2
            tensor.transpose(psT[:, 0:128], lrp[:, hp, 4, :], ident[:])
            INC("ST", tensor.transpose(psT[:, 128:256], lrp[:, hp, 5, :], ident[:]),
                1, ("B", g, c, h))
            if n >= 1:
                WM(tensor, "SV", ("Esc",) + k3(n - 1))
            tensor.matmul(ps_S[:, 0:128], lrp[:, hp, 4, :], lrp[:, hp, 0, :], start=True, stop=False)
            INC("ST", tensor.matmul(ps_S[:, 0:128], lrp[:, hp, 5, :], lrp[:, hp, 1, :], start=False, stop=True),
                1, ("D", g, c, h))

    def p2_C(n):
        g, c, h = k3(n)
        hp = h % 2

        @J("scalar")
        def _(scalar, g=g, c=c, h=h, hp=hp):
            WM(scalar, "ST", ("B", g, c, h))
            psT = ps_T if hp == 0 else ps_T2
            scalar.activation(kT[:, hp, 0, :], psT[:, 0:128], AF.Copy,
                              bias=0.0, scale=kdecpp[:, h:h + 1])
            INC("SS", scalar.activation(kT[:, hp, 1, :], psT[:, 128:256], AF.Copy,
                                        bias=0.0, scale=kdecpp[:, h:h + 1]), 1, ("C", g, c, h))

    def p2_Esc(n):
        g, c, h = k3(n)
        hp = h % 2

        @J("vector")
        def _(vector, g=g, c=c, h=h, hp=hp, n=n):
            WM(vector, "ST", ("D", g, c, h))
            if n >= 1:
                WM(vector, "ST", ("F",) + k3(n - 1))
            INC("SV", vector.tensor_tensor(scb[:, 0, :], ps_S[:, 0:128], diagT[:, h * 128:(h + 1) * 128], ALU.mult),
                1, ("Esc", g, c, h))

    def p2_F(n):
        g, c, h = k3(n)
        C_abs = n // H
        hp = h % 2
        cpar = C_abs % 2

        @J("tensor")
        def _(tensor, g=g, c=c, h=h, hp=hp, C_abs=C_abs, cpar=cpar, n=n):
            WM(tensor, "SV", ("Esc", g, c, h))
            WM(tensor, "SS", ("C", g, c, h))
            if h == 0:
                WM(tensor, "GQ" if cpar == 0 else "GQ2", ("vld", C_abs))
            if n >= 1:
                WM(tensor, "SS", ("G",) + k3(n - 1))
                WM(tensor, "SV", ("Hst",) + k3(n - 1))
            if C_abs >= 1:
                WM(tensor, "SV", ("Hst",) + k3(n - H))
            vsl = v_c[:, 0, h * 128:(h + 1) * 128]
            tensor.matmul(ps_O[:, 0:128], vsl, scb[:, 0, :], start=True, stop=False)
            tensor.matmul(ps_O[:, 0:128], stbf[:, 2 * h, :], lrp[:, hp, 2, :], start=False, stop=False)
            tensor.matmul(ps_O[:, 0:128], stbf[:, 2 * h + 1, :], lrp[:, hp, 3, :], start=False, stop=True)
            tensor.matmul(ps_U[:, 0:128], kT[:, hp, 0, :], vsl, start=True, stop=True)
            INC("ST", tensor.matmul(ps_U[:, 128:256], kT[:, hp, 1, :], vsl, start=True, stop=True),
                1, ("F", g, c, h))

    def p2_G(n):
        g, c, h = k3(n)

        @J("scalar")
        def _(scalar, g=g, c=c, h=h):
            WM(scalar, "ST", ("F", g, c, h))
            INC("SS", scalar.copy(y_g[:, g % 2, h, c * 128:(c + 1) * 128], ps_O[:, 0:128]), 1, ("G", g, c, h))

    def p2_Hst(n):
        g, c, h = k3(n)

        @J("vector")
        def _(vector, g=g, c=c, h=h, bd=bdec[h]):
            WM(vector, "ST", ("F", g, c, h))
            vector.scalar_tensor_tensor(st32[:, 2 * h, :], st32[:, 2 * h, :], bd, ps_U[:, 0:128], ALU.mult, ALU.add)
            vector.scalar_tensor_tensor(st32[:, 2 * h + 1, :], st32[:, 2 * h + 1, :], bd, ps_U[:, 128:256], ALU.mult, ALU.add)
            vector.tensor_copy(stbf[:, 2 * h, :], st32[:, 2 * h, :])
            INC("SV", vector.tensor_copy(stbf[:, 2 * h + 1, :], st32[:, 2 * h + 1, :]), 1, ("Hst", g, c, h))

    def p2_pjmm(pg, ot):
        @J("tensor")
        def _(tensor, pg=pg, ot=ot):
            if ot == 0:
                WM(tensor, "SS", ("G", pg, 3, H - 1))
                WM(tensor, "LQ", ("xtg", pg))
            else:
                WM(tensor, "SV", ("rres", pg, ot - 1))
            for k in range(DT):
                inst = tensor.matmul(ps_P[:, :], wpt[:, k, ot * 128:(ot + 1) * 128], y_g[:, pg % 2, k, :],
                                     start=(k == 0), stop=(k == DT - 1))
            INC("ST", inst, 1, ("pjmm", pg, ot))

    def p2_rres(pg, ot):
        @J("vector")
        def _(vector, pg=pg, ot=ot):
            WM(vector, "ST", ("pjmm", pg, ot))
            INC("SV", vector.tensor_tensor(xT_g[:, ot, :], ps_P[:, :], xT_g[:, ot, :], ALU.add),
                1, ("rres", pg, ot))

        @J("scalar")
        def _(scalar, pg=pg, ot=ot):
            WM(scalar, "SV", ("rres", pg, ot))
            if ot == 0 and pg >= 1:
                WM(scalar, "SQ", ("lndma", pg - 1, DT - 1))
            if ot >= 1:
                WM(scalar, "ST", ("sqmm", pg, ot - 1))
            INC("SS", scalar.activation(rsq[:, 0, :], xT_g[:, ot, :], AF.Square,
                                        bias=c_ap(0.0), scale=c_ap(1.0)), 1, ("rsq", pg, ot))

    def p2_summ(pg, ot):
        @J("tensor")
        def _(tensor, pg=pg, ot=ot):
            WM(tensor, "SV", ("rres", pg, ot))
            INC("ST", tensor.matmul(ps_st[0:1, 0:512], onescol[:], xT_g[:, ot, :],
                                    start=(ot == 0), stop=(ot == DT - 1)), 1, ("summ", pg, ot))

    def p2_sqmm(pg, ot):
        @J("tensor")
        def _(tensor, pg=pg, ot=ot):
            WM(tensor, "SS", ("rsq", pg, ot))
            INC("ST", tensor.matmul(ps_bc[0:1, 0:512], onescol[:], rsq[:, 0, :],
                                    start=(ot == 0), stop=(ot == DT - 1)), 1, ("sqmm", pg, ot))

    def p2_stats1(pg):
        @J("vector")
        def _(vector, pg=pg):
            WM(vector, "ST", ("summ", pg, DT - 1))
            WM(vector, "ST", ("sqmm", pg, DT - 1))
            vector.tensor_scalar(stats[0:1, 0, :], ps_st[0:1, 0:512], 1.0 / D, None, ALU.mult)
            vector.tensor_scalar(stats[0:1, 1, :], ps_bc[0:1, 0:512], 1.0 / D, None, ALU.mult)
            vector.tensor_tensor(stats[0:1, 3, :], stats[0:1, 0, :], stats[0:1, 0, :], ALU.mult)
            INC("SV", vector.tensor_tensor(stats[0:1, 1, :], stats[0:1, 1, :], stats[0:1, 3, :], ALU.subtract),
                1, ("var", pg))

        @J("scalar")
        def _(scalar, pg=pg):
            WM(scalar, "SV", ("var", pg))
            INC("SS", scalar.activation(stats[0:1, 2, :], stats[0:1, 1, :], AF.Sqrt,
                                        bias=c_ap1(1e-5), scale=c_ap1(1.0)), 1, ("sqv", pg))

    def p2_stats2(pg):
        @J("vector")
        def _(vector, pg=pg):
            WM(vector, "SS", ("sqv", pg))
            INC("SV", vector.reciprocal(stats[0:1, 3, :], stats[0:1, 2, :]), 1, ("fin", pg))

        @J("tensor")
        def _(tensor, pg=pg):
            WM(tensor, "SV", ("fin", pg))
            tensor.matmul(ps_st[:, 0:512], onesrow[:, :], stats[0:1, 0, :], start=True, stop=True)
            INC("ST", tensor.matmul(ps_bc[:, 0:512], onesrow[:, :], stats[0:1, 3, :], start=True, stop=True),
                1, ("bcast", pg))

    def p2_ln(pg, ot):
        @J("vector")
        def _(vector, pg=pg, ot=ot):
            WM(vector, "ST", ("bcast", pg))
            if ot >= 1:
                WM(vector, "SQ", ("lndma", pg, ot - 1))
            vector.tensor_tensor(rsq[:, 0, :], xT_g[:, ot, :], ps_st[:, 0:512], ALU.subtract)
            INC("SV", vector.tensor_tensor(rsq[:, 0, :], rsq[:, 0, :], ps_bc[:, 0:512], ALU.mult),
                1, ("ln", pg, ot))

        @J("scalar")
        def _(scalar, pg=pg, ot=ot):
            WM(scalar, "SV", ("ln", pg, ot))
            INC("SQ", scalar.dma_start(ln1s[ot * 128:(ot + 1) * 128, pg * GT:(pg + 1) * GT],
                                       rsq[:, 0, :]), 16, ("lndma", pg, ot))

    def p2_gdone(pg):
        @J("vector")
        def _(vector, pg=pg):
            INC("SV", vector.engine_nop(), 1, ("gdone", pg))

        @J("gpsimd")
        def _(gp, pg=pg):
            WM(gp, "SV", ("gdone", pg))
            gp.sem_inc(sems["GRP"], 1)
            cnt["GRP"] += 1

    def p2_ydump(pg):
        @J("scalar")
        def _(scalar, pg=pg):
            if DEBUG:
                WM(scalar, "SS", ("G", pg, 3, H - 1))
                for hh in range(H):
                    INC("SQ", scalar.dma_start(ysd[hh * 128:(hh + 1) * 128, pg * GT:(pg + 1) * GT],
                                               y_g[:, pg % 2, hh, :]), 16)

    def p2_proj_extras(pg, m):
        if pg < 0:
            return
        if m == 2 and DEBUG:
            p2_ydump(pg)
        if m >= 2 and m % 2 == 0 and (m - 2) // 2 < DT:
            p2_pjmm(pg, (m - 2) // 2)
        if m >= 3 and m % 2 == 1 and (m - 3) // 2 < DT:
            p2_rres(pg, (m - 3) // 2)
        if m >= 4 and m % 2 == 0 and (m - 4) // 2 < DT:
            p2_summ(pg, (m - 4) // 2)
        if m >= 5 and m % 2 == 1 and (m - 5) // 2 < DT:
            p2_sqmm(pg, (m - 5) // 2)
        if m == 37:
            p2_stats1(pg)
        elif m == 39:
            p2_stats2(pg)
        elif 41 <= m < 41 + DT:
            p2_ln(pg, m - 41)
        elif m == 41 + DT + 1:
            p2_gdone(pg)

    NSTEP = NCH * H
    p2_trig(0)
    for n in range(NSTEP + 2):
        if n < NSTEP:
            C_abs = n // H
            h = n % H
            if h == 4 and C_abs + 1 < NCH:
                p2_trig(C_abs + 1)
            p2_A(n)
        if 1 <= n <= NSTEP:
            p2_Esc(n - 1)
        if n >= 2:
            p2_Hst(n - 2)
        if n < NSTEP:
            p2_BD(n)
            p2_C(n)
        if n >= 1 and n - 1 < NSTEP:
            p2_F(n - 1)
            p2_G(n - 1)
        if n < NSTEP:
            p2_proj_extras(n // 64 - 1, n % 64)

    if DEBUG:
        p2_ydump(NG - 1)
    for ot in range(DT):
        p2_pjmm(NG - 1, ot)
        p2_rres(NG - 1, ot)
        p2_summ(NG - 1, ot)
        p2_sqmm(NG - 1, ot)
    p2_stats1(NG - 1)
    p2_stats2(NG - 1)
    for ot in range(DT):
        p2_ln(NG - 1, ot)
    p2_gdone(NG - 1)

    phase_barrier("p2")
    p2.close()
    if PHASES <= 2:
        blk_cm.__exit__(None, None, None)
        es.close()
        LAST_MARKS = dict(marks)
        return nc

    p3 = ExitStack()
    ps3_T = p3.enter_context(nc.psum_tensor("p3_T", [128, 1024], BF16))
    ps3_T2 = p3.enter_context(nc.psum_tensor("p3_T2", [128, 1024], BF16))
    ps3_T3 = p3.enter_context(nc.psum_tensor("p3_T3", [128, 1024], BF16))
    ps3_T4 = p3.enter_context(nc.psum_tensor("p3_T4", [128, 1024], BF16))
    ps3_G = p3.enter_context(nc.psum_tensor("p3_G", [128, 512], F32))
    ps3_G2 = p3.enter_context(nc.psum_tensor("p3_G2", [128, 512], F32))
    ps3_U = p3.enter_context(nc.psum_tensor("p3_U", [128, 512], F32))
    ps3_U2 = p3.enter_context(nc.psum_tensor("p3_U2", [128, 512], F32))
    TP = [(ps3_T, ps3_T2), (ps3_T3, ps3_T4)]

    def emit_cpool_init(cpool, mk):
        @J("gpsimd")
        def _(gp):
            gp.memset(cpool[:, 0, :], 251)
            gp.memset(cpool[:, 1, :], 127)
            INC("SP", gp.memset(cpool[:, 2, :], 126), 1, mk)

    DQ = {"q": []}

    def _dq_v1(d):
        @J("vector")
        def _(vector, d=d):
            WM(vector, d.get("ld_sem", "LQ"), d["ld_mark"])
            if d["prev2"] is not None:
                WM(vector, "SS", ("dqr", d["prev2"]))
            b = 5 * d["set"]
            dqa = d["dqa"]
            wN = d["wN"]
            L = 8 * wN
            src = d["src_ap_fn"]()
            vector.tensor_copy(dqa[:, b + 0, 0:wN], src)
            for j in range(1, 8):
                vector.tensor_scalar(dqa[:, b + 0, j * wN:(j + 1) * wN], src, 4 * j, None, ALU.logical_shift_right)
            vector.tensor_scalar(dqa[:, b + 1, 0:L], dqa[:, b + 0, 0:L], 0x7, None, ALU.bitwise_and)
            vector.tensor_scalar(dqa[:, b + 2, 0:L], dqa[:, b + 0, 0:L], 12, None, ALU.logical_shift_left)
            INC("SV", vector.tensor_scalar(dqa[:, b + 2, 0:L], dqa[:, b + 2, 0:L], 0x8000, None, ALU.bitwise_and),
                1, ("dqk", d["key"]))

    def _dq_p1(d):
        @J("gpsimd")
        def _(gp, d=d):
            WM(gp, "SV", ("dqk", d["key"]))
            b = 5 * d["set"]
            dqa = d["dqa"]
            cpool = d["cpool"]
            L = 8 * d["wN"]
            gp.tensor_tensor(dqa[:, b + 3, 0:L], dqa[:, b + 1, 0:L], cpool[:, 1, 0:L], ALU.add)
            INC("SP", gp.tensor_tensor(dqa[:, b + 4, 0:L], dqa[:, b + 1, 0:L], cpool[:, 2, 0:L], ALU.add),
                1, ("dqp1", d["key"]))

    def _dq_v2(d):
        @J("vector")
        def _(vector, d=d):
            WM(vector, "SP", ("dqp1", d["key"]))
            b = 5 * d["set"]
            dqa = d["dqa"]
            L = 8 * d["wN"]
            vector.tensor_scalar(dqa[:, b + 3, 0:L], dqa[:, b + 3, 0:L], 7, None, ALU.logical_shift_right)
            INC("SV", vector.tensor_scalar(dqa[:, b + 4, 0:L], dqa[:, b + 4, 0:L], 7, None, ALU.logical_shift_right),
                1, ("dqf", d["key"]))

    def _dq_p2(d):
        @J("gpsimd")
        def _(gp, d=d):
            WM(gp, "SV", ("dqf", d["key"]))
            b = 5 * d["set"]
            dqa = d["dqa"]
            cpool = d["cpool"]
            L = 8 * d["wN"]
            gp.tensor_tensor(dqa[:, b + 3, 0:L], dqa[:, b + 3, 0:L], cpool[:, 0, 0:L], ALU.mult)
            gp.tensor_tensor(dqa[:, b + 4, 0:L], dqa[:, b + 3, 0:L], dqa[:, b + 4, 0:L], ALU.add)
            INC("SP", gp.tensor_tensor(dqa[:, b + 4, 0:L], dqa[:, b + 4, 0:L], dqa[:, b + 1, 0:L], ALU.add),
                1, ("dqp2", d["key"]))

    def _dq_v3(d):
        @J("vector")
        def _(vector, d=d):
            WM(vector, "SP", ("dqp2", d["key"]))
            b = 5 * d["set"]
            dqa = d["dqa"]
            L = 8 * d["wN"]
            vector.tensor_scalar(dqa[:, b + 4, 0:L], dqa[:, b + 4, 0:L], 6, None, ALU.logical_shift_left)
            INC("SV", vector.tensor_tensor(dqa[:, b + 4, 0:L], dqa[:, b + 4, 0:L], dqa[:, b + 2, 0:L], ALU.bitwise_or),
                1, ("dqb", d["key"]))

    def _dq_a(d):
        @J("scalar")
        def _(scalar, d=d):
            WM(scalar, "SV", ("dqb", d["key"]))
            if d["sclc"] is not None:
                WM(scalar, "SV", d["sclc"])
            if d["dense_gate"] is not None:
                WM(scalar, "ST", d["dense_gate"])
            b = 5 * d["set"]
            dqa = d["dqa"]
            wN = d["wN"]
            L = 8 * wN
            ng = L // 128
            gw = wN // ng
            fb = dqa[:, b + 4, 0:L].bitcast(BF16)
            for gi in range(ng):
                srcap = bass.AP(fb.tensor, fb.offset + 2 * gi * gw, [[fb.ap[0][0], 128], [2 * wN, 8], [2, gw]])
                dst = bass.AP(d["dense_t"], d["dense_off"] + gi * 128, [[d["row_len"], 128], [1, 8], [8, gw]])
                inst = scalar.activation(dst, srcap, AF.Copy, bias=0.0, scale=d["scale_fn"](gi))
                if gi == ng - 1:
                    INC("SS", inst, 1, ("dqr", d["key"]))

    def _dq_round(r):
        q = DQ["q"]
        n = len(q)
        if 0 <= r - 2 < n and not q[r - 2]["s"][4]:
            _dq_v3(q[r - 2])
            q[r - 2]["s"][4] = True
        if 0 <= r - 1 < n and not q[r - 1]["s"][2]:
            _dq_v2(q[r - 1])
            q[r - 1]["s"][2] = True
        if r < n and not q[r]["s"][0]:
            _dq_v1(q[r])
            q[r]["s"][0] = True
        if 0 <= r - 1 < n and not q[r - 1]["s"][3]:
            _dq_p2(q[r - 1])
            q[r - 1]["s"][3] = True
        if r < n and not q[r]["s"][1]:
            _dq_p1(q[r])
            q[r]["s"][1] = True
        if 0 <= r - 2 < n and not q[r - 2]["s"][5]:
            _dq_a(q[r - 2])
            q[r - 2]["s"][5] = True

    def dq_pass(key, ld_mark, src_ap_fn, wN, dense_t, dense_off, row_len, scale_fn, dqa, cpool, sclc=None, dense_gate=None, ld_sem="LQ"):
        q = DQ["q"]
        idx = len(q)
        d = dict(key=key, ld_mark=ld_mark, src_ap_fn=src_ap_fn, wN=wN, dense_t=dense_t,
                 dense_off=dense_off, row_len=row_len, scale_fn=scale_fn, dqa=dqa,
                 cpool=cpool, sclc=sclc, dense_gate=dense_gate, set=idx % 2, ld_sem=ld_sem,
                 prev2=q[idx - 2]["key"] if idx >= 2 else None, s=[False] * 6)
        q.append(d)
        _dq_round(idx)

    def dq_drain():
        n = len(DQ["q"])
        _dq_round(n)
        _dq_round(n + 1)

    def dq_reset():
        DQ["q"] = []

    p3a = ExitStack()
    dqa_a = p3a.enter_context(nc.sbuf_tensor("dqa_a", [128, 10, 512], I32))
    cpool_a = p3a.enter_context(nc.sbuf_tensor("cpool_a", [128, 3, 512], I32))
    emit_cpool_init(cpool_a, ("p3ac",))
    ln1a = p3a.enter_context(nc.sbuf_tensor("ln1a", [128, DT, T], BF16))
    pk = p3a.enter_context(nc.sbuf_tensor("pk", [128, 2, 2, 256], I32))
    scl6 = p3a.enter_context(nc.sbuf_tensor("scl6", [128, 2 * IT, DT], F16))
    sclf = p3a.enter_context(nc.sbuf_tensor("sclf", [128, 2 * IT, DT], F32))
    dense = p3a.enter_context(nc.sbuf_tensor("dense", [128, 2, D], BF16))
    wtg = p3a.enter_context(nc.sbuf_tensor("wtg", [128, 2, 2, DT * 128], BF16))
    sg_sb = p3a.enter_context(nc.sbuf_tensor("sg_sb", [128, 2, 512], BF16))
    hstg = p3a.enter_context(nc.sbuf_tensor("hstg", [128, 2, 512], BF16))

    @J("sync")
    def _(sync):
        for b in range(2 * IT):
            sync.dma_start(scl6[:, b, :], gus[b * 128:(b + 1) * 128, :]).then_inc(sems["LQ"], 16)
            cnt["LQ"] += 16
        marks[("sclall",)] = cnt["LQ"]
        for k in range(DT):
            sync.dma_start(ln1a[:, k, :], ln1s[k * 128:(k + 1) * 128, :]).then_inc(sems["LQ"], 16)
            cnt["LQ"] += 16
        marks[("ln1a",)] = cnt["LQ"]
        for ig in range(IT):
            par = ig % 2
            gq = "GQ" if par == 0 else "GQ2"
            if ig >= 2:
                WM(sync, "SV", ("dqk", ("u", ig - 2, 3)))
            INC(gq, sync.dma_start(pk[:, par, 0, :], gup[ig * 128:(ig + 1) * 128, :]), 16)
            INC(gq, sync.dma_start(pk[:, par, 1, :], gup[IF + ig * 128:IF + (ig + 1) * 128, :]), 16,
                ("pkR", ig))

    @J("vector")
    def _(vector):
        WM(vector, "LQ", ("ln1a",))
        INC("SV", vector.tensor_copy(sclf[:], scl6[:]), 1, ("sclallc",))

    def emit_tr_group(tensor, tag, gu, igx, half):
        pst = TP[igx % 2][half]
        WM(tensor, "SS", ("dqr", (tag, igx, 2 * half + 1)))
        if gu == 1:
            WM(tensor, "SS", ("wtev", "g", igx, half))
        elif igx >= 2:
            WM(tensor, "SS", ("wtev", "u", igx - 2, half))
        for k in range(8):
            inst = tensor.transpose(pst[:, k * 128:(k + 1) * 128],
                                    dense[:, gu, (half * 8 + k) * 128:(half * 8 + k + 1) * 128], ident[:])
        INC("ST", inst, 1, ("wtr", tag, igx, half))

    def emit_evac(tag, gu, igx, half):
        @J("scalar")
        def _(scalar, tag=tag, gu=gu, igx=igx, half=half):
            WM(scalar, "ST", ("wtr", tag, igx, half))
            if igx >= 2:
                WM(scalar, "ST", ("umm", igx - 2, NG - 1))
            pst = TP[igx % 2][half]
            for k in range(8):
                INC("SS", scalar.copy(wtg[:, igx % 2, gu, (half * 8 + k) * 128:(half * 8 + k + 1) * 128],
                                      pst[:, k * 128:(k + 1) * 128]),
                    1, ("wtev", tag, igx, half) if k == 7 else None)

    def emit_silu(igx, gg):
        @J("scalar")
        def _(scalar, igx=igx, gg=gg):
            if gg >= 2:
                WM(scalar, "SV", ("hmul", igx, gg - 2))
            elif igx >= 1:
                WM(scalar, "SV", ("hmul", igx - 1, NG - 2 + gg))
            WM(scalar, "ST", ("gmm", igx, gg))
            psg = ps3_G if gg % 2 == 0 else ps3_G2
            INC("SS", scalar.activation(sg_sb[:, gg % 2, :], psg[:, :], AF.Silu,
                                        bias=c_ap(0.0), scale=c_ap(1.0)), 1, ("silu", igx, gg))

    def emit_hdma(igx, gg):
        @J("scalar")
        def _(scalar, igx=igx, gg=gg):
            WM(scalar, "SV", ("hmul", igx, gg))
            INC("SQ", scalar.dma_start(hs[igx * 128:(igx + 1) * 128, gg * GT:(gg + 1) * GT],
                                       hstg[:, gg % 2, :]), 16, ("hdma", igx, gg))

    def emit_hmul(igx, gg):
        @J("vector")
        def _(vector, igx=igx, gg=gg):
            WM(vector, "ST", ("umm", igx, gg))
            WM(vector, "SS", ("silu", igx, gg))
            if gg >= 2:
                WM(vector, "SQ", ("hdma", igx, gg - 1))
            elif igx >= 1:
                WM(vector, "SQ", ("hdma", igx - 1, NG - 1))
            psu = ps3_U if gg % 2 == 0 else ps3_U2
            INC("SV", vector.tensor_tensor(hstg[:, gg % 2, :], psu[:, :], sg_sb[:, gg % 2, :], ALU.mult),
                1, ("hmul", igx, gg))

    for ig in range(IT):
        par = ig % 2
        for p8 in range(8):
            gu = p8 // 4
            ps = p8 % 4
            tag = "g" if gu == 0 else "u"
            dq_pass(
                key=(tag, ig, ps),
                ld_mark=("pkR", ig), ld_sem="GQ" if ig % 2 == 0 else "GQ2",
                src_ap_fn=(lambda par=par, gu=gu, ps=ps: pk[:, par, gu, ps * 64:(ps + 1) * 64]),
                wN=64,
                dense_t=dense, dense_off=gu * D + ps * 512, row_len=2 * D,
                scale_fn=(lambda gi, ig=ig, gu=gu, ps=ps: sclf[:, gu * IT + ig, ps * 4 + gi:ps * 4 + gi + 1]),
                dqa=dqa_a, cpool=cpool_a, sclc=("sclallc",),
                dense_gate=("wtr", tag, ig - 1, ps // 2) if ig >= 1 else None,
            )
            if p8 == 7:
                dq_drain()
            if ig >= 1:
                if p8 >= 3:
                    emit_hdma(ig - 1, p8 - 3)
                if p8 >= 2:
                    emit_hmul(ig - 1, p8 - 2)
                emit_silu(ig - 1, p8)
            if p8 == 3:
                emit_evac("g", 0, ig, 0)
            elif p8 == 5:
                emit_evac("g", 0, ig, 1)
            elif p8 == 7:
                emit_evac("u", 1, ig, 0)
        emit_evac("u", 1, ig, 1)
        if ig >= 1:
            emit_hmul(ig - 1, NG - 2)
            emit_hmul(ig - 1, NG - 1)
            for gg in (NG - 3, NG - 2, NG - 1):
                emit_hdma(ig - 1, gg)
        if DEBUG:
            @J("scalar")
            def _(scalar, ig=ig):
                WM(scalar, "SS", ("dqr", ("g", ig, 3)))
                INC("SQ", scalar.dma_start(wgd[ig * 128:(ig + 1) * 128, :], dense[:, 0, :]), 16)
                WM(scalar, "SS", ("dqr", ("u", ig, 3)))
                INC("SQ", scalar.dma_start(wgd[IF + ig * 128:IF + (ig + 1) * 128, :], dense[:, 1, :]), 16)

        @J("tensor")
        def _(tensor, ig=ig, par=par):
            if ig == 0:
                WM(tensor, "LQ", ("ln1a",))
                for gu, tag in ((0, "g"), (1, "u")):
                    for half in range(2):
                        emit_tr_group(tensor, tag, gu, 0, half)
            for gg in range(NG):
                psg = ps3_G if gg % 2 == 0 else ps3_G2
                psu = ps3_U if gg % 2 == 0 else ps3_U2
                if gg == 0:
                    for half in range(2):
                        WM(tensor, "SS", ("wtev", "g", ig, half))
                if gg >= 2 or ig >= 1:
                    pkey = ("silu", ig, gg - 2) if gg >= 2 else ("silu", ig - 1, NG - 2 + gg)
                    WM(tensor, "SS", pkey)
                for k in range(DT):
                    inst = tensor.matmul(psg[:, :], wtg[:, par, 0, k * 128:(k + 1) * 128],
                                         ln1a[:, k, gg * GT:(gg + 1) * GT], start=(k == 0), stop=(k == DT - 1))
                INC("ST", inst, 1, ("gmm", ig, gg))
                if gg == 0:
                    for half in range(2):
                        WM(tensor, "SS", ("wtev", "u", ig, half))
                if gg >= 2 or ig >= 1:
                    pkey = ("hmul", ig, gg - 2) if gg >= 2 else ("hmul", ig - 1, NG - 2 + gg)
                    WM(tensor, "SV", pkey)
                for k in range(DT):
                    inst = tensor.matmul(psu[:, :], wtg[:, par, 1, k * 128:(k + 1) * 128],
                                         ln1a[:, k, gg * GT:(gg + 1) * GT], start=(k == 0), stop=(k == DT - 1))
                INC("ST", inst, 1, ("umm", ig, gg))
                if ig + 1 < IT:
                    if gg == 2:
                        emit_tr_group(tensor, "g", 0, ig + 1, 0)
                    elif gg == 4:
                        emit_tr_group(tensor, "g", 0, ig + 1, 1)
                    elif gg == 6:
                        emit_tr_group(tensor, "u", 1, ig + 1, 0)
                        emit_tr_group(tensor, "u", 1, ig + 1, 1)

    for p8 in range(8):
        if p8 >= 3:
            emit_hdma(IT - 1, p8 - 3)
        if p8 >= 2:
            emit_hmul(IT - 1, p8 - 2)
        emit_silu(IT - 1, p8)
    emit_hmul(IT - 1, NG - 2)
    emit_hmul(IT - 1, NG - 1)
    for gg in (NG - 3, NG - 2, NG - 1):
        emit_hdma(IT - 1, gg)


    phase_barrier("p3a")
    p3a.close()
    dq_reset()
    if PHASES <= 3:
        p3.close()
        blk_cm.__exit__(None, None, None)
        es.close()
        LAST_MARKS = dict(marks)
        return nc

    p3p = ExitStack()
    dqa_p = p3p.enter_context(nc.sbuf_tensor("dqa_p", [128, 10, 512], I32))
    cpool_p = p3p.enter_context(nc.sbuf_tensor("cpool_p", [128, 3, 512], I32))
    emit_cpool_init(cpool_p, ("p3pc",))
    pkd = p3p.enter_context(nc.sbuf_tensor("pkd", [128, 2, 704], I32))
    scld = p3p.enter_context(nc.sbuf_tensor("scld", [128, DT, IT], F16))
    scldf = p3p.enter_context(nc.sbuf_tensor("scldf", [128, DT, IT], F32))
    densed = p3p.enter_context(nc.sbuf_tensor("densed", [128, IF], BF16))
    wstg = p3p.enter_context(nc.sbuf_tensor("wstg", [128, IF], BF16))

    @J("sync")
    def _(sync):
        for b in range(DT):
            sync.dma_start(scld[:, b, :], dns[b * 128:(b + 1) * 128, :]).then_inc(sems["LQ"], 16)
            cnt["LQ"] += 16
        marks[("dsclall",)] = cnt["LQ"]
        WM(sync, "SV", ("dqk", ("u", IT - 1, 3)))
        for ot in range(DT):
            if ot >= 2:
                WM(sync, "SV", ("dqk", ("d", ot - 2, 10)))
            INC("GQ" if ot % 2 == 0 else "GQ2",
                sync.dma_start(pkd[:, ot % 2, :], dnp[ot * 128:(ot + 1) * 128, :]), 16, ("pkdR", ot))

    @J("vector")
    def _(vector):
        WM(vector, "LQ", ("dsclall",))
        INC("SV", vector.tensor_copy(scldf[:], scld[:]), 1, ("dsclallc",))

    NPASS = 11
    PSQ = [ps3_T, ps3_T2, ps3_T3, ps3_T4]
    for ot in range(DT):
        par = ot % 2
        for ps in range(NPASS):
            w0 = ps * 64
            dq_pass(
                key=("d", ot, ps),
                ld_mark=("pkdR", ot), ld_sem="GQ" if ot % 2 == 0 else "GQ2",
                src_ap_fn=(lambda par=par, w0=w0: pkd[:, par, w0:w0 + 64]),
                wN=64,
                dense_t=densed, dense_off=8 * w0, row_len=IF,
                scale_fn=(lambda gi, ot=ot, w0=w0: scldf[:, ot, (8 * w0) // 128 + gi:(8 * w0) // 128 + gi + 1]),
                dqa=dqa_p, cpool=cpool_p, sclc=("dsclallc",),
                dense_gate=("dtr", ot - 1, 4 * ps + 3) if ot >= 1 else None,
            )
        dq_drain()

        @J("tensor")
        def _(tensor, ot=ot):
            for ib in range(IT):
                pst = PSQ[ib % 4]
                if ib >= 4:
                    WM(tensor, "SS", ("dtev", ot, ib - 4))
                elif ot >= 1:
                    WM(tensor, "SS", ("dtev", ot - 1, IT - 4 + ib))
                WM(tensor, "SS", ("dqr", ("d", ot, ib // 4)))
                INC("ST", tensor.transpose(pst[:, 0:128], densed[:, ib * 128:(ib + 1) * 128], ident[:]),
                    1, ("dtr", ot, ib))

        @J("scalar")
        def _(scalar, ot=ot):
            if ot >= 1:
                WM(scalar, "SQ", ("wddma", ot - 1))
            for ib in range(IT):
                WM(scalar, "ST", ("dtr", ot, ib))
                pst = PSQ[ib % 4]
                INC("SS", scalar.copy(wstg[:, ib * 128:(ib + 1) * 128], pst[:, 0:128]), 1, ("dtev", ot, ib))
            WM(scalar, "SS", ("dtev", ot, IT - 1))
            INC("SQ", scalar.dma_start(wdTs[ot * 128:(ot + 1) * 128, :], wstg[:]), 16, ("wddma", ot))
            if DEBUG:
                INC("SQ", scalar.dma_start(wdd[ot * 128:(ot + 1) * 128, :], wstg[:]), 16)

    phase_barrier("p3prep")
    p3p.close()
    if PHASES <= 4:
        p3.close()
        blk_cm.__exit__(None, None, None)
        es.close()
        LAST_MARKS = dict(marks)
        return nc

    p3m = ExitStack()
    h_g = p3m.enter_context(nc.sbuf_tensor("h_g", [128, IT, GT], BF16))
    ln1g = p3m.enter_context(nc.sbuf_tensor("ln1g", [128, DT, GT], BF16))
    wdt = p3m.enter_context(nc.sbuf_tensor("wdt", [128, 2, IF], BF16))
    yg = p3m.enter_context(nc.sbuf_tensor("yg", [128, DT, GT], BF16))
    r2g = p3m.enter_context(nc.sbuf_tensor("r2g", [128, DT, GT], F32))
    sqb = p3m.enter_context(nc.sbuf_tensor("sqb", [128, 4, GT], BF16))
    st3 = p3m.enter_context(nc.sbuf_tensor("st3", [1, 6, GT], F32))
    ost = p3m.enter_context(nc.sbuf_tensor("ost", [128, 4, D], F32))
    lnb = p3m.enter_context(nc.sbuf_tensor("lnb", [128, 2, GT], BF16))

    @J("sync")
    def _(sync):
        for g in range(NG):
            if g >= 1:
                W(sync, "GRP", NG + g)
            for ib in range(IT):
                sync.dma_start(h_g[:, ib, :], hs[ib * 128:(ib + 1) * 128, g * GT:(g + 1) * GT]).then_inc(sems["LQ"], 16)
                cnt["LQ"] += 16
            for k in range(DT):
                sync.dma_start(ln1g[:, k, :], ln1s[k * 128:(k + 1) * 128, g * GT:(g + 1) * GT]).then_inc(sems["LQ"], 16)
                cnt["LQ"] += 16
            marks[("hld", g)] = cnt["LQ"]
            for ot in range(DT):
                if ot >= 2:
                    WM(sync, "ST", ("ymm", g, ot - 2))
                INC("GQ" if ot % 2 == 0 else "GQ2",
                    sync.dma_start(wdt[:, ot % 2, :], wdTs[ot * 128:(ot + 1) * 128, :]), 16, ("wdldR", g, ot))

    for g in range(NG):
        @J("tensor")
        def _(tensor, g=g):
            WM(tensor, "LQ", ("hld", g))
            for ot in range(DT):
                WM(tensor, "GQ" if ot % 2 == 0 else "GQ2", ("wdldR", g, ot))
                psm = ps3_G if ot % 2 == 0 else ps3_G2
                if ot >= 2:
                    WM(tensor, "SV", ("yev", g, ot - 2))
                    WM(tensor, "SS", ("ysq", g, ot - 2))
                for ib in range(IT):
                    inst = tensor.matmul(psm[:, :], wdt[:, ot % 2, ib * 128:(ib + 1) * 128],
                                         h_g[:, ib, :], start=(ib == 0), stop=(ib == IT - 1))
                INC("ST", inst, 1, ("ymm", g, ot))
                if ot >= 2:
                    WM(tensor, "SS", ("ysq", g, ot - 2))
                    INC("ST", tensor.matmul(ps3_U[0:1, 0:512], onescol[:], sqb[:, ot % 2, :],
                                            start=(ot - 2 == 0), stop=False), 1, ("rsmm", g, ot - 2))
            for ot in range(DT - 2, DT):
                WM(tensor, "SS", ("ysq", g, ot))
                INC("ST", tensor.matmul(ps3_U[0:1, 0:512], onescol[:], sqb[:, ot % 2, :],
                                        start=False, stop=(ot == DT - 1)), 1, ("rsmm", g, ot))
            WM(tensor, "SV", ("rmsfin", g))
            INC("ST", tensor.matmul(ps3_U2[:, 0:512], onesrow[:, :], st3[0:1, 3, :], start=True, stop=True),
                1, ("rmsbc", g))
            for ot in range(DT):
                WM(tensor, "SV", ("r2b", g, ot))
                INC("ST", tensor.matmul(ps3_G[0:1, 0:512], onescol[:], sqb[:, ot % 2, :],
                                        start=(ot == 0), stop=(ot == DT - 1)), 1, ("l2smm", g, ot))
                if ot >= 2:
                    WM(tensor, "SS", ("r2sq", g, ot - 2))
                    INC("ST", tensor.matmul(ps3_G2[0:1, 0:512], onescol[:], sqb[:, 2 + ot % 2, :],
                                            start=(ot - 2 == 0), stop=False), 1, ("l2qmm", g, ot - 2))
            for ot in range(DT - 2, DT):
                WM(tensor, "SS", ("r2sq", g, ot))
                INC("ST", tensor.matmul(ps3_G2[0:1, 0:512], onescol[:], sqb[:, 2 + ot % 2, :],
                                        start=False, stop=(ot == DT - 1)), 1, ("l2qmm", g, ot))
            WM(tensor, "SV", ("l2fin", g))
            tensor.matmul(ps3_U[:, 0:512], onesrow[:, :], st3[0:1, 0, :], start=True, stop=True)
            INC("ST", tensor.matmul(ps3_U2[:, 0:512], onesrow[:, :], st3[0:1, 3, :], start=True, stop=True),
                1, ("l2bc", g))
            for ot in range(DT):
                WM(tensor, "SV", ("lnb", g, ot))
                if ot >= 2:
                    WM(tensor, "SS", ("oev", g, ot - 2))
                pst = ps3_T if ot % 2 == 0 else ps3_T2
                for j in range(4):
                    inst = tensor.transpose(pst[:, j * 128:(j + 1) * 128],
                                            lnb[:, ot % 2, j * 128:(j + 1) * 128], ident[:])
                INC("ST", inst, 1, ("otr", g, ot))

        @J("vector")
        def _(vector, g=g):
            for ot in range(DT):
                WM(vector, "ST", ("ymm", g, ot))
                psm = ps3_G if ot % 2 == 0 else ps3_G2
                INC("SV", vector.tensor_copy(yg[:, ot, :], psm[:, :]), 1, ("yev", g, ot))
            WM(vector, "ST", ("rsmm", g, DT - 1))
            INC("SV", vector.tensor_scalar(st3[0:1, 0, :], ps3_U[0:1, 0:512], 1.0 / D, None, ALU.mult),
                1, ("rmse", g))
            WM(vector, "SS", ("rmsq", g))
            INC("SV", vector.reciprocal(st3[0:1, 3, :], st3[0:1, 1, :]), 1, ("rmsfin", g))
            for ot in range(DT):
                WM(vector, "ST", ("rmsbc", g))
                if ot >= 2:
                    WM(vector, "ST", ("l2smm", g, ot - 2))
                    WM(vector, "SS", ("r2sq", g, ot - 2))
                vector.tensor_tensor(r2g[:, ot, :], yg[:, ot, :], ps3_U2[:, 0:512], ALU.mult)
                vector.tensor_tensor(r2g[:, ot, :], r2g[:, ot, :], ln1g[:, ot, :], ALU.add)
                INC("SV", vector.tensor_copy(sqb[:, ot % 2, :], r2g[:, ot, :]), 1, ("r2b", g, ot))
            WM(vector, "ST", ("l2smm", g, DT - 1))
            WM(vector, "ST", ("l2qmm", g, DT - 1))
            vector.tensor_scalar(st3[0:1, 0, :], ps3_G[0:1, 0:512], 1.0 / D, None, ALU.mult)
            vector.tensor_scalar(st3[0:1, 1, :], ps3_G2[0:1, 0:512], 1.0 / D, None, ALU.mult)
            vector.tensor_tensor(st3[0:1, 2, :], st3[0:1, 0, :], st3[0:1, 0, :], ALU.mult)
            INC("SV", vector.tensor_tensor(st3[0:1, 1, :], st3[0:1, 1, :], st3[0:1, 2, :], ALU.subtract),
                1, ("l2var", g))
            WM(vector, "SS", ("l2sq", g))
            vector.reciprocal(st3[0:1, 3, :], st3[0:1, 4, :])
            INC("SV", vector.tensor_tensor(st3[0:1, 5, :], st3[0:1, 0, :], st3[0:1, 3, :], ALU.mult),
                1, ("l2fin", g))
            for ot in range(DT):
                WM(vector, "ST", ("l2bc", g))
                if ot >= 2:
                    WM(vector, "ST", ("otr", g, ot - 2))
                vector.tensor_tensor(lnb[:, ot % 2, :], r2g[:, ot, :], ps3_U[:, 0:512], ALU.subtract)
                INC("SV", vector.tensor_tensor(lnb[:, ot % 2, :], lnb[:, ot % 2, :], ps3_U2[:, 0:512], ALU.mult),
                    1, ("lnb", g, ot))

        @J("scalar")
        def _(scalar, g=g):
            for ot in range(DT):
                WM(scalar, "SV", ("yev", g, ot))
                if DEBUG:
                    INC("SQ", scalar.dma_start(ydd[ot * 128:(ot + 1) * 128, g * GT:(g + 1) * GT],
                                               yg[:, ot, :]), 16)
                if ot >= 2:
                    WM(scalar, "ST", ("rsmm", g, ot - 2))
                INC("SS", scalar.activation(sqb[:, ot % 2, :], yg[:, ot, :], AF.Square,
                                            bias=c_ap(0.0), scale=c_ap(1.0)), 1, ("ysq", g, ot))
            WM(scalar, "SV", ("rmse", g))
            INC("SS", scalar.activation(st3[0:1, 1, :], st3[0:1, 0, :], AF.Sqrt,
                                        bias=c_ap1(1e-6), scale=c_ap1(1.0)), 1, ("rmsq", g))
            for ot in range(DT):
                WM(scalar, "SV", ("r2b", g, ot))
                if DEBUG:
                    INC("SQ", scalar.dma_start(r2d[ot * 128:(ot + 1) * 128, g * GT:(g + 1) * GT],
                                               r2g[:, ot, :]), 16)
                if ot >= 2:
                    WM(scalar, "ST", ("l2qmm", g, ot - 2))
                INC("SS", scalar.activation(sqb[:, 2 + ot % 2, :], sqb[:, ot % 2, :], AF.Square,
                                            bias=c_ap(0.0), scale=c_ap(1.0)), 1, ("r2sq", g, ot))
            WM(scalar, "SV", ("l2var", g))
            INC("SS", scalar.activation(st3[0:1, 4, :], st3[0:1, 1, :], AF.Sqrt,
                                        bias=c_ap1(1e-5), scale=c_ap1(1.0)), 1, ("l2sq", g))
            if DEBUG:
                WM(scalar, "SV", ("l2fin", g))
                INC("SQ", scalar.dma_start(std[0:1, g, :, :], st3[0:1, :, :]), 16)
            for ot in range(DT):
                WM(scalar, "ST", ("otr", g, ot))
                pst = ps3_T if ot % 2 == 0 else ps3_T2
                for j in range(4):
                    INC("SS", scalar.copy(ost[:, j, ot * 128:(ot + 1) * 128], pst[:, j * 128:(j + 1) * 128]),
                        1, ("oev", g, ot) if j == 3 else None)
            WM(scalar, "SS", ("oev", g, DT - 1))
            for j in range(4):
                INC("SQ", scalar.dma_start(out[(g * 4 + j) * 128:(g * 4 + j + 1) * 128, :], ost[:, j, :]),
                    16, ("odma", g, j))

        @J("gpsimd")
        def _(gp, g=g):
            WM(gp, "SS", ("oev", g, DT - 1))
            WM(gp, "SQ", ("odma", g, 3))
            gp.sem_inc(sems["GRP"], 1)
            cnt["GRP"] += 1

    @J("scalar")
    def _(scalar):
        W(scalar, "SQ", cnt["SQ"])
    run_jobs()
    p3m.close()
    p3.close()

    blk_cm.__exit__(None, None, None)
    es.close()
    LAST_MARKS = dict(marks)
    return nc


def _host_consts():
    s = _slopes(H)
    idx = np.arange(CH, dtype=np.float32)
    ident = np.eye(128, dtype=BF)
    idxloc = np.broadcast_to(idx[None, :], (128, 128)).astype(np.float32).copy()
    quart = np.full((128, 128), 0.25, dtype=np.float32)
    diagTm = np.zeros((128, H * 128), dtype=np.float32)
    qdecv = np.zeros((128, H * 128), dtype=np.float32)
    kdecppv = np.zeros((128, H), dtype=np.float32)
    diffm = idx[:, None] - idx[None, :]
    for hh in range(H):
        dd = np.where(diffm >= 0, np.exp(-s[hh] * diffm), 0.0)
        diagTm[:, hh * 128:(hh + 1) * 128] = dd.T
        qdecv[:, hh * 128:(hh + 1) * 128] = np.exp(-s[hh] * (idx + 1.0))[None, :]
        kdecppv[:, hh] = np.exp(-s[hh] * (CH - 1.0 - idx))
    cvals = np.zeros((128, 8), dtype=np.float32)
    cvals[:, 1] = 1.0
    cvals[:, 2] = PI / 2
    cvals[:, 3] = 1e-5
    cvals[:, 4] = 1e-6
    onescolv = np.ones((128, 1), dtype=BF)
    onesrowv = np.ones((1, 128), dtype=np.float32)
    return dict(ident=ident, idxloc=idxloc, quart=quart,
                diagT=diagTm.astype(BF), qdec=qdecv.astype(BF),
                kdecpp=kdecppv, cvals=cvals, onescol=onescolv, onesrow=onesrowv)


def kernel(L0_attn_x, L0_attn_w_qkv, L0_attn_w_proj, L0_attn_theta,
           L0_ffn_gate_up_packed, L0_ffn_gate_up_scales,
           L0_ffn_down_packed, L0_ffn_down_scales):
    x = np.ascontiguousarray(np.asarray(L0_attn_x, dtype=np.float32))
    wq = np.ascontiguousarray(np.asarray(L0_attn_w_qkv, dtype=np.float32))
    wp = np.ascontiguousarray(np.asarray(L0_attn_w_proj, dtype=np.float32))
    th = np.asarray(L0_attn_theta, dtype=np.float32).reshape(H, E)
    theta_pp = np.ascontiguousarray(np.tile(th.T, (1, 8)))
    gup_a = np.ascontiguousarray(np.asarray(L0_ffn_gate_up_packed, dtype=np.int32))
    gus_a = np.ascontiguousarray(np.asarray(L0_ffn_gate_up_scales, dtype=np.float16).T)
    dnp_a = np.ascontiguousarray(np.asarray(L0_ffn_down_packed, dtype=np.int32))
    dns_a = np.ascontiguousarray(np.asarray(L0_ffn_down_scales, dtype=np.float16).T)
    consts = _host_consts()

    nc = build_nc()
    in_maps = []
    for c in range(8):
        in_maps.append(dict(x=x[c], wqkv=wq, wproj=wp, theta_pp=theta_pp,
                            gup=gup_a, gus=gus_a, dnp=dnp_a, dns=dns_a,
                            ident=consts["ident"], idxloc=consts["idxloc"],
                            quart=consts["quart"], diagT=consts["diagT"],
                            qdec=consts["qdec"], kdecpp=consts["kdecpp"],
                            cvals=consts["cvals"], onescol=consts["onescol"],
                            onesrow=consts["onesrow"]))
    res = run_bass_kernel_spmd(nc, in_maps, core_ids=list(range(8)))
    return np.stack([np.asarray(r["out"], dtype=np.float32) for r in res.results], axis=0)


# version 10
# speedup vs baseline: 1.1130x; 1.0075x over previous
import numpy as np
import ml_dtypes
from contextlib import ExitStack

import concourse.bass as bass
import concourse.mybir as mybir
from concourse.bass_utils import run_bass_kernel_spmd

F32 = mybir.dt.float32
BF16 = mybir.dt.bfloat16
F16 = mybir.dt.float16
I32 = mybir.dt.int32
AF = mybir.ActivationFunctionType
ALU = mybir.AluOpType

T = 4096
D = 2048
H = 16
E = 128
CH = 128
NCH = 32
NG = 8
GT = 512
DT = 16
IF = 5632
IT = 44
TWO_PI = float(2.0 * np.pi)
PI = float(np.pi)
BF = ml_dtypes.bfloat16

DEBUG = False
LAST_MARKS = {}
PHASES = 9
RACE_DET = True
P1SUB = 3


def _slopes(h):
    start = 2.0 ** (-2.0 ** (-(np.log2(h) - 3)))
    return np.array([start ** (i + 1) for i in range(h)], dtype=np.float32)


class _DummyInst:
    def then_inc(self, sem, k):
        return self


class _DryEng:
    def __getattr__(self, name):
        def f(*a, **kw):
            return _DummyInst()
        return f


class Marks(dict):
    dry = True

    def __missing__(self, key):
        if self.dry:
            return 0
        raise KeyError(key)


def build_nc():
    global LAST_MARKS
    nc = bass.Bass("TRN2", target_bir_lowering=False, detect_race_conditions=RACE_DET)

    x_in = nc.dram_tensor("x", [T, D], F32, kind="ExternalInput")
    wqkv = nc.dram_tensor("wqkv", [3 * D, D], F32, kind="ExternalInput")
    wproj = nc.dram_tensor("wproj", [D, D], F32, kind="ExternalInput")
    theta_pp_in = nc.dram_tensor("theta_pp", [128, 128], F32, kind="ExternalInput")
    gup = nc.dram_tensor("gup", [2 * IF, D // 8], I32, kind="ExternalInput")
    gus = nc.dram_tensor("gus", [2 * IF, DT], F16, kind="ExternalInput")
    dnp = nc.dram_tensor("dnp", [D, IF // 8], I32, kind="ExternalInput")
    dns = nc.dram_tensor("dns", [D, IT], F16, kind="ExternalInput")
    ident_in = nc.dram_tensor("ident", [128, 128], BF16, kind="ExternalInput")
    idxloc_in = nc.dram_tensor("idxloc", [128, 128], F32, kind="ExternalInput")
    quart_in = nc.dram_tensor("quart", [128, 128], F32, kind="ExternalInput")
    diag_in = nc.dram_tensor("diagT", [128, H * 128], BF16, kind="ExternalInput")
    qdec_in = nc.dram_tensor("qdec", [128, H * 128], BF16, kind="ExternalInput")
    kdec_in = nc.dram_tensor("kdecpp", [128, H], F32, kind="ExternalInput")
    cv_in = nc.dram_tensor("cvals", [128, 8], F32, kind="ExternalInput")
    onesc_in = nc.dram_tensor("onescol", [128, 1], BF16, kind="ExternalInput")
    onesr_in = nc.dram_tensor("onesrow", [1, 128], F32, kind="ExternalInput")

    out = nc.dram_tensor("out", [T, D], F32, kind="ExternalOutput")
    skind = "ExternalOutput" if DEBUG else "Internal"
    xTs = nc.dram_tensor("xTs", [D, T], BF16, kind=skind)
    qs = nc.dram_tensor("qs", [D, T], BF16, kind=skind)
    ks = nc.dram_tensor("ks", [D, T], BF16, kind=skind)
    vs = nc.dram_tensor("vs", [T, D], BF16, kind=skind)
    ln1s = nc.dram_tensor("ln1s", [D, T], BF16, kind=skind)
    hs = nc.dram_tensor("hs", [IF, T], BF16, kind=skind)
    ysd = nc.dram_tensor("ysd", [D, T], BF16, kind=skind)
    wgd = nc.dram_tensor("wgd", [2 * IF, D], BF16, kind=skind)
    ydd = nc.dram_tensor("ydd", [D, T], BF16, kind=skind)
    wdd = nc.dram_tensor("wdd", [DT * 128, IF], BF16, kind=skind)
    r2d = nc.dram_tensor("r2d", [D, T], F32, kind=skind)
    std = nc.dram_tensor("std", [1, NG, 6, GT], F32, kind=skind)
    tdmp = nc.dram_tensor("tdmp", [128, 2, H * 128], BF16, kind=skind)
    cdmp = nc.dram_tensor("cdmp", [128, 2, H], F32, kind=skind)
    ldmp = nc.dram_tensor("ldmp", [128, 6, 128], BF16, kind=skind)
    sdmp = nc.dram_tensor("sdmp", [128, 6, 128], BF16, kind=skind)
    wdTs = nc.dram_tensor("wdTs", [DT * 128, IF], BF16, kind="Internal")

    slopes = _slopes(H)
    bdec = [float(np.exp(-slopes[hh] * CH)) for hh in range(H)]

    es = ExitStack()
    ident = es.enter_context(nc.sbuf_tensor("identS", [128, 128], BF16))
    idxloc = es.enter_context(nc.sbuf_tensor("idxlocS", [128, 128], F32))
    diagT = es.enter_context(nc.sbuf_tensor("diagS", [128, H * 128], BF16))
    qdec = es.enter_context(nc.sbuf_tensor("qdecS", [128, H * 128], BF16))
    kdecpp = es.enter_context(nc.sbuf_tensor("kdecS", [128, H], F32))
    theta_pp = es.enter_context(nc.sbuf_tensor("thetaS", [128, 128], F32))
    cv = es.enter_context(nc.sbuf_tensor("cvS", [128, 8], F32))
    onescol = es.enter_context(nc.sbuf_tensor("onescolS", [128, 1], BF16))
    onesrow = es.enter_context(nc.sbuf_tensor("onesrowS", [1, 128], F32))

    CMAP = {0.0: 0, 1.0: 1, PI / 2: 2, 1e-5: 3, 1e-6: 4}
    for v, i in CMAP.items():
        nc.const_aps.register(F32, v, cv[:, i:i + 1])

    def c_ap(v):
        return cv[:, CMAP[v]:CMAP[v] + 1]

    def c_ap1(v):
        return cv[0:1, CMAP[v]:CMAP[v] + 1]

    sems = {}
    for name in ["LQ", "GQ", "GQ2", "SQ", "SV", "SS", "SP", "ST", "PH", "GRP"]:
        sems[name] = es.enter_context(nc.semaphore(name))

    blk_cm = nc.Block()
    blk = blk_cm.__enter__()

    jobs = {"sync": [], "vector": [], "scalar": [], "tensor": [], "gpsimd": []}
    marks = Marks()
    cnt = {k: 0 for k in sems}

    def J(eng):
        def deco(fn):
            jobs[eng].append(fn)
            return fn
        return deco

    def INC(sem, inst, k=1, mark=None):
        inst.then_inc(sems[sem], k)
        cnt[sem] += k
        if mark is not None:
            marks[mark] = cnt[sem]
        return cnt[sem]

    def W(eng, sem, val):
        eng.wait_ge(sems[sem], val)

    def WM(eng, sem, mark):
        eng.wait_ge(sems[sem], marks[mark])

    ENGS = ["sync", "vector", "scalar", "tensor", "gpsimd"]

    def run_jobs():
        saved = dict(cnt)
        Marks.dry = True
        for name in ENGS:
            eng = _DryEng()
            for fn in jobs[name]:
                fn(eng)
        for k in cnt:
            cnt[k] = saved[k]
        Marks.dry = False
        for name in ENGS:
            sub = list(jobs[name])

            def runner(fns):
                def _(eng):
                    for fn in fns:
                        fn(eng)
                return _
            getattr(blk, name)(runner(sub))
        for name in jobs:
            jobs[name] = []

    def phase_barrier(tag):
        target = cnt["PH"] + 5

        @J("scalar")
        def _(eng):
            W(eng, "SQ", cnt["SQ"])
            eng.drain()
            INC("PH", eng.nop())
            W(eng, "PH", target)

        @J("gpsimd")
        def _(eng):
            W(eng, "GQ", cnt["GQ"])
            W(eng, "GQ2", cnt["GQ2"])
            eng.drain()
            INC("PH", eng.nop())
            W(eng, "PH", target)
        for name in ["sync", "vector", "tensor"]:
            @J(name)
            def _(eng, name=name):
                eng.drain()
                INC("PH", eng.nop())
                W(eng, "PH", target)
        run_jobs()
        assert cnt["PH"] == target, (tag, cnt["PH"], target)

    p1 = ExitStack()
    xT = p1.enter_context(nc.sbuf_tensor("xT", [128, DT, T], BF16))
    xb = p1.enter_context(nc.sbuf_tensor("xb", [128, 1, D], BF16))
    xf = p1.enter_context(nc.sbuf_tensor("xf", [128, 1, D], F32))
    wb = p1.enter_context(nc.sbuf_tensor("wb", [128, 2, D], BF16))
    wf = p1.enter_context(nc.sbuf_tensor("wf", [128, 2, D], F32))
    wt = p1.enter_context(nc.sbuf_tensor("wt", [128, 2, DT * 128], BF16))
    wtv = p1.enter_context(nc.sbuf_tensor("wtv", [128, DT, 512], BF16))
    stg = p1.enter_context(nc.sbuf_tensor("stg", [128, 2, 512], BF16))
    ps_t = p1.enter_context(nc.psum_tensor("ps_t", [128, 1024], BF16))
    ps_t2 = p1.enter_context(nc.psum_tensor("ps_t2", [128, 1024], BF16))
    ps_m = p1.enter_context(nc.psum_tensor("ps_m", [128, 512], F32))
    ps_m2 = p1.enter_context(nc.psum_tensor("ps_m2", [128, 512], F32))

    @J("sync")
    def _(sync):
        for t_, s_ in [(ident, ident_in), (idxloc, idxloc_in),
                       (diagT, diag_in), (qdec, qdec_in), (kdecpp, kdec_in),
                       (theta_pp, theta_pp_in), (cv, cv_in), (onescol, onesc_in),
                       (onesrow, onesr_in)]:
            INC("LQ", sync.dma_start(t_[:], s_[:, :]), 16)
        marks["consts"] = cnt["LQ"]

    @J("sync")
    def _(sync):
        for tt in range(NCH):
            if tt >= 1:
                WM(sync, "SV", ("xcast", tt - 1))
            INC("GQ", sync.dma_start(xf[:, 0, :], x_in[tt * 128:(tt + 1) * 128, :]), 16,
                ("xldf", tt))
        for o in range(32 if P1SUB >= 2 else 0):
            if o >= 2:
                WM(sync, "SV", ("wcast", o - 2))
            INC("GQ2", sync.dma_start(wf[:, o % 2, :], wqkv[o * 128:(o + 1) * 128, :]), 16,
                ("wldf", o))
        for vsl in range(4 if P1SUB >= 3 else 0):
            for oo in range(4):
                o = 4 * vsl + oo
                if o >= 2:
                    WM(sync, "SV", ("vwcast", o - 2))
                elif P1SUB >= 2:
                    WM(sync, "SV", ("wcast", 30 + o))
                INC("GQ", sync.dma_start(wf[:, o % 2, :], wqkv[(32 + o) * 128:(33 + o) * 128, :]), 16,
                    ("vwldf", o))

    @J("vector")
    def _(vector):
        for tt in range(NCH):
            WM(vector, "GQ", ("xldf", tt))
            if tt >= 1:
                WM(vector, "ST", ("xtr2", tt - 1))
            INC("SV", vector.tensor_copy(xb[:, 0, :], xf[:, 0, :]), 1, ("xcast", tt))
        for o in range(32 if P1SUB >= 2 else 0):
            WM(vector, "GQ2", ("wldf", min(o + 1, 31)))
            if o >= 2:
                WM(vector, "ST", ("wtr2", o - 2))
            INC("SV", vector.tensor_copy(wb[:, o % 2, :], wf[:, o % 2, :]), 1, ("wcast", o))
        for vsl in range(4 if P1SUB >= 3 else 0):
            for oo in range(4):
                o = 4 * vsl + oo
                WM(vector, "GQ", ("vwldf", min(o + 1, 15)))
                if o >= 2:
                    WM(vector, "ST", ("vwtr2", o - 2))
                elif P1SUB >= 2:
                    WM(vector, "ST", ("wtr2", 30 + o))
                INC("SV", vector.tensor_copy(wb[:, o % 2, :], wf[:, o % 2, :]), 1, ("vwcast", o))

    @J("tensor")
    def _(tensor):
        WM(tensor, "LQ", "consts")
        for tt in range(NCH):
            WM(tensor, "SV", ("xcast", tt))
            if tt >= 1:
                WM(tensor, "SS", ("xev1", tt - 1))
            for k in range(8):
                inst = tensor.transpose(ps_t[:, k * 128:(k + 1) * 128],
                                        xb[:, 0, k * 128:(k + 1) * 128], ident[:])
            INC("ST", inst, 1, ("xtr1", tt))
            if tt >= 1:
                WM(tensor, "SS", ("xev2", tt - 1))
            for k in range(8):
                inst = tensor.transpose(ps_t2[:, k * 128:(k + 1) * 128],
                                        xb[:, 0, (8 + k) * 128:(9 + k) * 128], ident[:])
            INC("ST", inst, 1, ("xtr2", tt))
        for o in range(32 if P1SUB >= 2 else 0):
            WM(tensor, "SV", ("wcast", o))
            WM(tensor, "SS", ("xev2", NCH - 1) if o == 0 else ("wev1", o - 1))
            for k in range(8):
                inst = tensor.transpose(ps_t[:, k * 128:(k + 1) * 128],
                                        wb[:, o % 2, k * 128:(k + 1) * 128], ident[:])
            INC("ST", inst, 1, ("wtr1", o))
            if o >= 1:
                WM(tensor, "SS", ("wev2", o - 1))
            for k in range(8):
                inst = tensor.transpose(ps_t2[:, k * 128:(k + 1) * 128],
                                        wb[:, o % 2, (8 + k) * 128:(9 + k) * 128], ident[:])
            INC("ST", inst, 1, ("wtr2", o))
            WM(tensor, "SS", ("wev2", o))
            for tc in range(8):
                psm = ps_m if tc % 2 == 0 else ps_m2
                if tc >= 2:
                    WM(tensor, "SS", ("qkev", o, tc - 2))
                elif o >= 1:
                    WM(tensor, "SS", ("qkev", o - 1, 6 + tc))
                for k in range(DT):
                    inst = tensor.matmul(psm[:, :], wt[:, o % 2, k * 128:(k + 1) * 128],
                                         xT[:, k, tc * 512:(tc + 1) * 512],
                                         start=(k == 0), stop=(k == DT - 1))
                INC("ST", inst, 1, ("qkmm", o, tc))
        ci = 0
        for vsl in range(4 if P1SUB >= 3 else 0):
            for oo in range(4):
                o = 4 * vsl + oo
                WM(tensor, "SV", ("vwcast", o))
                WM(tensor, "SS", ("wev2", 31) if o == 0 else ("vwev1", o - 1))
                if vsl >= 1:
                    WM(tensor, "SS", ("vev", 32 * vsl - 1))
                for k in range(8):
                    inst = tensor.transpose(ps_t[:, k * 128:(k + 1) * 128],
                                            wb[:, o % 2, k * 128:(k + 1) * 128], ident[:])
                INC("ST", inst, 1, ("vwtr1", o))
                if o >= 1:
                    WM(tensor, "SS", ("vwev2", o - 1))
                for k in range(8):
                    inst = tensor.transpose(ps_t2[:, k * 128:(k + 1) * 128],
                                            wb[:, o % 2, (8 + k) * 128:(9 + k) * 128], ident[:])
                INC("ST", inst, 1, ("vwtr2", o))
            WM(tensor, "SS", ("vwev2", 4 * vsl + 3))
            for tt in range(NCH):
                psm = ps_m if ci % 2 == 0 else ps_m2
                if ci >= 2:
                    WM(tensor, "SS", ("vev", ci - 2))
                elif vsl == 0 and ci < 2:
                    WM(tensor, "SS", ("qkev", 31, 6 + ci))
                for k in range(DT):
                    inst = tensor.matmul(psm[:, :], xT[:, k, tt * 128:(tt + 1) * 128],
                                         wtv[:, k, :], start=(k == 0), stop=(k == DT - 1))
                INC("ST", inst, 1, ("vmm", ci))
                ci += 1

    @J("scalar")
    def _(scalar):
        for tt in range(NCH):
            WM(scalar, "ST", ("xtr1", tt))
            for k in range(8):
                INC("SS", scalar.copy(xT[:, k, tt * 128:(tt + 1) * 128], ps_t[:, k * 128:(k + 1) * 128]),
                    1, ("xev1", tt) if k == 7 else None)
            WM(scalar, "ST", ("xtr2", tt))
            for k in range(8):
                INC("SS", scalar.copy(xT[:, 8 + k, tt * 128:(tt + 1) * 128], ps_t2[:, k * 128:(k + 1) * 128]),
                    1, ("xev2", tt) if k == 7 else None)
        WM(scalar, "SS", ("xev2", NCH - 1))
        for k in range(DT):
            INC("SQ", scalar.dma_start(xTs[k * 128:(k + 1) * 128, :], xT[:, k, :]), 16)
        for o in range(32 if P1SUB >= 2 else 0):
            WM(scalar, "ST", ("wtr1", o))
            for k in range(8):
                INC("SS", scalar.copy(wt[:, o % 2, k * 128:(k + 1) * 128], ps_t[:, k * 128:(k + 1) * 128]),
                    1, ("wev1", o) if k == 7 else None)
            WM(scalar, "ST", ("wtr2", o))
            for k in range(8):
                INC("SS", scalar.copy(wt[:, o % 2, (8 + k) * 128:(9 + k) * 128], ps_t2[:, k * 128:(k + 1) * 128]),
                    1, ("wev2", o) if k == 7 else None)
            dst = qs if o < DT else ks
            row0 = (o % DT) * 128
            for tc in range(8):
                WM(scalar, "ST", ("qkmm", o, tc))
                if not (o == 0 and tc == 0):
                    WM(scalar, "SQ", ("qkdma", o, tc - 1) if tc >= 1 else ("qkdma", o - 1, 7))
                psm = ps_m if tc % 2 == 0 else ps_m2
                INC("SS", scalar.copy(stg[:, tc % 2, :], psm[:, :]), 1, ("qkev", o, tc))
                WM(scalar, "SS", ("qkev", o, tc))
                INC("SQ", scalar.dma_start(dst[row0:row0 + 128, tc * 512:(tc + 1) * 512],
                                           stg[:, tc % 2, :]), 16, ("qkdma", o, tc))
        ci = 0
        for vsl in range(4 if P1SUB >= 3 else 0):
            for oo in range(4):
                o = 4 * vsl + oo
                WM(scalar, "ST", ("vwtr1", o))
                for k in range(8):
                    INC("SS", scalar.copy(wtv[:, k, oo * 128:(oo + 1) * 128], ps_t[:, k * 128:(k + 1) * 128]),
                        1, ("vwev1", o) if k == 7 else None)
                WM(scalar, "ST", ("vwtr2", o))
                for k in range(8):
                    INC("SS", scalar.copy(wtv[:, 8 + k, oo * 128:(oo + 1) * 128], ps_t2[:, k * 128:(k + 1) * 128]),
                        1, ("vwev2", o) if k == 7 else None)
            for tt in range(NCH):
                WM(scalar, "ST", ("vmm", ci))
                if ci >= 1:
                    WM(scalar, "SQ", ("vdma", ci - 1))
                elif P1SUB >= 2:
                    WM(scalar, "SQ", ("qkdma", 31, 7))
                psm = ps_m if ci % 2 == 0 else ps_m2
                INC("SS", scalar.copy(stg[:, ci % 2, :], psm[:, :]), 1, ("vev", ci))
                WM(scalar, "SS", ("vev", ci))
                INC("SQ", scalar.dma_start(vs[tt * 128:(tt + 1) * 128, vsl * 512:(vsl + 1) * 512],
                                           stg[:, ci % 2, :]), 16, ("vdma", ci))
                ci += 1

    phase_barrier("p1")
    p1.close()
    if PHASES <= 1:
        blk_cm.__exit__(None, None, None)
        es.close()
        global LAST_MARKS
        LAST_MARKS = dict(marks)
        return nc

    p2 = ExitStack()
    wpt = p2.enter_context(nc.sbuf_tensor("wpt", [128, DT, D], BF16))
    r_g = p2.enter_context(nc.sbuf_tensor("r_g", [128, DT, GT], BF16))
    st32 = p2.enter_context(nc.sbuf_tensor("st32", [128, 2 * H, 128], F32))
    stbf = p2.enter_context(nc.sbuf_tensor("stbf", [128, 2 * H, 128], BF16))
    cil = p2.enter_context(nc.sbuf_tensor("cil", [128, H, 128], BF16))
    silt = p2.enter_context(nc.sbuf_tensor("silt", [128, H, 128], BF16))


    ps_T = p2.enter_context(nc.psum_tensor("p2_T", [128, 1024], BF16))
    ps_T2 = p2.enter_context(nc.psum_tensor("p2_Tb", [128, 1024], BF16))
    ps_S = p2.enter_context(nc.psum_tensor("p2_S", [128, 512], F32))
    ps_O = p2.enter_context(nc.psum_tensor("p2_O", [128, 512], F32))
    ps_U = p2.enter_context(nc.psum_tensor("p2_U", [128, 512], F32))
    ps_P = p2.enter_context(nc.psum_tensor("p2_P", [128, 512], F32))
    ps_st = p2.enter_context(nc.psum_tensor("p2_st", [128, 512], F32))
    ps_bc = p2.enter_context(nc.psum_tensor("p2_bc", [128, 512], F32))

    ploc = ExitStack()
    loc = ploc.enter_context(nc.sbuf_tensor("loc", [128, 4, 128], F32))
    loci = ploc.enter_context(nc.sbuf_tensor("loci", [128, 2, 128], I32))
    wbp = ploc.enter_context(nc.sbuf_tensor("wbp", [128, 1, D], BF16))
    wfp = ploc.enter_context(nc.sbuf_tensor("wfp", [128, 1, D], F32))

    @J("sync")
    def _(sync):
        for o in range(DT):
            if o >= 1:
                WM(sync, "SV", ("pcast", o - 1))
            INC("GQ2", sync.dma_start(wfp[:, 0, :], wproj[o * 128:(o + 1) * 128, :]), 16,
                ("pldf", o))

    @J("vector")
    def _(vector):
        for o in range(DT):
            WM(vector, "GQ2", ("pldf", o))
            if o >= 1:
                WM(vector, "ST", ("ptr2", o - 1))
            INC("SV", vector.tensor_copy(wbp[:, 0, :], wfp[:, 0, :]), 1, ("pcast", o))

    @J("vector")
    def _(vector):
        WM(vector, "LQ", "consts")
        for hh in range(H):
            if hh >= 1:
                WM(vector, "SS", ("loct", hh - 1))
            vector.tensor_scalar(loc[:, 0, :], idxloc[:], theta_pp[:, hh:hh + 1], None, ALU.mult)
            vector.tensor_scalar(loc[:, 1, :], loc[:, 0, :], 1.0 / TWO_PI, None, ALU.mult)
            vector.tensor_copy(loci[:, 0, :], loc[:, 1, :])
            vector.tensor_scalar(loc[:, 2, :], loc[:, 0, :], 1.0 / TWO_PI, None, ALU.mult)
            vector.tensor_scalar(loc[:, 2, :], loc[:, 2, :], 0.25, None, ALU.add)
            INC("SV", vector.tensor_copy(loci[:, 1, :], loc[:, 2, :]), 1, ("loci", hh))
            WM(vector, "SS", ("locf", hh))
            vector.scalar_tensor_tensor(loc[:, 3, :], loc[:, 1, :], -TWO_PI, loc[:, 0, :], ALU.mult, ALU.add)
            vector.scalar_tensor_tensor(loc[:, 2, :], loc[:, 2, :], -TWO_PI, loc[:, 0, :], ALU.mult, ALU.add)
            vector.tensor_scalar(loc[:, 1, :], loc[:, 3, :], PI, None, ALU.is_gt)
            vector.scalar_tensor_tensor(loc[:, 3, :], loc[:, 1, :], -TWO_PI, loc[:, 3, :], ALU.mult, ALU.add)
            vector.tensor_scalar(loc[:, 1, :], loc[:, 2, :], PI / 2, None, ALU.is_gt)
            INC("SV", vector.scalar_tensor_tensor(loc[:, 2, :], loc[:, 1, :], -TWO_PI, loc[:, 2, :], ALU.mult, ALU.add),
                1, ("locm", hh))
        vector.memset(st32[:], 0.0)
        INC("SV", vector.memset(stbf[:], 0.0), 1, ("stinit",))

    @J("tensor")
    def _(tensor):
        for o in range(DT):
            WM(tensor, "SV", ("pcast", o))
            if o >= 1:
                WM(tensor, "SS", ("pev2", o - 1))
            for k in range(8):
                inst = tensor.transpose(ps_T[:, k * 128:(k + 1) * 128],
                                        wbp[:, 0, k * 128:(k + 1) * 128], ident[:])
            INC("ST", inst, 1, ("ptr1", o))
            WM(tensor, "SS", ("pev1", o))
            for k in range(8):
                inst = tensor.transpose(ps_T[:, k * 128:(k + 1) * 128],
                                        wbp[:, 0, (8 + k) * 128:(9 + k) * 128], ident[:])
            INC("ST", inst, 1, ("ptr2", o))

    @J("scalar")
    def _(scalar):
        for o in range(DT):
            WM(scalar, "ST", ("ptr1", o))
            for k in range(8):
                INC("SS", scalar.copy(wpt[:, k, o * 128:(o + 1) * 128], ps_T[:, k * 128:(k + 1) * 128]),
                    1, ("pev1", o) if k == 7 else None)
            WM(scalar, "ST", ("ptr2", o))
            for k in range(8):
                INC("SS", scalar.copy(wpt[:, 8 + k, o * 128:(o + 1) * 128], ps_T[:, k * 128:(k + 1) * 128]),
                    1, ("pev2", o) if k == 7 else None)
        for hh in range(H):
            WM(scalar, "SV", ("loci", hh))
            scalar.copy(loc[:, 1, :], loci[:, 0, :])
            INC("SS", scalar.copy(loc[:, 2, :], loci[:, 1, :]), 1, ("locf", hh))
            WM(scalar, "SV", ("locm", hh))
            scalar.activation(silt[:, hh, :], loc[:, 3, :], AF.Sin, bias=c_ap(0.0), scale=c_ap(1.0))
            INC("SS", scalar.activation(cil[:, hh, :], loc[:, 2, :], AF.Sin, bias=c_ap(PI / 2), scale=c_ap(1.0)),
                1, ("loct", hh))

    @J("scalar")
    def _(scalar):
        if DEBUG:
            WM(scalar, "SS", ("loct", H - 1))
            INC("SQ", scalar.dma_start(tdmp[:, 0, :], cil[:, :, :]), 16)
            INC("SQ", scalar.dma_start(tdmp[:, 1, :], silt[:, :, :]), 16)

    phase_barrier("p2prolog")
    ploc.close()
    q_c = p2.enter_context(nc.sbuf_tensor("q_c", [128, 2, H, 128], BF16))
    k_c = p2.enter_context(nc.sbuf_tensor("k_c", [128, 2, H, 128], BF16))
    v_c = p2.enter_context(nc.sbuf_tensor("v_c", [128, 1, D], BF16))
    xT_g = p2.enter_context(nc.sbuf_tensor("xT_g", [128, DT, GT], BF16))
    y_g = p2.enter_context(nc.sbuf_tensor("y_g", [128, 2, H, GT], BF16))
    tabw = p2.enter_context(nc.sbuf_tensor("tabw", [128, 2, 128], F32))
    lrp = p2.enter_context(nc.sbuf_tensor("lrp", [128, 2, 6, 128], BF16))
    kT = p2.enter_context(nc.sbuf_tensor("kT", [128, 2, 2, 128], BF16))
    scb = p2.enter_context(nc.sbuf_tensor("scb", [128, 1, 128], BF16))
    c0s0 = p2.enter_context(nc.sbuf_tensor("c0s0", [128, 2, 2, 128], F32))
    rrq = p2.enter_context(nc.sbuf_tensor("rrq", [128, 3, 128], F32))
    stats = p2.enter_context(nc.sbuf_tensor("stats", [1, 4, GT], F32))
    rsq = p2.enter_context(nc.sbuf_tensor("rsq", [128, 1, GT], BF16))


    def prev_ch(C_abs, hh):
        pc = C_abs - 1
        return (pc // 4, pc % 4, hh)

    @J("sync")
    def _(sync):
        for g in range(NG):
            for c in range(4):
                C_abs = 4 * g + c
                par = C_abs % 2
                gq = "GQ" if par == 0 else "GQ2"
                if C_abs >= 2:
                    WM(sync, "SV", ("A", (C_abs - 2) // 4, (C_abs - 2) % 4, H - 1))
                    WM(sync, "ST", ("F", (C_abs - 2) // 4, (C_abs - 2) % 4, H - 1))
                for hh in range(H):
                    INC(gq, sync.dma_start(q_c[:, par, hh, :],
                                           qs[hh * 128:(hh + 1) * 128, C_abs * 128:(C_abs + 1) * 128]), 16)
                for hh in range(H - 1):
                    INC(gq, sync.dma_start(k_c[:, par, hh, :],
                                           ks[hh * 128:(hh + 1) * 128, C_abs * 128:(C_abs + 1) * 128]), 16)
                INC(gq, sync.dma_start(k_c[:, par, H - 1, :],
                                       ks[(H - 1) * 128:H * 128, C_abs * 128:(C_abs + 1) * 128]), 16,
                    ("chld", C_abs))
                if C_abs >= 1:
                    WM(sync, "ST", ("F", (C_abs - 1) // 4, (C_abs - 1) % 4, H - 1))
                INC(gq, sync.dma_start(v_c[:, 0, :], vs[C_abs * 128:(C_abs + 1) * 128, :]), 16,
                    ("vld", C_abs))
            if g >= 1:
                W(sync, "GRP", g)
            for k in range(DT):
                sync.dma_start(xT_g[:, k, :], xTs[k * 128:(k + 1) * 128, g * GT:(g + 1) * GT]).then_inc(sems["LQ"], 16)
                cnt["LQ"] += 16
            marks[("xtg", g)] = cnt["LQ"]

    def k3(n):
        C = n // H
        return (C // 4, C % 4, n % H)

    def p2_trig(Cn):
        g, c = Cn // 4, Cn % 4
        t0f = float(Cn * CH)

        @J("vector")
        def _(vector, g=g, c=c, t0f=t0f, Cn=Cn):
            if Cn >= 1:
                WM(vector, "SS", ("ts2", (Cn - 1) // 4, (Cn - 1) % 4))
            vector.tensor_scalar(rrq[:, 0, :], theta_pp[:], t0f, None, ALU.mult)
            vector.tensor_scalar(rrq[:, 1, :], rrq[:, 0, :], 1.0 / TWO_PI, None, ALU.mult)
            par = Cn % 2
            vector.tensor_copy(c0s0[:, par, 0, :].bitcast(I32), rrq[:, 1, :])
            vector.tensor_scalar(rrq[:, 2, :], rrq[:, 0, :], 1.0 / TWO_PI, None, ALU.mult)
            vector.tensor_scalar(rrq[:, 2, :], rrq[:, 2, :], 0.25, None, ALU.add)
            INC("SV", vector.tensor_copy(c0s0[:, par, 1, :].bitcast(I32), rrq[:, 2, :]), 1, ("tv1", g, c))

        @J("scalar")
        def _(scalar, g=g, c=c, Cn=Cn):
            WM(scalar, "SV", ("tv1", g, c))
            par = Cn % 2
            scalar.copy(rrq[:, 1, :], c0s0[:, par, 0, :].bitcast(I32))
            INC("SS", scalar.copy(rrq[:, 2, :], c0s0[:, par, 1, :].bitcast(I32)), 1, ("ts1", g, c))

        @J("vector")
        def _(vector, g=g, c=c):
            WM(vector, "SS", ("ts1", g, c))
            vector.scalar_tensor_tensor(rrq[:, 1, :], rrq[:, 1, :], -TWO_PI, rrq[:, 0, :], ALU.mult, ALU.add)
            vector.scalar_tensor_tensor(rrq[:, 2, :], rrq[:, 2, :], -TWO_PI, rrq[:, 0, :], ALU.mult, ALU.add)
            vector.tensor_scalar(rrq[:, 0, :], rrq[:, 1, :], PI, None, ALU.is_gt)
            vector.scalar_tensor_tensor(rrq[:, 1, :], rrq[:, 0, :], -TWO_PI, rrq[:, 1, :], ALU.mult, ALU.add)
            vector.tensor_scalar(rrq[:, 0, :], rrq[:, 2, :], PI / 2, None, ALU.is_gt)
            INC("SV", vector.scalar_tensor_tensor(rrq[:, 2, :], rrq[:, 0, :], -TWO_PI, rrq[:, 2, :], ALU.mult, ALU.add),
                1, ("tv2", g, c))

        @J("scalar")
        def _(scalar, g=g, c=c, Cn=Cn):
            WM(scalar, "SV", ("tv2", g, c))
            if Cn >= 2:
                WM(scalar, "SV", ("A",) + k3((Cn - 2) * H + H - 1))
            par = Cn % 2
            scalar.activation(c0s0[:, par, 1, :], rrq[:, 1, :], AF.Sin, bias=c_ap(0.0), scale=c_ap(1.0))
            INC("SS", scalar.activation(c0s0[:, par, 0, :], rrq[:, 2, :], AF.Sin, bias=c_ap(PI / 2), scale=c_ap(1.0)),
                1, ("ts2", g, c))

    def p2_A(n):
        g, c, h = k3(n)
        C_abs = n // H
        hp = h % 2
        cpar = C_abs % 2

        @J("vector")
        def _(vector, g=g, c=c, h=h, hp=hp, C_abs=C_abs, cpar=cpar, n=n):
            if h == 0:
                WM(vector, "GQ" if cpar == 0 else "GQ2", ("chld", C_abs))
            WM(vector, "SS", ("ts2", g, c))
            if n >= 2:
                WM(vector, "ST", ("F",) + k3(n - 2))
            vector.tensor_scalar(tabw[:, 0, :], silt[:, h, :], c0s0[:, cpar, 1, h:h + 1], None, ALU.mult)
            vector.scalar_tensor_tensor(tabw[:, 1, :], cil[:, h, :], c0s0[:, cpar, 0, h:h + 1], tabw[:, 0, :], ALU.mult, ALU.subtract)
            vector.tensor_scalar(tabw[:, 0, :], cil[:, h, :], c0s0[:, cpar, 1, h:h + 1], None, ALU.mult)
            vector.scalar_tensor_tensor(tabw[:, 0, :], silt[:, h, :], c0s0[:, cpar, 0, h:h + 1], tabw[:, 0, :], ALU.mult, ALU.add)
            qsl = q_c[:, cpar, h, :]
            ksl = k_c[:, cpar, h, :]
            vector.tensor_tensor(lrp[:, hp, 0, :], qsl, tabw[:, 1, :], ALU.mult)
            vector.tensor_tensor(lrp[:, hp, 1, :], qsl, tabw[:, 0, :], ALU.mult)
            vector.tensor_tensor(lrp[:, hp, 2, :], lrp[:, hp, 0, :], qdec[:, h * 128:(h + 1) * 128], ALU.mult)
            vector.tensor_tensor(lrp[:, hp, 3, :], lrp[:, hp, 1, :], qdec[:, h * 128:(h + 1) * 128], ALU.mult)
            vector.tensor_tensor(lrp[:, hp, 4, :], ksl, tabw[:, 1, :], ALU.mult)
            INC("SV", vector.tensor_tensor(lrp[:, hp, 5, :], ksl, tabw[:, 0, :], ALU.mult), 1, ("A", g, c, h))

    def p2_BD(n):
        g, c, h = k3(n)
        hp = h % 2

        @J("tensor")
        def _(tensor, g=g, c=c, h=h, hp=hp, n=n):
            WM(tensor, "SV", ("A", g, c, h))
            if n >= 2:
                WM(tensor, "SS", ("C",) + k3(n - 2))
            psT = ps_T if hp == 0 else ps_T2
            tensor.transpose(psT[:, 0:128], lrp[:, hp, 4, :], ident[:])
            INC("ST", tensor.transpose(psT[:, 128:256], lrp[:, hp, 5, :], ident[:]),
                1, ("B", g, c, h))
            if n >= 1:
                WM(tensor, "SV", ("Esc",) + k3(n - 1))
            tensor.matmul(ps_S[:, 0:128], lrp[:, hp, 4, :], lrp[:, hp, 0, :], start=True, stop=False)
            INC("ST", tensor.matmul(ps_S[:, 0:128], lrp[:, hp, 5, :], lrp[:, hp, 1, :], start=False, stop=True),
                1, ("D", g, c, h))

    def p2_C(n):
        g, c, h = k3(n)
        hp = h % 2

        @J("scalar")
        def _(scalar, g=g, c=c, h=h, hp=hp):
            WM(scalar, "ST", ("B", g, c, h))
            psT = ps_T if hp == 0 else ps_T2
            scalar.activation(kT[:, hp, 0, :], psT[:, 0:128], AF.Copy,
                              bias=0.0, scale=kdecpp[:, h:h + 1])
            INC("SS", scalar.activation(kT[:, hp, 1, :], psT[:, 128:256], AF.Copy,
                                        bias=0.0, scale=kdecpp[:, h:h + 1]), 1, ("C", g, c, h))

    def p2_Esc(n):
        g, c, h = k3(n)
        hp = h % 2

        @J("vector")
        def _(vector, g=g, c=c, h=h, hp=hp, n=n):
            WM(vector, "ST", ("D", g, c, h))
            if n >= 1:
                WM(vector, "ST", ("F",) + k3(n - 1))
            INC("SV", vector.tensor_tensor(scb[:, 0, :], ps_S[:, 0:128], diagT[:, h * 128:(h + 1) * 128], ALU.mult),
                1, ("Esc", g, c, h))

    def p2_F(n):
        g, c, h = k3(n)
        C_abs = n // H
        hp = h % 2
        cpar = C_abs % 2

        @J("tensor")
        def _(tensor, g=g, c=c, h=h, hp=hp, C_abs=C_abs, cpar=cpar, n=n):
            WM(tensor, "SV", ("Esc", g, c, h))
            WM(tensor, "SS", ("C", g, c, h))
            if h == 0:
                WM(tensor, "GQ" if cpar == 0 else "GQ2", ("vld", C_abs))
            if n >= 1:
                WM(tensor, "SS", ("G",) + k3(n - 1))
                WM(tensor, "SV", ("Hst",) + k3(n - 1))
            if C_abs >= 1:
                WM(tensor, "SV", ("Hst",) + k3(n - H))
            vsl = v_c[:, 0, h * 128:(h + 1) * 128]
            tensor.matmul(ps_O[:, 0:128], vsl, scb[:, 0, :], start=True, stop=False)
            tensor.matmul(ps_O[:, 0:128], stbf[:, 2 * h, :], lrp[:, hp, 2, :], start=False, stop=False)
            tensor.matmul(ps_O[:, 0:128], stbf[:, 2 * h + 1, :], lrp[:, hp, 3, :], start=False, stop=True)
            tensor.matmul(ps_U[:, 0:128], kT[:, hp, 0, :], vsl, start=True, stop=True)
            INC("ST", tensor.matmul(ps_U[:, 128:256], kT[:, hp, 1, :], vsl, start=True, stop=True),
                1, ("F", g, c, h))

    def p2_G(n):
        g, c, h = k3(n)

        @J("scalar")
        def _(scalar, g=g, c=c, h=h):
            WM(scalar, "ST", ("F", g, c, h))
            INC("SS", scalar.copy(y_g[:, g % 2, h, c * 128:(c + 1) * 128], ps_O[:, 0:128]), 1, ("G", g, c, h))

    def p2_Hst(n):
        g, c, h = k3(n)

        @J("vector")
        def _(vector, g=g, c=c, h=h, bd=bdec[h]):
            WM(vector, "ST", ("F", g, c, h))
            vector.scalar_tensor_tensor(st32[:, 2 * h, :], st32[:, 2 * h, :], bd, ps_U[:, 0:128], ALU.mult, ALU.add)
            vector.scalar_tensor_tensor(st32[:, 2 * h + 1, :], st32[:, 2 * h + 1, :], bd, ps_U[:, 128:256], ALU.mult, ALU.add)
            vector.tensor_copy(stbf[:, 2 * h, :], st32[:, 2 * h, :])
            INC("SV", vector.tensor_copy(stbf[:, 2 * h + 1, :], st32[:, 2 * h + 1, :]), 1, ("Hst", g, c, h))

    def p2_pjmm(pg, ot):
        @J("tensor")
        def _(tensor, pg=pg, ot=ot):
            if ot == 0:
                WM(tensor, "SS", ("G", pg, 3, H - 1))
                WM(tensor, "LQ", ("xtg", pg))
            else:
                WM(tensor, "SV", ("rres", pg, ot - 1))
            for k in range(DT):
                inst = tensor.matmul(ps_P[:, :], wpt[:, k, ot * 128:(ot + 1) * 128], y_g[:, pg % 2, k, :],
                                     start=(k == 0), stop=(k == DT - 1))
            INC("ST", inst, 1, ("pjmm", pg, ot))

    def p2_rres(pg, ot):
        @J("vector")
        def _(vector, pg=pg, ot=ot):
            WM(vector, "ST", ("pjmm", pg, ot))
            INC("SV", vector.tensor_tensor(xT_g[:, ot, :], ps_P[:, :], xT_g[:, ot, :], ALU.add),
                1, ("rres", pg, ot))

        @J("scalar")
        def _(scalar, pg=pg, ot=ot):
            WM(scalar, "SV", ("rres", pg, ot))
            if ot == 0 and pg >= 1:
                WM(scalar, "SQ", ("lndma", pg - 1, DT - 1))
            if ot >= 1:
                WM(scalar, "ST", ("sqmm", pg, ot - 1))
            INC("SS", scalar.activation(rsq[:, 0, :], xT_g[:, ot, :], AF.Square,
                                        bias=c_ap(0.0), scale=c_ap(1.0)), 1, ("rsq", pg, ot))

    def p2_summ(pg, ot):
        @J("tensor")
        def _(tensor, pg=pg, ot=ot):
            WM(tensor, "SV", ("rres", pg, ot))
            INC("ST", tensor.matmul(ps_st[0:1, 0:512], onescol[:], xT_g[:, ot, :],
                                    start=(ot == 0), stop=(ot == DT - 1)), 1, ("summ", pg, ot))

    def p2_sqmm(pg, ot):
        @J("tensor")
        def _(tensor, pg=pg, ot=ot):
            WM(tensor, "SS", ("rsq", pg, ot))
            INC("ST", tensor.matmul(ps_bc[0:1, 0:512], onescol[:], rsq[:, 0, :],
                                    start=(ot == 0), stop=(ot == DT - 1)), 1, ("sqmm", pg, ot))

    def p2_stats1(pg):
        @J("vector")
        def _(vector, pg=pg):
            WM(vector, "ST", ("summ", pg, DT - 1))
            WM(vector, "ST", ("sqmm", pg, DT - 1))
            vector.tensor_scalar(stats[0:1, 0, :], ps_st[0:1, 0:512], 1.0 / D, None, ALU.mult)
            vector.tensor_scalar(stats[0:1, 1, :], ps_bc[0:1, 0:512], 1.0 / D, None, ALU.mult)
            vector.tensor_tensor(stats[0:1, 3, :], stats[0:1, 0, :], stats[0:1, 0, :], ALU.mult)
            INC("SV", vector.tensor_tensor(stats[0:1, 1, :], stats[0:1, 1, :], stats[0:1, 3, :], ALU.subtract),
                1, ("var", pg))

        @J("scalar")
        def _(scalar, pg=pg):
            WM(scalar, "SV", ("var", pg))
            INC("SS", scalar.activation(stats[0:1, 2, :], stats[0:1, 1, :], AF.Sqrt,
                                        bias=c_ap1(1e-5), scale=c_ap1(1.0)), 1, ("sqv", pg))

    def p2_stats2(pg):
        @J("vector")
        def _(vector, pg=pg):
            WM(vector, "SS", ("sqv", pg))
            INC("SV", vector.reciprocal(stats[0:1, 3, :], stats[0:1, 2, :]), 1, ("fin", pg))

        @J("tensor")
        def _(tensor, pg=pg):
            WM(tensor, "SV", ("fin", pg))
            tensor.matmul(ps_st[:, 0:512], onesrow[:, :], stats[0:1, 0, :], start=True, stop=True)
            INC("ST", tensor.matmul(ps_bc[:, 0:512], onesrow[:, :], stats[0:1, 3, :], start=True, stop=True),
                1, ("bcast", pg))

    def p2_ln(pg, ot):
        @J("vector")
        def _(vector, pg=pg, ot=ot):
            WM(vector, "ST", ("bcast", pg))
            if ot >= 1:
                WM(vector, "SQ", ("lndma", pg, ot - 1))
            vector.tensor_tensor(rsq[:, 0, :], xT_g[:, ot, :], ps_st[:, 0:512], ALU.subtract)
            INC("SV", vector.tensor_tensor(rsq[:, 0, :], rsq[:, 0, :], ps_bc[:, 0:512], ALU.mult),
                1, ("ln", pg, ot))

        @J("scalar")
        def _(scalar, pg=pg, ot=ot):
            WM(scalar, "SV", ("ln", pg, ot))
            INC("SQ", scalar.dma_start(ln1s[ot * 128:(ot + 1) * 128, pg * GT:(pg + 1) * GT],
                                       rsq[:, 0, :]), 16, ("lndma", pg, ot))

    def p2_gdone(pg):
        @J("vector")
        def _(vector, pg=pg):
            INC("SV", vector.engine_nop(), 1, ("gdone", pg))

        @J("gpsimd")
        def _(gp, pg=pg):
            WM(gp, "SV", ("gdone", pg))
            gp.sem_inc(sems["GRP"], 1)
            cnt["GRP"] += 1

    def p2_ydump(pg):
        @J("scalar")
        def _(scalar, pg=pg):
            if DEBUG:
                WM(scalar, "SS", ("G", pg, 3, H - 1))
                for hh in range(H):
                    INC("SQ", scalar.dma_start(ysd[hh * 128:(hh + 1) * 128, pg * GT:(pg + 1) * GT],
                                               y_g[:, pg % 2, hh, :]), 16)

    def p2_proj_extras(pg, m):
        if pg < 0:
            return
        if m == 2 and DEBUG:
            p2_ydump(pg)
        if m >= 2 and m % 2 == 0 and (m - 2) // 2 < DT:
            p2_pjmm(pg, (m - 2) // 2)
        if m >= 3 and m % 2 == 1 and (m - 3) // 2 < DT:
            p2_rres(pg, (m - 3) // 2)
        if m >= 4 and m % 2 == 0 and (m - 4) // 2 < DT:
            p2_summ(pg, (m - 4) // 2)
        if m >= 5 and m % 2 == 1 and (m - 5) // 2 < DT:
            p2_sqmm(pg, (m - 5) // 2)
        if m == 37:
            p2_stats1(pg)
        elif m == 39:
            p2_stats2(pg)
        elif 41 <= m < 41 + DT:
            p2_ln(pg, m - 41)
        elif m == 41 + DT + 1:
            p2_gdone(pg)

    NSTEP = NCH * H
    p2_trig(0)
    for n in range(NSTEP + 2):
        if n < NSTEP:
            C_abs = n // H
            h = n % H
            if h == 4 and C_abs + 1 < NCH:
                p2_trig(C_abs + 1)
            p2_A(n)
        if 1 <= n <= NSTEP:
            p2_Esc(n - 1)
        if n >= 2:
            p2_Hst(n - 2)
        if n < NSTEP:
            p2_BD(n)
            p2_C(n)
        if n >= 1 and n - 1 < NSTEP:
            p2_F(n - 1)
            p2_G(n - 1)
        if n < NSTEP:
            p2_proj_extras(n // 64 - 1, n % 64)

    if DEBUG:
        p2_ydump(NG - 1)
    for ot in range(DT):
        p2_pjmm(NG - 1, ot)
        p2_rres(NG - 1, ot)
        p2_summ(NG - 1, ot)
        p2_sqmm(NG - 1, ot)
    p2_stats1(NG - 1)
    p2_stats2(NG - 1)
    for ot in range(DT):
        p2_ln(NG - 1, ot)
    p2_gdone(NG - 1)

    phase_barrier("p2")
    p2.close()
    if PHASES <= 2:
        blk_cm.__exit__(None, None, None)
        es.close()
        LAST_MARKS = dict(marks)
        return nc

    p3 = ExitStack()
    ps3_T = p3.enter_context(nc.psum_tensor("p3_T", [128, 1024], BF16))
    ps3_T2 = p3.enter_context(nc.psum_tensor("p3_T2", [128, 1024], BF16))
    ps3_T3 = p3.enter_context(nc.psum_tensor("p3_T3", [128, 1024], BF16))
    ps3_T4 = p3.enter_context(nc.psum_tensor("p3_T4", [128, 1024], BF16))
    ps3_G = p3.enter_context(nc.psum_tensor("p3_G", [128, 512], F32))
    ps3_G2 = p3.enter_context(nc.psum_tensor("p3_G2", [128, 512], F32))
    ps3_U = p3.enter_context(nc.psum_tensor("p3_U", [128, 512], F32))
    ps3_U2 = p3.enter_context(nc.psum_tensor("p3_U2", [128, 512], F32))
    TP = [(ps3_T, ps3_T2), (ps3_T3, ps3_T4)]

    def emit_cpool_init(cpool, mk):
        @J("gpsimd")
        def _(gp):
            gp.memset(cpool[:, 0, :], 251)
            gp.memset(cpool[:, 1, :], 127)
            INC("SP", gp.memset(cpool[:, 2, :], 126), 1, mk)

    DQ = {"q": []}

    def _dq_v1(d):
        @J("vector")
        def _(vector, d=d):
            WM(vector, d.get("ld_sem", "LQ"), d["ld_mark"])
            if d["prev2"] is not None:
                WM(vector, "SS", ("dqr", d["prev2"]))
            b = 5 * d["set"]
            dqa = d["dqa"]
            wN = d["wN"]
            L = 8 * wN
            src = d["src_ap_fn"]()
            vector.tensor_copy(dqa[:, b + 0, 0:wN], src)
            for j in range(1, 8):
                vector.tensor_scalar(dqa[:, b + 0, j * wN:(j + 1) * wN], src, 4 * j, None, ALU.logical_shift_right)
            vector.tensor_scalar(dqa[:, b + 1, 0:L], dqa[:, b + 0, 0:L], 0x7, None, ALU.bitwise_and)
            vector.tensor_scalar(dqa[:, b + 2, 0:L], dqa[:, b + 0, 0:L], 12, None, ALU.logical_shift_left)
            INC("SV", vector.tensor_scalar(dqa[:, b + 2, 0:L], dqa[:, b + 2, 0:L], 0x8000, None, ALU.bitwise_and),
                1, ("dqk", d["key"]))

    def _dq_p1(d):
        @J("gpsimd")
        def _(gp, d=d):
            WM(gp, "SV", ("dqk", d["key"]))
            b = 5 * d["set"]
            dqa = d["dqa"]
            cpool = d["cpool"]
            L = 8 * d["wN"]
            gp.tensor_tensor(dqa[:, b + 3, 0:L], dqa[:, b + 1, 0:L], cpool[:, 1, 0:L], ALU.add)
            INC("SP", gp.tensor_tensor(dqa[:, b + 4, 0:L], dqa[:, b + 1, 0:L], cpool[:, 2, 0:L], ALU.add),
                1, ("dqp1", d["key"]))

    def _dq_v2(d):
        @J("vector")
        def _(vector, d=d):
            WM(vector, "SP", ("dqp1", d["key"]))
            b = 5 * d["set"]
            dqa = d["dqa"]
            L = 8 * d["wN"]
            vector.tensor_scalar(dqa[:, b + 3, 0:L], dqa[:, b + 3, 0:L], 7, None, ALU.logical_shift_right)
            INC("SV", vector.tensor_scalar(dqa[:, b + 4, 0:L], dqa[:, b + 4, 0:L], 7, None, ALU.logical_shift_right),
                1, ("dqf", d["key"]))

    def _dq_p2(d):
        @J("gpsimd")
        def _(gp, d=d):
            WM(gp, "SV", ("dqf", d["key"]))
            b = 5 * d["set"]
            dqa = d["dqa"]
            cpool = d["cpool"]
            L = 8 * d["wN"]
            gp.tensor_tensor(dqa[:, b + 3, 0:L], dqa[:, b + 3, 0:L], cpool[:, 0, 0:L], ALU.mult)
            gp.tensor_tensor(dqa[:, b + 4, 0:L], dqa[:, b + 3, 0:L], dqa[:, b + 4, 0:L], ALU.add)
            INC("SP", gp.tensor_tensor(dqa[:, b + 4, 0:L], dqa[:, b + 4, 0:L], dqa[:, b + 1, 0:L], ALU.add),
                1, ("dqp2", d["key"]))

    def _dq_v3(d):
        @J("vector")
        def _(vector, d=d):
            WM(vector, "SP", ("dqp2", d["key"]))
            b = 5 * d["set"]
            dqa = d["dqa"]
            L = 8 * d["wN"]
            vector.tensor_scalar(dqa[:, b + 4, 0:L], dqa[:, b + 4, 0:L], 6, None, ALU.logical_shift_left)
            INC("SV", vector.tensor_tensor(dqa[:, b + 4, 0:L], dqa[:, b + 4, 0:L], dqa[:, b + 2, 0:L], ALU.bitwise_or),
                1, ("dqb", d["key"]))

    def _dq_a(d):
        @J("scalar")
        def _(scalar, d=d):
            WM(scalar, "SV", ("dqb", d["key"]))
            if d["sclc"] is not None:
                WM(scalar, "SV", d["sclc"])
            if d["dense_gate"] is not None:
                WM(scalar, "ST", d["dense_gate"])
            b = 5 * d["set"]
            dqa = d["dqa"]
            wN = d["wN"]
            L = 8 * wN
            ng = L // 128
            gw = wN // ng
            fb = dqa[:, b + 4, 0:L].bitcast(BF16)
            for gi in range(ng):
                srcap = bass.AP(fb.tensor, fb.offset + 2 * gi * gw, [[fb.ap[0][0], 128], [2 * wN, 8], [2, gw]])
                dst = bass.AP(d["dense_t"], d["dense_off"] + gi * 128, [[d["row_len"], 128], [1, 8], [8, gw]])
                inst = scalar.activation(dst, srcap, AF.Copy, bias=0.0, scale=d["scale_fn"](gi))
                if gi == ng - 1:
                    INC("SS", inst, 1, ("dqr", d["key"]))

    def _dq_round(r):
        q = DQ["q"]
        n = len(q)
        if 0 <= r - 2 < n and not q[r - 2]["s"][4]:
            _dq_v3(q[r - 2])
            q[r - 2]["s"][4] = True
        if 0 <= r - 1 < n and not q[r - 1]["s"][2]:
            _dq_v2(q[r - 1])
            q[r - 1]["s"][2] = True
        if r < n and not q[r]["s"][0]:
            _dq_v1(q[r])
            q[r]["s"][0] = True
        if 0 <= r - 1 < n and not q[r - 1]["s"][3]:
            _dq_p2(q[r - 1])
            q[r - 1]["s"][3] = True
        if r < n and not q[r]["s"][1]:
            _dq_p1(q[r])
            q[r]["s"][1] = True
        if 0 <= r - 2 < n and not q[r - 2]["s"][5]:
            _dq_a(q[r - 2])
            q[r - 2]["s"][5] = True

    def dq_pass(key, ld_mark, src_ap_fn, wN, dense_t, dense_off, row_len, scale_fn, dqa, cpool, sclc=None, dense_gate=None, ld_sem="LQ"):
        q = DQ["q"]
        idx = len(q)
        d = dict(key=key, ld_mark=ld_mark, src_ap_fn=src_ap_fn, wN=wN, dense_t=dense_t,
                 dense_off=dense_off, row_len=row_len, scale_fn=scale_fn, dqa=dqa,
                 cpool=cpool, sclc=sclc, dense_gate=dense_gate, set=idx % 2, ld_sem=ld_sem,
                 prev2=q[idx - 2]["key"] if idx >= 2 else None, s=[False] * 6)
        q.append(d)
        _dq_round(idx)

    def dq_drain():
        n = len(DQ["q"])
        _dq_round(n)
        _dq_round(n + 1)

    def dq_reset():
        DQ["q"] = []

    p3a = ExitStack()
    dqa_a = p3a.enter_context(nc.sbuf_tensor("dqa_a", [128, 10, 512], I32))
    cpool_a = p3a.enter_context(nc.sbuf_tensor("cpool_a", [128, 3, 512], I32))
    emit_cpool_init(cpool_a, ("p3ac",))
    ln1a = p3a.enter_context(nc.sbuf_tensor("ln1a", [128, DT, T], BF16))
    pk = p3a.enter_context(nc.sbuf_tensor("pk", [128, 2, 2, 256], I32))
    scl6 = p3a.enter_context(nc.sbuf_tensor("scl6", [128, 2 * IT, DT], F16))
    sclf = p3a.enter_context(nc.sbuf_tensor("sclf", [128, 2 * IT, DT], F32))
    dense = p3a.enter_context(nc.sbuf_tensor("dense", [128, 2, D], BF16))
    wtg = p3a.enter_context(nc.sbuf_tensor("wtg", [128, 2, 2, DT * 128], BF16))
    sg_sb = p3a.enter_context(nc.sbuf_tensor("sg_sb", [128, 2, 512], BF16))
    hstg = p3a.enter_context(nc.sbuf_tensor("hstg", [128, 2, 512], BF16))

    @J("sync")
    def _(sync):
        for b in range(2 * IT):
            sync.dma_start(scl6[:, b, :], gus[b * 128:(b + 1) * 128, :]).then_inc(sems["LQ"], 16)
            cnt["LQ"] += 16
        marks[("sclall",)] = cnt["LQ"]
        for k in range(DT):
            sync.dma_start(ln1a[:, k, :], ln1s[k * 128:(k + 1) * 128, :]).then_inc(sems["LQ"], 16)
            cnt["LQ"] += 16
        marks[("ln1a",)] = cnt["LQ"]
        for ig in range(IT):
            par = ig % 2
            gq = "GQ" if par == 0 else "GQ2"
            if ig >= 2:
                WM(sync, "SV", ("dqk", ("u", ig - 2, 3)))
            INC(gq, sync.dma_start(pk[:, par, 0, :], gup[ig * 128:(ig + 1) * 128, :]), 16)
            INC(gq, sync.dma_start(pk[:, par, 1, :], gup[IF + ig * 128:IF + (ig + 1) * 128, :]), 16,
                ("pkR", ig))

    @J("vector")
    def _(vector):
        WM(vector, "LQ", ("ln1a",))
        INC("SV", vector.tensor_copy(sclf[:], scl6[:]), 1, ("sclallc",))

    def emit_tr_group(tensor, tag, gu, igx, half):
        pst = TP[igx % 2][half]
        WM(tensor, "SS", ("dqr", (tag, igx, 2 * half + 1)))
        if gu == 1:
            WM(tensor, "SS", ("wtev", "g", igx, half))
        elif igx >= 2:
            WM(tensor, "SS", ("wtev", "u", igx - 2, half))
        for k in range(8):
            inst = tensor.transpose(pst[:, k * 128:(k + 1) * 128],
                                    dense[:, gu, (half * 8 + k) * 128:(half * 8 + k + 1) * 128], ident[:])
        INC("ST", inst, 1, ("wtr", tag, igx, half))

    def emit_evac(tag, gu, igx, half):
        @J("scalar")
        def _(scalar, tag=tag, gu=gu, igx=igx, half=half):
            WM(scalar, "ST", ("wtr", tag, igx, half))
            if igx >= 2:
                WM(scalar, "ST", ("umm", igx - 2, NG - 1))
            pst = TP[igx % 2][half]
            for k in range(8):
                INC("SS", scalar.copy(wtg[:, igx % 2, gu, (half * 8 + k) * 128:(half * 8 + k + 1) * 128],
                                      pst[:, k * 128:(k + 1) * 128]),
                    1, ("wtev", tag, igx, half) if k == 7 else None)

    def emit_silu(igx, gg):
        @J("scalar")
        def _(scalar, igx=igx, gg=gg):
            if gg >= 2:
                WM(scalar, "SV", ("hmul", igx, gg - 2))
            elif igx >= 1:
                WM(scalar, "SV", ("hmul", igx - 1, NG - 2 + gg))
            WM(scalar, "ST", ("gmm", igx, gg))
            psg = ps3_G if gg % 2 == 0 else ps3_G2
            INC("SS", scalar.activation(sg_sb[:, gg % 2, :], psg[:, :], AF.Silu,
                                        bias=c_ap(0.0), scale=c_ap(1.0)), 1, ("silu", igx, gg))

    def emit_hdma(igx, gg):
        @J("scalar")
        def _(scalar, igx=igx, gg=gg):
            WM(scalar, "SV", ("hmul", igx, gg))
            INC("SQ", scalar.dma_start(hs[igx * 128:(igx + 1) * 128, gg * GT:(gg + 1) * GT],
                                       hstg[:, gg % 2, :]), 16, ("hdma", igx, gg))

    def emit_hmul(igx, gg):
        @J("vector")
        def _(vector, igx=igx, gg=gg):
            WM(vector, "ST", ("umm", igx, gg))
            WM(vector, "SS", ("silu", igx, gg))
            if gg >= 2:
                WM(vector, "SQ", ("hdma", igx, gg - 1))
            elif igx >= 1:
                WM(vector, "SQ", ("hdma", igx - 1, NG - 1))
            psu = ps3_U if gg % 2 == 0 else ps3_U2
            INC("SV", vector.tensor_tensor(hstg[:, gg % 2, :], psu[:, :], sg_sb[:, gg % 2, :], ALU.mult),
                1, ("hmul", igx, gg))

    for ig in range(IT):
        par = ig % 2
        for p8 in range(8):
            gu = p8 // 4
            ps = p8 % 4
            tag = "g" if gu == 0 else "u"
            dq_pass(
                key=(tag, ig, ps),
                ld_mark=("pkR", ig), ld_sem="GQ" if ig % 2 == 0 else "GQ2",
                src_ap_fn=(lambda par=par, gu=gu, ps=ps: pk[:, par, gu, ps * 64:(ps + 1) * 64]),
                wN=64,
                dense_t=dense, dense_off=gu * D + ps * 512, row_len=2 * D,
                scale_fn=(lambda gi, ig=ig, gu=gu, ps=ps: sclf[:, gu * IT + ig, ps * 4 + gi:ps * 4 + gi + 1]),
                dqa=dqa_a, cpool=cpool_a, sclc=("sclallc",),
                dense_gate=("wtr", tag, ig - 1, ps // 2) if ig >= 1 else None,
            )
            if p8 == 7:
                dq_drain()
            if ig >= 1:
                if p8 >= 3:
                    emit_hdma(ig - 1, p8 - 3)
                if p8 >= 2:
                    emit_hmul(ig - 1, p8 - 2)
                emit_silu(ig - 1, p8)
            if p8 == 3:
                emit_evac("g", 0, ig, 0)
            elif p8 == 5:
                emit_evac("g", 0, ig, 1)
            elif p8 == 7:
                emit_evac("u", 1, ig, 0)
        emit_evac("u", 1, ig, 1)
        if ig >= 1:
            emit_hmul(ig - 1, NG - 2)
            emit_hmul(ig - 1, NG - 1)
            for gg in (NG - 3, NG - 2, NG - 1):
                emit_hdma(ig - 1, gg)
        if DEBUG:
            @J("scalar")
            def _(scalar, ig=ig):
                WM(scalar, "SS", ("dqr", ("g", ig, 3)))
                INC("SQ", scalar.dma_start(wgd[ig * 128:(ig + 1) * 128, :], dense[:, 0, :]), 16)
                WM(scalar, "SS", ("dqr", ("u", ig, 3)))
                INC("SQ", scalar.dma_start(wgd[IF + ig * 128:IF + (ig + 1) * 128, :], dense[:, 1, :]), 16)

        @J("tensor")
        def _(tensor, ig=ig, par=par):
            if ig == 0:
                WM(tensor, "LQ", ("ln1a",))
                for gu, tag in ((0, "g"), (1, "u")):
                    for half in range(2):
                        emit_tr_group(tensor, tag, gu, 0, half)
            for gg in range(NG):
                psg = ps3_G if gg % 2 == 0 else ps3_G2
                psu = ps3_U if gg % 2 == 0 else ps3_U2
                if gg == 0:
                    for half in range(2):
                        WM(tensor, "SS", ("wtev", "g", ig, half))
                if gg >= 2 or ig >= 1:
                    pkey = ("silu", ig, gg - 2) if gg >= 2 else ("silu", ig - 1, NG - 2 + gg)
                    WM(tensor, "SS", pkey)
                for k in range(DT):
                    inst = tensor.matmul(psg[:, :], wtg[:, par, 0, k * 128:(k + 1) * 128],
                                         ln1a[:, k, gg * GT:(gg + 1) * GT], start=(k == 0), stop=(k == DT - 1))
                INC("ST", inst, 1, ("gmm", ig, gg))
                if gg == 0:
                    for half in range(2):
                        WM(tensor, "SS", ("wtev", "u", ig, half))
                if gg >= 2 or ig >= 1:
                    pkey = ("hmul", ig, gg - 2) if gg >= 2 else ("hmul", ig - 1, NG - 2 + gg)
                    WM(tensor, "SV", pkey)
                for k in range(DT):
                    inst = tensor.matmul(psu[:, :], wtg[:, par, 1, k * 128:(k + 1) * 128],
                                         ln1a[:, k, gg * GT:(gg + 1) * GT], start=(k == 0), stop=(k == DT - 1))
                INC("ST", inst, 1, ("umm", ig, gg))
                if ig + 1 < IT:
                    if gg == 2:
                        emit_tr_group(tensor, "g", 0, ig + 1, 0)
                    elif gg == 4:
                        emit_tr_group(tensor, "g", 0, ig + 1, 1)
                    elif gg == 6:
                        emit_tr_group(tensor, "u", 1, ig + 1, 0)
                        emit_tr_group(tensor, "u", 1, ig + 1, 1)

    for p8 in range(8):
        if p8 >= 3:
            emit_hdma(IT - 1, p8 - 3)
        if p8 >= 2:
            emit_hmul(IT - 1, p8 - 2)
        emit_silu(IT - 1, p8)
    emit_hmul(IT - 1, NG - 2)
    emit_hmul(IT - 1, NG - 1)
    for gg in (NG - 3, NG - 2, NG - 1):
        emit_hdma(IT - 1, gg)


    phase_barrier("p3a")
    p3a.close()
    dq_reset()
    if PHASES <= 3:
        p3.close()
        blk_cm.__exit__(None, None, None)
        es.close()
        LAST_MARKS = dict(marks)
        return nc

    p3p = ExitStack()
    dqa_p = p3p.enter_context(nc.sbuf_tensor("dqa_p", [128, 10, 512], I32))
    cpool_p = p3p.enter_context(nc.sbuf_tensor("cpool_p", [128, 3, 512], I32))
    emit_cpool_init(cpool_p, ("p3pc",))
    pkd = p3p.enter_context(nc.sbuf_tensor("pkd", [128, 2, 704], I32))
    scld = p3p.enter_context(nc.sbuf_tensor("scld", [128, DT, IT], F16))
    scldf = p3p.enter_context(nc.sbuf_tensor("scldf", [128, DT, IT], F32))
    densed = p3p.enter_context(nc.sbuf_tensor("densed", [128, IF], BF16))
    wstg = p3p.enter_context(nc.sbuf_tensor("wstg", [128, IF], BF16))

    @J("sync")
    def _(sync):
        for b in range(DT):
            sync.dma_start(scld[:, b, :], dns[b * 128:(b + 1) * 128, :]).then_inc(sems["LQ"], 16)
            cnt["LQ"] += 16
        marks[("dsclall",)] = cnt["LQ"]
        WM(sync, "SV", ("dqk", ("u", IT - 1, 3)))
        for ot in range(DT):
            if ot >= 2:
                WM(sync, "SV", ("dqk", ("d", ot - 2, 10)))
            INC("GQ" if ot % 2 == 0 else "GQ2",
                sync.dma_start(pkd[:, ot % 2, :], dnp[ot * 128:(ot + 1) * 128, :]), 16, ("pkdR", ot))

    @J("vector")
    def _(vector):
        WM(vector, "LQ", ("dsclall",))
        INC("SV", vector.tensor_copy(scldf[:], scld[:]), 1, ("dsclallc",))

    NPASS = 11
    PSQ = [ps3_T, ps3_T2, ps3_T3, ps3_T4]
    for ot in range(DT):
        par = ot % 2
        for ps in range(NPASS):
            w0 = ps * 64
            dq_pass(
                key=("d", ot, ps),
                ld_mark=("pkdR", ot), ld_sem="GQ" if ot % 2 == 0 else "GQ2",
                src_ap_fn=(lambda par=par, w0=w0: pkd[:, par, w0:w0 + 64]),
                wN=64,
                dense_t=densed, dense_off=8 * w0, row_len=IF,
                scale_fn=(lambda gi, ot=ot, w0=w0: scldf[:, ot, (8 * w0) // 128 + gi:(8 * w0) // 128 + gi + 1]),
                dqa=dqa_p, cpool=cpool_p, sclc=("dsclallc",),
                dense_gate=("dtr", ot - 1, 4 * ps + 3) if ot >= 1 else None,
            )
        dq_drain()

        @J("tensor")
        def _(tensor, ot=ot):
            for ib in range(IT):
                pst = PSQ[ib % 4]
                if ib >= 4:
                    WM(tensor, "SS", ("dtev", ot, ib - 4))
                elif ot >= 1:
                    WM(tensor, "SS", ("dtev", ot - 1, IT - 4 + ib))
                WM(tensor, "SS", ("dqr", ("d", ot, ib // 4)))
                INC("ST", tensor.transpose(pst[:, 0:128], densed[:, ib * 128:(ib + 1) * 128], ident[:]),
                    1, ("dtr", ot, ib))

        @J("scalar")
        def _(scalar, ot=ot):
            if ot >= 1:
                WM(scalar, "SQ", ("wddma", ot - 1))
            for ib in range(IT):
                WM(scalar, "ST", ("dtr", ot, ib))
                pst = PSQ[ib % 4]
                INC("SS", scalar.copy(wstg[:, ib * 128:(ib + 1) * 128], pst[:, 0:128]), 1, ("dtev", ot, ib))
            WM(scalar, "SS", ("dtev", ot, IT - 1))
            INC("SQ", scalar.dma_start(wdTs[ot * 128:(ot + 1) * 128, :], wstg[:]), 16, ("wddma", ot))
            if DEBUG:
                INC("SQ", scalar.dma_start(wdd[ot * 128:(ot + 1) * 128, :], wstg[:]), 16)

    phase_barrier("p3prep")
    p3p.close()
    if PHASES <= 4:
        p3.close()
        blk_cm.__exit__(None, None, None)
        es.close()
        LAST_MARKS = dict(marks)
        return nc

    p3m = ExitStack()
    h_g = p3m.enter_context(nc.sbuf_tensor("h_g", [128, IT, GT], BF16))
    ln1g = p3m.enter_context(nc.sbuf_tensor("ln1g", [128, DT, GT], BF16))
    wdt = p3m.enter_context(nc.sbuf_tensor("wdt", [128, 2, IF], BF16))
    yg = p3m.enter_context(nc.sbuf_tensor("yg", [128, DT, GT], BF16))
    r2g = p3m.enter_context(nc.sbuf_tensor("r2g", [128, DT, GT], F32))
    sqb = p3m.enter_context(nc.sbuf_tensor("sqb", [128, 4, GT], BF16))
    st3 = p3m.enter_context(nc.sbuf_tensor("st3", [1, 6, GT], F32))
    ost = p3m.enter_context(nc.sbuf_tensor("ost", [128, 4, D], F32))
    lnb = p3m.enter_context(nc.sbuf_tensor("lnb", [128, 2, GT], BF16))

    @J("sync")
    def _(sync):
        for g in range(NG):
            if g >= 1:
                W(sync, "GRP", NG + g)
            for ib in range(IT):
                sync.dma_start(h_g[:, ib, :], hs[ib * 128:(ib + 1) * 128, g * GT:(g + 1) * GT]).then_inc(sems["LQ"], 16)
                cnt["LQ"] += 16
            for k in range(DT):
                sync.dma_start(ln1g[:, k, :], ln1s[k * 128:(k + 1) * 128, g * GT:(g + 1) * GT]).then_inc(sems["LQ"], 16)
                cnt["LQ"] += 16
            marks[("hld", g)] = cnt["LQ"]
            for ot in range(DT):
                if ot >= 2:
                    WM(sync, "ST", ("ymm", g, ot - 2))
                INC("GQ" if ot % 2 == 0 else "GQ2",
                    sync.dma_start(wdt[:, ot % 2, :], wdTs[ot * 128:(ot + 1) * 128, :]), 16, ("wdldR", g, ot))

    for g in range(NG):
        @J("tensor")
        def _(tensor, g=g):
            WM(tensor, "LQ", ("hld", g))
            for ot in range(DT):
                WM(tensor, "GQ" if ot % 2 == 0 else "GQ2", ("wdldR", g, ot))
                psm = ps3_G if ot % 2 == 0 else ps3_G2
                if ot >= 2:
                    WM(tensor, "SV", ("yev", g, ot - 2))
                    WM(tensor, "SS", ("ysq", g, ot - 2))
                for ib in range(IT):
                    inst = tensor.matmul(psm[:, :], wdt[:, ot % 2, ib * 128:(ib + 1) * 128],
                                         h_g[:, ib, :], start=(ib == 0), stop=(ib == IT - 1))
                INC("ST", inst, 1, ("ymm", g, ot))
                if ot >= 2:
                    WM(tensor, "SS", ("ysq", g, ot - 2))
                    INC("ST", tensor.matmul(ps3_U[0:1, 0:512], onescol[:], sqb[:, ot % 2, :],
                                            start=(ot - 2 == 0), stop=False), 1, ("rsmm", g, ot - 2))
            for ot in range(DT - 2, DT):
                WM(tensor, "SS", ("ysq", g, ot))
                INC("ST", tensor.matmul(ps3_U[0:1, 0:512], onescol[:], sqb[:, ot % 2, :],
                                        start=False, stop=(ot == DT - 1)), 1, ("rsmm", g, ot))
            WM(tensor, "SV", ("rmsfin", g))
            INC("ST", tensor.matmul(ps3_U2[:, 0:512], onesrow[:, :], st3[0:1, 3, :], start=True, stop=True),
                1, ("rmsbc", g))
            for ot in range(DT):
                WM(tensor, "SV", ("r2b", g, ot))
                INC("ST", tensor.matmul(ps3_G[0:1, 0:512], onescol[:], sqb[:, ot % 2, :],
                                        start=(ot == 0), stop=(ot == DT - 1)), 1, ("l2smm", g, ot))
                if ot >= 2:
                    WM(tensor, "SS", ("r2sq", g, ot - 2))
                    INC("ST", tensor.matmul(ps3_G2[0:1, 0:512], onescol[:], sqb[:, 2 + ot % 2, :],
                                            start=(ot - 2 == 0), stop=False), 1, ("l2qmm", g, ot - 2))
            for ot in range(DT - 2, DT):
                WM(tensor, "SS", ("r2sq", g, ot))
                INC("ST", tensor.matmul(ps3_G2[0:1, 0:512], onescol[:], sqb[:, 2 + ot % 2, :],
                                        start=False, stop=(ot == DT - 1)), 1, ("l2qmm", g, ot))
            WM(tensor, "SV", ("l2fin", g))
            tensor.matmul(ps3_U[:, 0:512], onesrow[:, :], st3[0:1, 0, :], start=True, stop=True)
            INC("ST", tensor.matmul(ps3_U2[:, 0:512], onesrow[:, :], st3[0:1, 3, :], start=True, stop=True),
                1, ("l2bc", g))
            for ot in range(DT):
                WM(tensor, "SV", ("lnb", g, ot))
                if ot >= 2:
                    WM(tensor, "SS", ("oev", g, ot - 2))
                pst = ps3_T if ot % 2 == 0 else ps3_T2
                for j in range(4):
                    inst = tensor.transpose(pst[:, j * 128:(j + 1) * 128],
                                            lnb[:, ot % 2, j * 128:(j + 1) * 128], ident[:])
                INC("ST", inst, 1, ("otr", g, ot))

        @J("vector")
        def _(vector, g=g):
            for ot in range(DT):
                WM(vector, "ST", ("ymm", g, ot))
                psm = ps3_G if ot % 2 == 0 else ps3_G2
                INC("SV", vector.tensor_copy(yg[:, ot, :], psm[:, :]), 1, ("yev", g, ot))
            WM(vector, "ST", ("rsmm", g, DT - 1))
            INC("SV", vector.tensor_scalar(st3[0:1, 0, :], ps3_U[0:1, 0:512], 1.0 / D, None, ALU.mult),
                1, ("rmse", g))
            WM(vector, "SS", ("rmsq", g))
            INC("SV", vector.reciprocal(st3[0:1, 3, :], st3[0:1, 1, :]), 1, ("rmsfin", g))
            for ot in range(DT):
                WM(vector, "ST", ("rmsbc", g))
                if ot >= 2:
                    WM(vector, "ST", ("l2smm", g, ot - 2))
                    WM(vector, "SS", ("r2sq", g, ot - 2))
                vector.tensor_tensor(r2g[:, ot, :], yg[:, ot, :], ps3_U2[:, 0:512], ALU.mult)
                vector.tensor_tensor(r2g[:, ot, :], r2g[:, ot, :], ln1g[:, ot, :], ALU.add)
                INC("SV", vector.tensor_copy(sqb[:, ot % 2, :], r2g[:, ot, :]), 1, ("r2b", g, ot))
            WM(vector, "ST", ("l2smm", g, DT - 1))
            WM(vector, "ST", ("l2qmm", g, DT - 1))
            vector.tensor_scalar(st3[0:1, 0, :], ps3_G[0:1, 0:512], 1.0 / D, None, ALU.mult)
            vector.tensor_scalar(st3[0:1, 1, :], ps3_G2[0:1, 0:512], 1.0 / D, None, ALU.mult)
            vector.tensor_tensor(st3[0:1, 2, :], st3[0:1, 0, :], st3[0:1, 0, :], ALU.mult)
            INC("SV", vector.tensor_tensor(st3[0:1, 1, :], st3[0:1, 1, :], st3[0:1, 2, :], ALU.subtract),
                1, ("l2var", g))
            WM(vector, "SS", ("l2sq", g))
            vector.reciprocal(st3[0:1, 3, :], st3[0:1, 4, :])
            INC("SV", vector.tensor_tensor(st3[0:1, 5, :], st3[0:1, 0, :], st3[0:1, 3, :], ALU.mult),
                1, ("l2fin", g))
            for ot in range(DT):
                WM(vector, "ST", ("l2bc", g))
                if ot >= 2:
                    WM(vector, "ST", ("otr", g, ot - 2))
                vector.tensor_tensor(lnb[:, ot % 2, :], r2g[:, ot, :], ps3_U[:, 0:512], ALU.subtract)
                INC("SV", vector.tensor_tensor(lnb[:, ot % 2, :], lnb[:, ot % 2, :], ps3_U2[:, 0:512], ALU.mult),
                    1, ("lnb", g, ot))

        @J("scalar")
        def _(scalar, g=g):
            for ot in range(DT):
                WM(scalar, "SV", ("yev", g, ot))
                if DEBUG:
                    INC("SQ", scalar.dma_start(ydd[ot * 128:(ot + 1) * 128, g * GT:(g + 1) * GT],
                                               yg[:, ot, :]), 16)
                if ot >= 2:
                    WM(scalar, "ST", ("rsmm", g, ot - 2))
                INC("SS", scalar.activation(sqb[:, ot % 2, :], yg[:, ot, :], AF.Square,
                                            bias=c_ap(0.0), scale=c_ap(1.0)), 1, ("ysq", g, ot))
            WM(scalar, "SV", ("rmse", g))
            INC("SS", scalar.activation(st3[0:1, 1, :], st3[0:1, 0, :], AF.Sqrt,
                                        bias=c_ap1(1e-6), scale=c_ap1(1.0)), 1, ("rmsq", g))
            for ot in range(DT):
                WM(scalar, "SV", ("r2b", g, ot))
                if DEBUG:
                    INC("SQ", scalar.dma_start(r2d[ot * 128:(ot + 1) * 128, g * GT:(g + 1) * GT],
                                               r2g[:, ot, :]), 16)
                if ot >= 2:
                    WM(scalar, "ST", ("l2qmm", g, ot - 2))
                INC("SS", scalar.activation(sqb[:, 2 + ot % 2, :], sqb[:, ot % 2, :], AF.Square,
                                            bias=c_ap(0.0), scale=c_ap(1.0)), 1, ("r2sq", g, ot))
            WM(scalar, "SV", ("l2var", g))
            INC("SS", scalar.activation(st3[0:1, 4, :], st3[0:1, 1, :], AF.Sqrt,
                                        bias=c_ap1(1e-5), scale=c_ap1(1.0)), 1, ("l2sq", g))
            if DEBUG:
                WM(scalar, "SV", ("l2fin", g))
                INC("SQ", scalar.dma_start(std[0:1, g, :, :], st3[0:1, :, :]), 16)
            for ot in range(DT):
                WM(scalar, "ST", ("otr", g, ot))
                pst = ps3_T if ot % 2 == 0 else ps3_T2
                for j in range(4):
                    INC("SS", scalar.copy(ost[:, j, ot * 128:(ot + 1) * 128], pst[:, j * 128:(j + 1) * 128]),
                        1, ("oev", g, ot) if j == 3 else None)
            WM(scalar, "SS", ("oev", g, DT - 1))
            for j in range(4):
                INC("SQ", scalar.dma_start(out[(g * 4 + j) * 128:(g * 4 + j + 1) * 128, :], ost[:, j, :]),
                    16, ("odma", g, j))

        @J("gpsimd")
        def _(gp, g=g):
            WM(gp, "SS", ("oev", g, DT - 1))
            WM(gp, "SQ", ("odma", g, 3))
            gp.sem_inc(sems["GRP"], 1)
            cnt["GRP"] += 1

    @J("scalar")
    def _(scalar):
        W(scalar, "SQ", cnt["SQ"])
    run_jobs()
    p3m.close()
    p3.close()

    blk_cm.__exit__(None, None, None)
    es.close()
    LAST_MARKS = dict(marks)
    return nc


def _host_consts():
    s = _slopes(H)
    idx = np.arange(CH, dtype=np.float32)
    ident = np.eye(128, dtype=BF)
    idxloc = np.broadcast_to(idx[None, :], (128, 128)).astype(np.float32).copy()
    quart = np.full((128, 128), 0.25, dtype=np.float32)
    diagTm = np.zeros((128, H * 128), dtype=np.float32)
    qdecv = np.zeros((128, H * 128), dtype=np.float32)
    kdecppv = np.zeros((128, H), dtype=np.float32)
    diffm = idx[:, None] - idx[None, :]
    for hh in range(H):
        dd = np.where(diffm >= 0, np.exp(-s[hh] * diffm), 0.0)
        diagTm[:, hh * 128:(hh + 1) * 128] = dd.T
        qdecv[:, hh * 128:(hh + 1) * 128] = np.exp(-s[hh] * (idx + 1.0))[None, :]
        kdecppv[:, hh] = np.exp(-s[hh] * (CH - 1.0 - idx))
    cvals = np.zeros((128, 8), dtype=np.float32)
    cvals[:, 1] = 1.0
    cvals[:, 2] = PI / 2
    cvals[:, 3] = 1e-5
    cvals[:, 4] = 1e-6
    onescolv = np.ones((128, 1), dtype=BF)
    onesrowv = np.ones((1, 128), dtype=np.float32)
    return dict(ident=ident, idxloc=idxloc, quart=quart,
                diagT=diagTm.astype(BF), qdec=qdecv.astype(BF),
                kdecpp=kdecppv, cvals=cvals, onescol=onescolv, onesrow=onesrowv)


def kernel(L0_attn_x, L0_attn_w_qkv, L0_attn_w_proj, L0_attn_theta,
           L0_ffn_gate_up_packed, L0_ffn_gate_up_scales,
           L0_ffn_down_packed, L0_ffn_down_scales):
    x = np.ascontiguousarray(np.asarray(L0_attn_x, dtype=np.float32))
    wq = np.ascontiguousarray(np.asarray(L0_attn_w_qkv, dtype=np.float32))
    wp = np.ascontiguousarray(np.asarray(L0_attn_w_proj, dtype=np.float32))
    th = np.asarray(L0_attn_theta, dtype=np.float32).reshape(H, E)
    theta_pp = np.ascontiguousarray(np.tile(th.T, (1, 8)))
    gup_a = np.ascontiguousarray(np.asarray(L0_ffn_gate_up_packed, dtype=np.int32))
    gus_a = np.ascontiguousarray(np.asarray(L0_ffn_gate_up_scales, dtype=np.float16).T)
    dnp_a = np.ascontiguousarray(np.asarray(L0_ffn_down_packed, dtype=np.int32))
    dns_a = np.ascontiguousarray(np.asarray(L0_ffn_down_scales, dtype=np.float16).T)
    consts = _host_consts()

    nc = build_nc()
    in_maps = []
    for c in range(8):
        in_maps.append(dict(x=x[c], wqkv=wq, wproj=wp, theta_pp=theta_pp,
                            gup=gup_a, gus=gus_a, dnp=dnp_a, dns=dns_a,
                            ident=consts["ident"], idxloc=consts["idxloc"],
                            quart=consts["quart"], diagT=consts["diagT"],
                            qdec=consts["qdec"], kdecpp=consts["kdecpp"],
                            cvals=consts["cvals"], onescol=consts["onescol"],
                            onesrow=consts["onesrow"]))
    res = run_bass_kernel_spmd(nc, in_maps, core_ids=list(range(8)))
    return np.stack([np.asarray(r["out"], dtype=np.float32) for r in res.results], axis=0)
